# version 07
# baseline (speedup 1.0000x reference)
import contextlib
import math
import os
DBG_NOSTAT = os.environ.get("DBG_NOSTAT", "0") == "1"
DBG_NOROWS = os.environ.get("DBG_NOROWS", "0") == "1"

import numpy as np
import ml_dtypes
import concourse.bass as bass
import concourse.mybir as mybir
from concourse.bass_utils import run_bass_kernel_spmd

F32 = mybir.dt.float32
BF16 = mybir.dt.bfloat16
I32 = mybir.dt.int32
AF = mybir.ActivationFunctionType
ALU = mybir.AluOpType
AX = mybir.AxisListType

D = 4096
NDC = 32
NH = 32
NKV = 8
HD = 128
S = 4096
TOK = 2048
NT = TOK // 512
C_SHIFT = 12.0
SM_SCALE = 1.0 / math.sqrt(HD)
EPS = 1e-5
LN1E4 = math.log(10000.0)
PI = math.pi
LOOP_LEN = (16, 32)
NSLOT = 2


class Sem:
    def __init__(self, nc, es, name):
        self.h = es.enter_context(nc.semaphore(name))
        self.n = 0

    def inc(self, ins, k=1):
        ins.then_inc(self.h, k)
        self.n += k
        return ins


def build_program(taps=(), max_phase=99):
    nc = bass.Bass("TRN2", target_bir_lowering=False,
                   detect_race_conditions=False)

    xT_own = nc.dram_tensor("xT_own", [D, TOK], F32, kind="ExternalInput").ap()
    xT_full = nc.dram_tensor("xT_full", [D, S], F32, kind="ExternalInput").ap()
    posq = nc.dram_tensor("posq", [1, TOK], F32, kind="ExternalInput").ap()
    wqkvT = nc.dram_tensor("wqkvT", [D, 6144], F32, kind="ExternalInput").ap()
    bq_col_d = nc.dram_tensor("bq_col_d", [128, 40], F32, kind="ExternalInput").ap()
    bv_row_d = nc.dram_tensor("bv_row_d", [1, 1024], F32, kind="ExternalInput").ap()
    woT = nc.dram_tensor("woT", [D, D], F32, kind="ExternalInput").ap()
    bo_col_d = nc.dram_tensor("bo_col_d", [128, 32], F32, kind="ExternalInput").ap()
    wbT = nc.dram_tensor("wbT", [D, D], F32, kind="ExternalInput").ap()
    bb_col_d = nc.dram_tensor("bb_col_d", [128, 32], F32, kind="ExternalInput").ap()
    wrT = nc.dram_tensor("wrT", [D, 8], F32, kind="ExternalInput").ap()
    At_d = nc.dram_tensor("At", [D, 128], F32, kind="ExternalInput").ap()
    Bt_d = nc.dram_tensor("Bt", [128, D], F32, kind="ExternalInput").ap()
    outT = nc.dram_tensor("outT", [D, TOK], F32, kind="ExternalOutput").ap()

    q_scr = nc.dram_tensor("q_scr", [NH, 128, TOK], BF16, kind="Internal").ap()
    k_scr = nc.dram_tensor("k_scr", [NKV, 128, S], BF16, kind="Internal").ap()
    v_scr = nc.dram_tensor("v_scr", [S // 128, 128, NKV * HD], BF16,
                           kind="Internal").ap()
    attn_scr = nc.dram_tensor("attn_scr", [NH, 128, TOK], BF16, kind="Internal").ap()
    y1_scr = nc.dram_tensor("y1_scr", [NDC, 128, TOK], BF16, kind="Internal").ap()
    y2_scr = nc.dram_tensor("y2_scr", [NDC, 128, TOK], BF16, kind="Internal").ap()

    tap_out = {}
    for name, shape in (
        ("qkv_q", (NH, 128, TOK)), ("qkv_k", (NKV, 128, S)),
        ("qkv_v", (S // 128, 128, NKV * HD)),
        ("attn", (128, NDC * TOK)), ("y1", (NDC, 128, TOK)),
        ("y1n", (128, NDC * TOK)), ("y2", (NDC, 128, TOK)),
        ("lnr1", (1, 2 * TOK)), ("route", (8, TOK)),
        ("gsb", (128, TOK)), ("trig", (128, 3 * TOK)), ("p0c", (128, 8)),
        ("trigq", (128, 3 * TOK)),
    ):
        if name in taps:
            tap_out[name] = nc.dram_tensor("tap_" + name, list(shape), F32,
                                           kind="ExternalOutput").ap()

    _uid = [0]

    def _nm(base):
        _uid[0] += 1
        return f"{base}_{_uid[0]}"

    with contextlib.ExitStack() as es:
        iota_p = es.enter_context(nc.sbuf_tensor("iota_p", [128, 1], F32))
        ident_b = es.enter_context(nc.sbuf_tensor("ident_b", [128, 128], BF16))
        ident_f8 = es.enter_context(nc.sbuf_tensor("ident_f8", [8, 8], F32))
        ones_r1 = es.enter_context(nc.sbuf_tensor("ones_r1", [1, 128], BF16))
        ones_r1f = es.enter_context(nc.sbuf_tensor("ones_r1f", [1, 128], F32))
        ones_c1 = es.enter_context(nc.sbuf_tensor("ones_c1", [128, 1], BF16))
        ones_c1f = es.enter_context(nc.sbuf_tensor("ones_c1f", [128, 1], F32))
        negc = es.enter_context(nc.sbuf_tensor("negc", [128, 1], F32))
        invf = es.enter_context(nc.sbuf_tensor("invf", [128, 1], F32))
        epsb = es.enter_context(nc.sbuf_tensor("epsb", [1, 1], F32))
        bq_col = es.enter_context(nc.sbuf_tensor("bq_col", [128, 40], F32))
        bo_col = es.enter_context(nc.sbuf_tensor("bo_col", [128, 32], F32))
        bb_col = es.enter_context(nc.sbuf_tensor("bb_col", [128, 32], F32))
        bv_row = es.enter_context(nc.sbuf_tensor("bv_row", [1, 1024], BF16))
        ebc = es.enter_context(nc.sbuf_tensor("ebc", [8, 128], BF16))
        fslot = es.enter_context(nc.sbuf_tensor("fslot", [1, 8], F32))
        fcol = es.enter_context(nc.sbuf_tensor("fcol", [128, 1], F32))
        lnb = es.enter_context(nc.sbuf_tensor("lnb", [128, 1], F32))
        scol = es.enter_context(nc.sbuf_tensor("scol", [128, 1], F32))

        sx = Sem(nc, es, "sx")
        sw = Sem(nc, es, "sw")
        swc = Sem(nc, es, "swc")
        sps = Sem(nc, es, "sps")
        spst = Sem(nc, es, "spst")
        sev = Sem(nc, es, "sev")
        sdv = Sem(nc, es, "sdv")
        sscw = Sem(nc, es, "sscw")
        sq2 = Sem(nc, es, "sq2")
        skv = Sem(nc, es, "skv")
        ssc = Sem(nc, es, "ssc")
        sex = Sem(nc, es, "sex")
        smk = Sem(nc, es, "smk")
        sav = Sem(nc, es, "sav")
        sfin = Sem(nc, es, "sfin")
        sst = Sem(nc, es, "sst")
        sini = Sem(nc, es, "sini")
        swp = [Sem(nc, es, "swpa"), Sem(nc, es, "swpb")]
        sxp = [Sem(nc, es, "sxpa"), Sem(nc, es, "sxpb")]
        scp = [Sem(nc, es, "scpa"), Sem(nc, es, "scpb")]
        sstp = [Sem(nc, es, "sstpa"), Sem(nc, es, "sstpb")]
        sad = Sem(nc, es, "sad")
        sae = Sem(nc, es, "sae")
        srm = Sem(nc, es, "srm")
        sat = Sem(nc, es, "sat")
        spl = Sem(nc, es, "spl")
        smg = Sem(nc, es, "smg")

        ep = contextlib.ExitStack()
        posq_bc = ep.enter_context(nc.sbuf_tensor("posq_bc", [128, TOK], F32))
        ep1 = contextlib.ExitStack()
        big_a = ep1.enter_context(nc.sbuf_tensor("xbuf", [128, NDC * TOK], BF16))

        def vf(v, ap):
            return v.tensor_copy(fslot[0:1, 0:1], ap[0:1, 0:1])

        def sf(s_, ap):
            return s_.copy(fslot[0:1, 2:3], ap[0:1, 0:1])

        def gf(g_, ap):
            return g_.tensor_copy(fcol[:, 0:1], ap[:, 0:1])

        def _finalize_dbg():
            with nc.Block() as blk:
                @blk.sync
                def _(sy):
                    sy.wait_ge(sscw.h, sscw.n)
                    sy.wait_ge(sst.h, sst.n)
                    sy.wait_ge(sdv.h, sdv.n)

        with contextlib.ExitStack() as p0:
            t0 = p0.enter_context(nc.sbuf_tensor("t0", [128, 128], F32))
            t1 = p0.enter_context(nc.sbuf_tensor("t1", [128, 128], F32))
            iota128 = p0.enter_context(nc.sbuf_tensor("iota128", [128, 128], F32))
            posq_row = p0.enter_context(nc.sbuf_tensor("posq_row", [1, TOK], F32))
            identf_full = p0.enter_context(nc.sbuf_tensor("identf_full",
                                                          [128, 128], F32))
            pq_ps = p0.enter_context(nc.psum_tensor("pq_ps", [128, TOK], F32))

            with nc.Block() as blk:
                @blk.gpsimd
                def _(g):
                    for dst, src in ((posq_row[:, :], posq), (bq_col[:, :], bq_col_d),
                                     (bo_col[:, :], bo_col_d), (bb_col[:, :], bb_col_d),
                                     (bv_row[:, :], bv_row_d)):
                        sx.inc(g.dma_start(out=dst, in_=src), 16)
                    g.iota(iota_p[:, :], [[0, 1]], channel_multiplier=1,
                           allow_small_or_imprecise_dtypes=True)
                    g.iota(iota128[:, :], [[1, 128]], channel_multiplier=0,
                           allow_small_or_imprecise_dtypes=True)
                    g.memset(ones_r1[:, :], 1.0)
                    g.memset(ones_r1f[:, :], 1.0)
                    g.memset(ones_c1[:, :], 1.0)
                    g.memset(ones_c1f[:, :], 1.0)
                    g.memset(epsb[:, :], EPS)
                    g.memset(lnb[:, :], LN1E4)
                    sini.inc(g.memset(negc[:, :], -C_SHIFT))

                @blk.vector
                def _(v):
                    def spacer():
                        v.tensor_copy(scol[:, 0:1], iota_p[:, 0:1])
                    v.wait_ge(sini.h, 1)
                    v.tensor_scalar(identf_full[:, :], iota128[:, :], iota_p[:, 0:1],
                                    None, ALU.is_equal)
                    v.tensor_scalar(t1[0:8, :], iota128[0:8, :], 1.0 / 16.0,
                                    iota_p[0:8, 0:1], ALU.mult, ALU.subtract)
                    v.tensor_copy(ident_b[:, :], identf_full[:, :])
                    v.tensor_scalar(t0[0:8, :], t1[0:8, :], 0.0, None, ALU.is_ge)
                    v.tensor_copy(ident_f8[:, :], identf_full[0:8, 0:8])
                    v.tensor_scalar(t1[0:8, :], t1[0:8, :], 0.999, None, ALU.is_lt)
                    spacer()
                    v.tensor_mul(ebc[0:8, :], t0[0:8, :], t1[0:8, :])
                    sdv.inc(vf(v, ebc))

                b_sdv0 = sdv.n

                @blk.scalar
                def _(s):
                    s.wait_ge(sini.h, 1)
                    s.activation(invf[0:64, :], iota_p[0:64, 0:1], AF.Exp,
                                 bias=0.0, scale=-LN1E4 / 64.0)
                    s.activation(invf[64:128, :], iota_p[64:128, 0:1], AF.Exp,
                                 bias=lnb[64:128, 0:1], scale=-LN1E4 / 64.0)
                    sini.inc(sf(s, invf))

                @blk.tensor
                def _(t):
                    t.wait_ge(sx.h, 80)
                    ins = None
                    for i in range(NT):
                        ins = t.matmul(pq_ps[:, i * 512:(i + 1) * 512], ones_r1f[:, :],
                                       posq_row[:, i * 512:(i + 1) * 512],
                                       start=True, stop=True)
                    sps.inc(ins)

                b_sps0 = sps.n

                @blk.scalar
                def _(s):
                    s.wait_ge(sps.h, b_sps0)
                    s.copy(posq_bc[:, :], pq_ps[:, :])
                    sini.inc(sf(s, posq_bc))

        if "p0c" in taps:
            with nc.Block() as blk:
                @blk.gpsimd
                def _(g):
                    g.wait_ge(sini.h, sini.n)
                    with nc.allow_non_contiguous_dma(reason="debug"):
                        sst.inc(g.dma_start(out=tap_out["p0c"][:, 0:1],
                                            in_=iota_p[:, :],
                                            allow_slow_non_contiguous=True), 16)
                        sst.inc(g.dma_start(out=tap_out["p0c"][:, 4:5],
                                            in_=invf[:, :],
                                            allow_slow_non_contiguous=True), 16)

        cs_scope = contextlib.ExitStack()
        cos_t = cs_scope.enter_context(nc.sbuf_tensor("cos_t", [128, TOK], BF16))
        sin_t = cs_scope.enter_context(nc.sbuf_tensor("sin_t", [128, TOK], BF16))

        def emit_trig(pass_i):
            with contextlib.ExitStack() as ts_:
                ang = ts_.enter_context(nc.sbuf_tensor(_nm("ang"), [128, TOK], F32))
                u = ts_.enter_context(nc.sbuf_tensor(_nm("u"), [128, TOK], F32))
                ki = ts_.enter_context(nc.sbuf_tensor(_nm("ki"), [128, TOK], I32))
                kf = ts_.enter_context(nc.sbuf_tensor(_nm("kf"), [128, TOK], F32))
                b_sev = sev.n
                with nc.Block() as blk:
                    if pass_i < 2:
                        @blk.gpsimd
                        def _(g):
                            g.wait_ge(sini.h, sini.n)
                            sini.inc(g.iota(ang[:, :], [[1, TOK]],
                                            base=pass_i * TOK,
                                            channel_multiplier=0,
                                            allow_small_or_imprecise_dtypes=True))

                    @blk.vector
                    def _(v):
                        v.wait_ge(sini.h, sini.n)
                        src = ang if pass_i < 2 else posq_bc
                        v.tensor_scalar(ang[:, :], src[:, :], invf[:, 0:1],
                                        None, ALU.mult)
                        for which in (0, 1):
                            if which == 0:
                                v.tensor_copy(u[:, :], ang[:, :])
                            else:
                                v.wait_ge(sev.h, b_sev + 1)
                                v.tensor_scalar(u[:, :], ang[:, :], PI / 2.0,
                                                None, ALU.add)
                            v.tensor_scalar(ki[:, :], u[:, :], 1.0 / (2 * PI),
                                            None, ALU.mult)
                            v.tensor_copy(kf[:, :], ki[:, :])
                            v.scalar_tensor_tensor(u[:, :], kf[:, :], -2 * PI,
                                                   u[:, :], ALU.mult, ALU.add)
                            v.tensor_scalar(kf[:, :], u[:, :], PI, None, ALU.is_ge)
                            v.scalar_tensor_tensor(u[:, :], kf[:, :], -2 * PI,
                                                   u[:, :], ALU.mult, ALU.add)
                            v.tensor_scalar(kf[:, :], u[:, :], -PI, None, ALU.is_lt)
                            v.scalar_tensor_tensor(
                                u[:, :], kf[:, :], 2 * PI, u[:, :],
                                ALU.mult, ALU.add)
                            sdv.inc(vf(v, u))

                    b_sdv = sdv.n - 2

                    @blk.scalar
                    def _(s):
                        s.wait_ge(sdv.h, b_sdv + 1)
                        s.activation(sin_t[:, :], u[:, :], AF.Sin,
                                     bias=0.0, scale=1.0)
                        sev.inc(sf(s, sin_t))
                        s.wait_ge(sdv.h, b_sdv + 2)
                        s.activation(cos_t[:, :], u[:, :], AF.Sin,
                                     bias=0.0, scale=1.0)
                        sev.inc(sf(s, cos_t))

        def emit_load_x(src_ap, wait_sev_to):
            with nc.Block() as blk:
                @blk.gpsimd
                def _(g):
                    g.wait_ge(sev.h, wait_sev_to)
                    for i in range(8):
                        ins = g.dma_start(
                            out=big_a[:, :].rearrange("p (c t) -> p c t", c=NDC)
                                [:, 4 * i:4 * (i + 1), :],
                            in_=src_ap[i * 512:(i + 1) * 512, :]
                                .rearrange("(c p) t -> p c t", p=128))
                        sx.inc(ins, 16)

        def emit_qk_loop(jlist, dsts, dst_off):
            b_sx = sx.n
            with contextlib.ExitStack() as ls:
                stage_w = ls.enter_context(
                    nc.sbuf_tensor(_nm("stage_w1"), [128, 2 * NDC * 128], BF16))
                evac = ls.enter_context(nc.sbuf_tensor(_nm("evac1"), [128, 2 * 512], F32))
                ropa = ls.enter_context(nc.sbuf_tensor(_nm("ropa"), [128, 512], F32))
                ropb = ls.enter_context(nc.sbuf_tensor(_nm("ropb"), [128, 512], F32))
                kq_st = ls.enter_context(nc.sbuf_tensor(_nm("kq_st"), [128, 2 * TOK], BF16))
                ps = ls.enter_context(nc.psum_tensor(_nm("p1_ps"), [128, 2 * 512], F32))
                nJ = len(jlist)
                b_sw = [swp[0].n, swp[1].n]
                b_sc = [scp[0].n, scp[1].n]
                b_sps, b_sev, b_sdv = sps.n, sev.n, sdv.n

                with nc.Block() as blk:
                    @blk.gpsimd
                    def _(g):
                        g.wait_ge(sx.h, b_sx)
                        for idx, (jc, _b) in enumerate(jlist):
                            if idx >= 2:
                                g.wait_ge(sps.h, b_sps + NT * (idx - 1))
                            ins = g.dma_start(
                                out=stage_w[:, :]
                                    .rearrange("p (b c j) -> p b c j", b=2, c=NDC)
                                    [:, idx % 2, :, :],
                                in_=wqkvT[:, jc * 128:(jc + 1) * 128]
                                    .rearrange("(c p) j -> p c j", p=128))
                            swp[idx % 2].inc(ins, 16)

                    @blk.tensor
                    def _(t):
                        t.wait_ge(sx.h, b_sx)
                        for idx in range(nJ):
                            t.wait_ge(swp[idx % 2].h,
                                      b_sw[idx % 2] + 16 * (idx // 2 + 1))
                            for nt in range(NT):
                                gi = idx * NT + nt
                                if gi >= 2:
                                    t.wait_ge(sev.h, b_sev + gi - 1)
                                ins = None
                                for c in range(NDC):
                                    ins = t.matmul(
                                        ps[:, (gi % 2) * 512:(gi % 2) * 512 + 512],
                                        stage_w[:, (idx % 2) * NDC * 128 + c * 128:
                                                (idx % 2) * NDC * 128 + c * 128 + 128],
                                        big_a[:, c * TOK + nt * 512:
                                              c * TOK + nt * 512 + 512],
                                        start=(c == 0), stop=(c == NDC - 1))
                                sps.inc(ins)

                    @blk.scalar
                    def _(s):
                        for idx, (jc, bcol) in enumerate(jlist):
                            for nt in range(NT):
                                gi = idx * NT + nt
                                s.wait_ge(sps.h, b_sps + gi + 1)
                                if gi >= 2:
                                    s.wait_ge(sdv.h, b_sdv + gi - 1)
                                sev.inc(s.activation(
                                    evac[:, (gi % 2) * 512:(gi % 2) * 512 + 512],
                                    ps[:, (gi % 2) * 512:(gi % 2) * 512 + 512],
                                    AF.Identity, bias=bq_col[:, bcol:bcol + 1],
                                    scale=1.0))

                    @blk.vector
                    def _(v):
                        for idx in range(nJ):
                            for nt in range(NT):
                                gi = idx * NT + nt
                                v.wait_ge(sev.h, b_sev + gi + 1)
                                if idx >= 2 and nt == 0:
                                    v.wait_ge(scp[idx % 2].h,
                                              b_sc[idx % 2] + 16 * (idx // 2))
                                ev = evac[:, (gi % 2) * 512:(gi % 2) * 512 + 512]
                                co = cos_t[:, nt * 512:nt * 512 + 512]
                                si = sin_t[:, nt * 512:nt * 512 + 512]
                                ks = kq_st[:, (idx % 2) * TOK + nt * 512:
                                           (idx % 2) * TOK + nt * 512 + 512]
                                v.tensor_mul(ropa[:, :], ev, co)
                                v.tensor_mul(ropb[0:64, :], ev[64:128, :],
                                             si[64:128, :])
                                v.tensor_mul(ropb[64:128, :], ev[0:64, :],
                                             si[0:64, :])
                                v.tensor_sub(ks[0:64, :], ropa[0:64, :],
                                             ropb[0:64, :])
                                sdv.inc(v.tensor_add(ks[64:128, :],
                                                     ropb[64:128, :],
                                                     ropa[64:128, :]))

                    @blk.sync
                    def _(sy):
                        for idx in range(nJ):
                            sy.wait_ge(sdv.h, b_sdv + NT * (idx + 1))
                            ins = sy.dma_start(
                                out=dsts[idx][:, dst_off:dst_off + TOK],
                                in_=kq_st[:, (idx % 2) * TOK:(idx % 2) * TOK + TOK])
                            scp[idx % 2].inc(ins, 16)

        def emit_v_loop(pass_i):
            b_sx = sx.n
            with contextlib.ExitStack() as ls:
                stage_wv = ls.enter_context(
                    nc.sbuf_tensor(_nm("stage_wv"), [128, NDC * 512], BF16))
                v_st = ls.enter_context(nc.sbuf_tensor(_nm("v_st"), [128, 8 * 512], BF16))
                ps = ls.enter_context(nc.psum_tensor(_nm("pv_ps"), [128, 2 * 512], F32))
                b_sw, b_sps, b_sev, b_sscw, b_swc = sw.n, sps.n, sev.n, sscw.n, swc.n

                with nc.Block() as blk:
                    @blk.gpsimd
                    def _(g):
                        g.wait_ge(sx.h, b_sx)
                        for jb in range(2):
                            if jb >= 1:
                                g.wait_ge(sps.h, b_sps + 16 * jb)
                            ins = g.dma_start(
                                out=stage_wv[:, :]
                                    .rearrange("p (c j) -> p c j", c=NDC),
                                in_=wqkvT[:, 5120 + jb * 512:5120 + jb * 512 + 512]
                                    .rearrange("(c p) j -> p c j", p=128))
                            sw.inc(ins, 16)

                    @blk.tensor
                    def _(t):
                        t.wait_ge(sx.h, b_sx)
                        for jb in range(2):
                            t.wait_ge(sw.h, b_sw + 16 * (jb + 1))
                            for tt in range(16):
                                gi = jb * 16 + tt
                                if gi >= 2:
                                    t.wait_ge(sev.h, b_sev + gi - 1)
                                for c in range(NDC):
                                    t.matmul(
                                        ps[:, (gi % 2) * 512:(gi % 2) * 512 + 512],
                                        big_a[:, c * TOK + tt * 128:
                                              c * TOK + tt * 128 + 128],
                                        stage_wv[:, c * 512:c * 512 + 512],
                                        start=(c == 0), stop=False)
                                ins = t.matmul(
                                    ps[:, (gi % 2) * 512:(gi % 2) * 512 + 512],
                                    ones_r1[:, :],
                                    bv_row[0:1, jb * 512:jb * 512 + 512],
                                    start=False, stop=True)
                                sps.inc(ins)

                    @blk.scalar
                    def _(s):
                        for jb in range(2):
                            for tt in range(16):
                                gi = jb * 16 + tt
                                s.wait_ge(sps.h, b_sps + gi + 1)
                                half_i = gi // 8
                                if half_i >= 1 and gi % 8 == 0:
                                    s.wait_ge(sscw.h, b_sscw + 16 * half_i)
                                sev.inc(s.copy(
                                    v_st[:, (tt % 8) * 512:(tt % 8) * 512 + 512],
                                    ps[:, (gi % 2) * 512:(gi % 2) * 512 + 512]))

                    @blk.sync
                    def _(sy):
                        for jb in range(2):
                            for h8 in range(2):
                                sy.wait_ge(sev.h, b_sev + jb * 16 + h8 * 8 + 8)
                                t0g = pass_i * 16 + h8 * 8
                                ins = sy.dma_start(
                                    out=v_scr[t0g:t0g + 8, :, jb * 512:jb * 512 + 512]
                                        .rearrange("t p j -> p t j"),
                                    in_=v_st[:, :].rearrange("p (t j) -> p t j", t=8))
                                sscw.inc(ins, 16)

        for pass_i in range(2):
            emit_trig(pass_i)
            if pass_i == 0 and "trig" in taps:
                with nc.Block() as blk:
                    @blk.gpsimd
                    def _(g):
                        g.wait_ge(sev.h, sev.n)
                        sst.inc(g.dma_start(out=tap_out["trig"][:, 0:TOK],
                                            in_=sin_t[:, :]), 16)
                        sst.inc(g.dma_start(out=tap_out["trig"][:, TOK:2 * TOK],
                                            in_=cos_t[:, :]), 16)
                        with nc.allow_non_contiguous_dma(reason="debug tap"):
                            sst.inc(g.dma_start(
                                out=tap_out["trig"][:, 2 * TOK:2 * TOK + 1],
                                in_=invf[:, :],
                                allow_slow_non_contiguous=True), 16)
            emit_load_x(xT_full[:, pass_i * TOK:(pass_i + 1) * TOK], sev.n)
            emit_qk_loop([(32 + jk, 32 + jk) for jk in range(NKV)],
                         [k_scr[jk] for jk in range(NKV)], pass_i * TOK)
            emit_v_loop(pass_i)
        emit_trig(2)
        if "trigq" in taps:
            with nc.Block() as blk:
                @blk.gpsimd
                def _(g):
                    g.wait_ge(sev.h, sev.n)
                    sst.inc(g.dma_start(out=tap_out["trigq"][:, 0:TOK],
                                        in_=sin_t[:, :]), 16)
                    sst.inc(g.dma_start(out=tap_out["trigq"][:, TOK:2 * TOK],
                                        in_=cos_t[:, :]), 16)
                    sst.inc(g.dma_start(out=tap_out["trigq"][:, 2 * TOK:3 * TOK],
                                        in_=posq_bc[:, :]), 16)
        emit_load_x(xT_own, sev.n)
        emit_qk_loop([(jq, jq) for jq in range(NH)],
                     [q_scr[jq] for jq in range(NH)], 0)

        cs_scope.close()

        if "qkv_q" in taps or "qkv_k" in taps or "qkv_v" in taps:
            with nc.Block() as blk:
                @blk.gpsimd
                def _(g):
                    g.wait_ge(sscw.h, sscw.n)
                    for nm, src in (("qkv_q", q_scr), ("qkv_k", k_scr),
                                    ("qkv_v", v_scr)):
                        if nm in taps:
                            sst.inc(g.dma_start(out=tap_out[nm], in_=src), 16)

        if max_phase < 2:
            _finalize_dbg()
            ep.close()
            return nc, tap_out
        ep1.close()
        with contextlib.ExitStack() as p2:
            masks = p2.enter_context(nc.sbuf_tensor("masks", [128, 16 * 1024], BF16))
            k_g = p2.enter_context(nc.sbuf_tensor("k_g", [128, 2 * S], BF16))
            v_g = p2.enter_context(nc.sbuf_tensor("v_g", [128, 2 * S], BF16))
            NEXP = 8
            expb4 = [p2.enter_context(nc.sbuf_tensor(f"expb{j}", [128, 2 * 1024],
                                                     BF16)) for j in range(4)]

            def ebsl(vi, hf=None):
                o = ((vi // 4) % 2) * 1024
                if hf is None:
                    return expb4[vi % 4][:, o:o + 1024]
                return expb4[vi % 4][:, o + hf * 512:o + hf * 512 + 512]
            acc2 = [p2.enter_context(nc.sbuf_tensor(f"acc{i}", [128, 1024], F32))
                    for i in range(3)]
            accb2 = [p2.enter_context(nc.sbuf_tensor(f"accb{i}", [128, 1024], F32))
                     for i in range(2)]
            rrow = p2.enter_context(nc.sbuf_tensor("rrow_t", [1, 1024], F32))
            av_sb = p2.enter_context(nc.sbuf_tensor("av_sb", [128, 2 * 1024], BF16))
            att_st = p2.enter_context(nc.sbuf_tensor("att_st", [128, 2 * 1024], BF16))
            q_t = p2.enter_context(nc.sbuf_tensor("q_t2", [128, 2 * 1024], BF16))
            ps_sc = p2.enter_context(nc.psum_tensor("ps_sc", [128, 3 * 1024], F32))
            ps_av = p2.enter_context(nc.psum_tensor("ps_av", [128, 1024], F32))

            heads = [(sl, g, qh) for sl in range(NSLOT) for g in range(NKV)
                     for qh in range(4)]
            vstart = {}
            _vv = 0
            for h_ in heads:
                vstart[h_] = _vv
                _vv += LOOP_LEN[h_[0]]
            NVIS = _vv
            vhead = [0] * NVIS
            vkk = [0] * NVIS
            for hidx, h_ in enumerate(heads):
                for kk in range(LOOP_LEN[h_[0]]):
                    vhead[vstart[h_] + kk] = hidx
                    vkk[vstart[h_] + kk] = kk
            K0 = 5
            vslot = [(k % 2) if k < K0 else [2, 0, 1][(k - K0) % 3] for k in vkk]
            vmask = [False] * NVIS
            vpar = [0] * NVIS
            mcnt = [0] * NVIS
            acnt = [[0] * NVIS, [0] * NVIS]
            firstpar = {}
            mc = 0
            ac = [0, 0]
            for vi in range(NVIS):
                hidx = vhead[vi]
                sl = heads[hidx][0]
                L = LOOP_LEN[sl]
                kk = vkk[vi]
                vmask[vi] = (kk - (L - 16)) >= 0
                par = 0 if (kk % 16) in (0, 2, 4, 6, 9, 11, 13) else 1
                vpar[vi] = par
                if (hidx, par) not in firstpar:
                    firstpar[(hidx, par)] = vi
                if vmask[vi]:
                    mc += 1
                ac[par] += 1
                mcnt[vi] = mc
                acnt[0][vi] = ac[0]
                acnt[1][vi] = ac[1]

            b_ssc, b_sex, b_smk, b_sav = ssc.n, sex.n, smk.n, sav.n
            b_sfin, b_sev = sfin.n, sev.n
            b_sq2, b_skv = sq2.n, skv.n
            b_sad, b_spl, b_smg, b_sae = sad.n, spl.n, smg.n, sae.n
            b_srm, b_sdv = srm.n, sdv.n
            b_scp = [scp[0].n, scp[1].n]

            def wait_exp_consumed(eng, vi):
                pv = vi - NEXP
                if pv < 0:
                    return
                eng.wait_ge(sav.h, b_sav + pv + 1)
                if vpar[pv] == 0:
                    eng.wait_ge(sad.h, b_sad + acnt[0][pv])
                else:
                    eng.wait_ge(spl.h, b_spl + acnt[1][pv])

            def wait_masked_exp(eng, vi):
                if vmask[vi]:
                    eng.wait_ge(smk.h, b_smk + mcnt[vi])
                else:
                    eng.wait_ge(sex.h, b_sex + vi + 1)

            pe_slot_user = {}

            def pe_score_wait(t, slot):
                u = pe_slot_user.get(slot)
                if u is None:
                    return
                if u[0] == "exp":
                    t.wait_ge(sex.h, b_sex + u[1] + 1)
                else:
                    assert u[0] == "fin"
                    t.wait_ge(srm.h, b_srm + u[1] + 1)

            def pe_av(t, pv):
                ph = vhead[pv]
                psl = heads[ph][0]
                ggi = psl * NKV + heads[ph][1]
                kkp = vkk[pv]
                Lp = LOOP_LEN[psl]
                wait_masked_exp(t, pv)
                if kkp == 0 and ph >= 1:
                    t.wait_ge(sae.h, b_sae + ph)
                ins = None
                for hf in range(2):
                    ins = t.matmul(
                        ps_av[:, hf * 512:hf * 512 + 512],
                        v_g[:, (ggi % 2) * S + kkp * 128:
                            (ggi % 2) * S + kkp * 128 + 128],
                        ebsl(pv, hf),
                        start=(kkp == 0), stop=(kkp == Lp - 1),
                        skip_group_check=True)
                sav.inc(ins)

            def pe_fin_den(t, h2):
                t.wait_ge(smg.h, b_smg + h2 + 1)
                pe_score_wait(t, 2)
                ins = None
                for hf in range(2):
                    ins = t.matmul(
                        ps_sc[0:1, 2048 + hf * 512:2048 + hf * 512 + 512],
                        ones_c1f[:, :],
                        acc2[h2 % 3][:, hf * 512:hf * 512 + 512],
                        start=True, stop=True, skip_group_check=True)
                sfin.inc(ins)

            def pe_fin_bc(t, h2):
                t.wait_ge(sev.h, b_sev + h2 + 1)
                ins = None
                for hf in range(2):
                    ins = t.matmul(
                        ps_sc[:, 2048 + hf * 512:2048 + hf * 512 + 512],
                        ones_r1f[:, :],
                        rrow[0:1, hf * 512:hf * 512 + 512],
                        start=True, stop=True, skip_group_check=True)
                sfin.inc(ins)
                pe_slot_user[2] = ("fin", h2)

            def act_avevac(s, h2):
                s.wait_ge(sav.h, b_sav + vstart[heads[h2]]
                          + LOOP_LEN[heads[h2][0]])
                if h2 >= 2:
                    s.wait_ge(srm.h, b_srm + h2 - 1)
                s.copy(av_sb[:, (h2 % 2) * 1024:(h2 % 2) * 1024 + 1024],
                       ps_av[:, :])
                sae.inc(sf(s, av_sb))

            def act_denrow(s, h2):
                s.wait_ge(sfin.h, b_sfin + 2 * h2 + 1)
                s.copy(rrow[0:1, :], ps_sc[0:1, 2048:3072])
                sev.inc(sf(s, rrow))

            def dve_merge(v, h2):
                L2 = LOOP_LEN[heads[h2][0]]
                vs2 = vstart[heads[h2]]
                v.wait_ge(spl.h, b_spl + acnt[1][vs2 + L2 - 1])
                smg.inc(v.tensor_add(acc2[h2 % 3][:, :], acc2[h2 % 3][:, :],
                                     accb2[h2 % 2][:, :]))

            def dve_recip_mul(v, h2):
                v.wait_ge(sfin.h, b_sfin + 2 * h2 + 2)
                v.reciprocal(acc2[h2 % 3][:, :], ps_sc[:, 2048:3072])
                v.wait_ge(sae.h, b_sae + h2 + 1)
                if h2 >= 2:
                    v.wait_ge(scp[h2 % 2].h, b_scp[h2 % 2] + 16 * (h2 // 2))
                srm.inc(v.tensor_mul(
                    att_st[:, (h2 % 2) * 1024:(h2 % 2) * 1024 + 1024],
                    av_sb[:, (h2 % 2) * 1024:(h2 % 2) * 1024 + 1024],
                    acc2[h2 % 3][:, :]))

            def st_store(sy, ph):
                psl2, pg2, pqh2 = heads[ph]
                phead = pg2 * 4 + pqh2
                sy.wait_ge(srm.h, b_srm + ph + 1)
                ins = sy.dma_start(
                    out=attn_scr[phead][:, psl2 * 1024:psl2 * 1024 + 1024],
                    in_=att_st[:, (ph % 2) * 1024:(ph % 2) * 1024 + 1024])
                scp[ph % 2].inc(ins, 16)

            for sl in range(NSLOT):
                L = LOOP_LEN[sl]
                with nc.Block() as blk:
                    @blk.vector
                    def _(v, sl=sl, L=L):
                        ins = None
                        for r in range(16):
                            kk0 = float((L - 16 + r) * 128)
                            ins = v.tensor_scalar(
                                masks[:, r * 1024:(r + 1) * 1024],
                                posq_bc[:, sl * 1024:(sl + 1) * 1024],
                                kk0, iota_p[:, 0:1], ALU.subtract, ALU.is_ge)
                        sdv.inc(ins)

                with nc.Block() as blk:
                    @blk.gpsimd
                    def _(g_, sl=sl, L=L):
                        for g in range(NKV):
                            for qh in range(4):
                                hidx = sl * 32 + g * 4 + qh
                                vs = vstart[(sl, g, qh)]
                                for kk in range(L):
                                    vi = vs + kk
                                    if vpar[vi] != 1:
                                        continue
                                    wait_masked_exp(g_, vi)
                                    eb = ebsl(vi)
                                    if vi == firstpar[(hidx, 1)]:
                                        if hidx >= 2:
                                            g_.wait_ge(smg.h, b_smg + hidx - 1)
                                        spl.inc(g_.tensor_copy(
                                            accb2[hidx % 2][:, :], eb))
                                    else:
                                        spl.inc(g_.tensor_tensor(
                                            accb2[hidx % 2][:, :],
                                            accb2[hidx % 2][:, :], eb, ALU.add))

                    @blk.sync
                    def _(sy, sl=sl, L=L):
                        for g in range(NKV):
                            for qh in range(4):
                                hidx = sl * 32 + g * 4 + qh
                                head = g * 4 + qh
                                if hidx >= 2:
                                    prev = heads[hidx - 2]
                                    sy.wait_ge(ssc.h, b_ssc + vstart[prev]
                                               + LOOP_LEN[prev[0]])
                                ins = sy.dma_start(
                                    out=q_t[:, (hidx % 2) * 1024:
                                            (hidx % 2) * 1024 + 1024],
                                    in_=q_scr[head][:, sl * 1024:sl * 1024 + 1024])
                                sq2.inc(ins, 16)
                                if hidx >= 2:
                                    st_store(sy, hidx - 2)
                        if sl == NSLOT - 1:
                            st_store(sy, 62)
                            st_store(sy, 63)

                    @blk.tensor
                    def _(t, sl=sl, L=L):
                        t.wait_ge(sdv.h, b_sdv + sl + 1)
                        for g in range(NKV):
                            gidx = sl * NKV + g
                            for qh in range(4):
                                hidx = sl * 32 + g * 4 + qh
                                vs = vstart[(sl, g, qh)]
                                for kk in range(L):
                                    vi = vs + kk
                                    slot = vslot[vi]
                                    if kk == 0:
                                        t.wait_ge(skv.h, b_skv + 32 * (gidx + 1))
                                        t.wait_ge(sq2.h, b_sq2 + 16 * (hidx + 1))
                                    pe_score_wait(t, slot)
                                    ins = None
                                    for hf in range(2):
                                        ins = t.matmul(
                                            ps_sc[:, slot * 1024 + hf * 512:
                                                  slot * 1024 + hf * 512 + 512],
                                            k_g[:, (gidx % 2) * S + kk * 128:
                                                (gidx % 2) * S + kk * 128 + 128],
                                            q_t[:, (hidx % 2) * 1024 + hf * 512:
                                                (hidx % 2) * 1024 + hf * 512 + 512],
                                            start=True, stop=True,
                                            skip_group_check=True)
                                    ssc.inc(ins)
                                    pe_slot_user[slot] = ("exp", vi)
                                    if vi >= 2:
                                        pe_av(t, vi - 2)
                                    if hidx >= 2:
                                        if kk == 0:
                                            pe_fin_den(t, hidx - 2)
                                        elif kk == 1:
                                            pe_fin_bc(t, hidx - 2)
                        if sl == NSLOT - 1:
                            pe_av(t, NVIS - 2)
                            pe_av(t, NVIS - 1)
                            pe_fin_den(t, 62)
                            pe_fin_bc(t, 62)
                            pe_fin_den(t, 63)
                            pe_fin_bc(t, 63)

                    @blk.scalar
                    def _(s, sl=sl, L=L):
                        def kv_dma(gp):
                            g2 = gp % NKV
                            Lg = LOOP_LEN[gp // NKV]
                            if gp >= 2:
                                s.wait_ge(sae.h, b_sae + 4 * (gp - 1))
                            ins = s.dma_start(
                                out=k_g[:, (gp % 2) * S:(gp % 2) * S + Lg * 128],
                                in_=k_scr[g2][:, 0:Lg * 128])
                            skv.inc(ins, 16)
                            ins = s.dma_start(
                                out=v_g[:, (gp % 2) * S:(gp % 2) * S + Lg * 128]
                                    .rearrange("p (t h) -> p t h", t=Lg),
                                in_=v_scr[0:Lg, :, g2 * 128:(g2 + 1) * 128]
                                    .rearrange("t p h -> p t h"))
                            skv.inc(ins, 16)

                        if sl == 0:
                            kv_dma(0)
                            kv_dma(1)
                        for g in range(NKV):
                            for qh in range(4):
                                hidx = sl * 32 + g * 4 + qh
                                vs = vstart[(sl, g, qh)]
                                for kk in range(L):
                                    vi = vs + kk
                                    slot = vslot[vi]
                                    s.wait_ge(ssc.h, b_ssc + vi + 1)
                                    wait_exp_consumed(s, vi)
                                    sex.inc(s.activation(
                                        ebsl(vi),
                                        ps_sc[:, slot * 1024:slot * 1024 + 1024],
                                        AF.Exp, bias=negc[:, 0:1],
                                        scale=SM_SCALE))
                                    if kk == 0:
                                        if hidx >= 1:
                                            act_avevac(s, hidx - 1)
                                        if hidx % 4 == 0 and 2 <= hidx // 4 + 1 <= 15:
                                            kv_dma(hidx // 4 + 1)
                                    elif kk == 1 and hidx >= 2:
                                        act_denrow(s, hidx - 2)
                        if sl == NSLOT - 1:
                            act_avevac(s, 63)
                            act_denrow(s, 62)
                            act_denrow(s, 63)

                    @blk.vector
                    def _(v, sl=sl, L=L):
                        def dve_add(vj):
                            if vj < 0 or vpar[vj] != 0:
                                return
                            hj = vhead[vj]
                            eb = ebsl(vj)
                            if not vmask[vj]:
                                v.wait_ge(sex.h, b_sex + vj + 1)
                            if vj == firstpar[(hj, 0)]:
                                sad.inc(v.tensor_copy(acc2[hj % 3][:, :], eb))
                            else:
                                sad.inc(v.tensor_add(acc2[hj % 3][:, :],
                                                     acc2[hj % 3][:, :], eb))

                        for g in range(NKV):
                            for qh in range(4):
                                hidx = sl * 32 + g * 4 + qh
                                vs = vstart[(sl, g, qh)]
                                for kk in range(L):
                                    vi = vs + kk
                                    if vmask[vi]:
                                        v.wait_ge(sex.h, b_sex + vi + 1)
                                        rel = kk - (L - 16)
                                        eb = ebsl(vi)
                                        smk.inc(v.tensor_mul(
                                            eb, eb,
                                            masks[:, rel * 1024:rel * 1024 + 1024]))
                                    dve_add(vi - 2)
                                    if kk == 1 and hidx >= 1:
                                        dve_merge(v, hidx - 1)
                                    elif kk == 2 and hidx >= 2:
                                        dve_recip_mul(v, hidx - 2)
                        if sl == NSLOT - 1:
                            dve_add(NVIS - 2)
                            dve_add(NVIS - 1)
                            dve_merge(v, 63)
                            dve_recip_mul(v, 62)
                            dve_recip_mul(v, 63)

            assert ssc.n == b_ssc + NVIS and sex.n == b_sex + NVIS
            assert sav.n == b_sav + NVIS
            assert sad.n == b_sad + acnt[0][-1] and spl.n == b_spl + acnt[1][-1]
            assert sfin.n == b_sfin + 2 * 64 and sev.n == b_sev + 64
            assert sdv.n == b_sdv + NSLOT and smg.n == b_smg + 64
            assert sae.n == b_sae + 64 and srm.n == b_srm + 64



        ep.close()

        lp = contextlib.ExitStack()
        big_a = lp.enter_context(nc.sbuf_tensor("big_a", [128, NDC * TOK], BF16))
        rstd_bc = lp.enter_context(nc.sbuf_tensor("rstd_bc", [128, TOK], BF16))
        mr_bc = lp.enter_context(nc.sbuf_tensor("mr_bc", [128, TOK], BF16))
        ssum = lp.enter_context(nc.sbuf_tensor("ssum", [1, TOK], F32))
        ssq = lp.enter_context(nc.sbuf_tensor("ssq", [1, TOK], F32))

        with nc.Block() as blk:
            @blk.sync
            def _(sy):
                sy.wait_ge(scp[0].h, scp[0].n)
                sy.wait_ge(scp[1].h, scp[1].n)
                for hc in range(NH):
                    ins = sy.dma_start(out=big_a[:, hc * TOK:(hc + 1) * TOK],
                                       in_=attn_scr[hc])
                    sat.inc(ins, 16)

            @blk.tensor
            def _(t):
                t.wait_ge(sat.h, sat.n)

        if "attn" in taps:
            with nc.Block() as blk:
                @blk.gpsimd
                def _(g):
                    g.wait_ge(sat.h, sat.n)
                    sst.inc(g.dma_start(out=tap_out["attn"], in_=big_a[:, :]), 16)

        if max_phase < 3:
            _finalize_dbg()
            lp.close()
            return nc, tap_out

        def emit_proj_ln(wT_ap, b_col, lora, scr_out, bt_st=None, g_sb=None):
            with contextlib.ExitStack() as ls:
                stage_w = ls.enter_context(
                    nc.sbuf_tensor(_nm("stage_w3"), [128, 2 * NDC * 128], BF16))
                x_st = (None if lora else
                        ls.enter_context(nc.sbuf_tensor(_nm("x_st"), [128, 2 * TOK], F32)))
                evac = ls.enter_context(nc.sbuf_tensor(_nm("evac3"), [128, 2 * 512], F32))
                sqb = ls.enter_context(nc.sbuf_tensor(_nm("sqb"), [128, 2 * 512], BF16))
                ystg = ls.enter_context(nc.sbuf_tensor(_nm("ystg"), [128, TOK], BF16))
                ps = ls.enter_context(nc.psum_tensor(_nm("p3_ps"), [128, 2 * 512], F32))
                ps_st = ls.enter_context(nc.psum_tensor(_nm("p3_st"), [1, 1024], F32))

                b_sw = [swp[0].n, swp[1].n]
                b_sx2 = [sxp[0].n, sxp[1].n]
                b_sps, b_spst = sps.n, spst.n
                b_sev, b_sdv, b_sscw, b_sfin = sev.n, sdv.n, sscw.n, sfin.n
                NG = NDC * NT

                def emit_stat_mm(t, gi):
                    if DBG_NOSTAT:
                        return
                    t.wait_ge(sev.h, b_sev + 2 * gi + 2)
                    if gi >= 1:
                        t.wait_ge(sfin.h, b_sfin + gi)
                    nt = gi % NT
                    t.matmul(ps_st[0:1, 0:512], ones_c1[:, :],
                             ystg[:, nt * 512:nt * 512 + 512],
                             start=True, stop=True, skip_group_check=True)
                    ins = t.matmul(ps_st[0:1, 512:1024], ones_c1[:, :],
                                   sqb[:, (gi % 2) * 512:(gi % 2) * 512 + 512],
                                   start=True, stop=True, skip_group_check=True)
                    spst.inc(ins)

                with nc.Block() as blk:
                    @blk.gpsimd
                    def _(g):
                        for o in range(NDC):
                            if o >= 2:
                                g.wait_ge(sps.h, b_sps + NT * (o - 1))
                            ins = g.dma_start(
                                out=stage_w[:, :]
                                    .rearrange("p (b c j) -> p b c j", b=2, c=NDC)
                                    [:, o % 2, :, :],
                                in_=wT_ap[:, o * 128:(o + 1) * 128]
                                    .rearrange("(c p) j -> p c j", p=128))
                            swp[o % 2].inc(ins, 16)


                    @blk.tensor
                    def _(t):
                        for o in range(NDC):
                            t.wait_ge(swp[o % 2].h, b_sw[o % 2] + 16 * (o // 2 + 1))
                            for nt in range(NT):
                                gi = o * NT + nt
                                if gi >= 2:
                                    t.wait_ge(sev.h, b_sev + 2 * (gi - 1))
                                ins = None
                                for c in range(NDC):
                                    ins = t.matmul(
                                        ps[:, (gi % 2) * 512:(gi % 2) * 512 + 512],
                                        stage_w[:, (o % 2) * NDC * 128 + c * 128:
                                                (o % 2) * NDC * 128 + c * 128 + 128],
                                        big_a[:, c * TOK + nt * 512:
                                              c * TOK + nt * 512 + 512],
                                        start=(c == 0),
                                        stop=(c == NDC - 1 and not lora))
                                if lora:
                                    ins = t.matmul(
                                        ps[:, (gi % 2) * 512:(gi % 2) * 512 + 512],
                                        bt_st[:, o * 128:o * 128 + 128],
                                        g_sb[:, nt * 512:nt * 512 + 512],
                                        start=False, stop=True,
                                        skip_group_check=True)
                                sps.inc(ins)
                                if gi >= 2:
                                    emit_stat_mm(t, gi - 2)
                        emit_stat_mm(t, NG - 2)
                        emit_stat_mm(t, NG - 1)

                    @blk.scalar
                    def _(s):
                        for o in range(NDC):
                            for nt in range(NT):
                                gi = o * NT + nt
                                s.wait_ge(sps.h, b_sps + gi + 1)
                                if gi >= 2:
                                    s.wait_ge(sdv.h, b_sdv + gi - 1)
                                sev.inc(s.activation(
                                    evac[:, (gi % 2) * 512:(gi % 2) * 512 + 512],
                                    ps[:, (gi % 2) * 512:(gi % 2) * 512 + 512],
                                    AF.Identity, bias=b_col[:, o:o + 1], scale=1.0))
                                s.wait_ge(sdv.h, b_sdv + gi + 1)
                                if gi >= 2 and not DBG_NOSTAT:
                                    s.wait_ge(spst.h, b_spst + gi - 1)
                                sev.inc(s.activation(
                                    sqb[:, (gi % 2) * 512:(gi % 2) * 512 + 512],
                                    ystg[:, nt * 512:nt * 512 + 512],
                                    AF.Square, bias=0.0, scale=1.0))

                    @blk.vector
                    def _(v):
                        def rowadd(gi2):
                            if DBG_NOSTAT:
                                return
                            nt2 = gi2 % NT
                            v.wait_ge(spst.h, b_spst + gi2 + 1)
                            if gi2 < NT:
                                v.tensor_copy(ssum[0:1, nt2 * 512:nt2 * 512 + 512],
                                              ps_st[0:1, 0:512])
                                ins = v.tensor_copy(
                                    ssq[0:1, nt2 * 512:nt2 * 512 + 512],
                                    ps_st[0:1, 512:1024])
                            else:
                                v.tensor_add(ssum[0:1, nt2 * 512:nt2 * 512 + 512],
                                             ssum[0:1, nt2 * 512:nt2 * 512 + 512],
                                             ps_st[0:1, 0:512])
                                ins = v.tensor_add(
                                    ssq[0:1, nt2 * 512:nt2 * 512 + 512],
                                    ssq[0:1, nt2 * 512:nt2 * 512 + 512],
                                    ps_st[0:1, 512:1024])
                            sfin.inc(ins)

                        for o in range(NDC):
                            for nt in range(NT):
                                gi = o * NT + nt
                                v.wait_ge(sev.h, b_sev + 2 * gi + 1)
                                if nt == 0 and o >= 1:
                                    v.wait_ge(sscw.h, b_sscw + 16 * o)
                                src1 = (x_st[:, (o % 2) * TOK + nt * 512:
                                             (o % 2) * TOK + nt * 512 + 512]
                                        if not lora else
                                        big_a[:, o * TOK + nt * 512:
                                              o * TOK + nt * 512 + 512])
                                if not lora:
                                    v.wait_ge(sxp[o % 2].h,
                                              b_sx2[o % 2] + 16 * (o // 2 + 1))
                                sdv.inc(v.tensor_add(
                                    ystg[:, nt * 512:nt * 512 + 512],
                                    evac[:, (gi % 2) * 512:(gi % 2) * 512 + 512],
                                    src1))
                                if gi >= 2:
                                    rowadd(gi - 2)
                        rowadd(NG - 2)
                        rowadd(NG - 1)

                    @blk.sync
                    def _(sy):
                        for o in range(NDC):
                            if not lora:
                                if o >= 2:
                                    sy.wait_ge(sdv.h, b_sdv + NT * (o - 1))
                                ins = sy.dma_start(
                                    out=x_st[:, (o % 2) * TOK:(o % 2) * TOK + TOK],
                                    in_=xT_own[o * 128:(o + 1) * 128, :])
                                sxp[o % 2].inc(ins, 16)
                            if o >= 1:
                                sy.wait_ge(sdv.h, b_sdv + NT * o)
                                ins = sy.dma_start(out=scr_out[o - 1], in_=ystg[:, :])
                                sscw.inc(ins, 16)
                        sy.wait_ge(sdv.h, b_sdv + NT * NDC)
                        ins = sy.dma_start(out=scr_out[NDC - 1], in_=ystg[:, :])
                        sscw.inc(ins, 16)

            if DBG_NOROWS:
                return
            with contextlib.ExitStack() as ls2:
                ps_ln = ls2.enter_context(nc.psum_tensor(_nm("ps_ln"), [128, TOK], F32))
                b2_sdv, b2_sev, b2_sps, b2_sfin = sdv.n, sev.n, sps.n, sfin.n
                with nc.Block() as blk:
                    @blk.vector
                    def _(v):
                        v.wait_ge(sfin.h, b2_sfin)
                        v.tensor_scalar(ssum[0:1, :], ssum[0:1, :], 1.0 / D,
                                        None, ALU.mult)
                        v.tensor_scalar(ssq[0:1, :], ssq[0:1, :], 1.0 / D,
                                        None, ALU.mult)
                        v.tensor_mul(ps_ln[0:1, 0:TOK], ssum[0:1, :], ssum[0:1, :])
                        v.tensor_sub(ssq[0:1, :], ssq[0:1, :], ps_ln[0:1, 0:TOK])
                        sdv.inc(vf(v, ssq))

                    @blk.scalar
                    def _(s):
                        s.wait_ge(sdv.h, b2_sdv + 1)
                        s.activation(ps_ln[0:1, 0:TOK], ssq[0:1, :],
                                     AF.Sqrt, bias=epsb[0:1, 0:1], scale=1.0)
                        sev.inc(sf(s, ps_ln))

                    @blk.vector
                    def _(v):
                        v.wait_ge(sev.h, b2_sev + 1)
                        v.reciprocal(ssq[0:1, :], ps_ln[0:1, 0:TOK])
                        v.tensor_mul(ssum[0:1, :], ssum[0:1, :], ssq[0:1, :])
                        sdv.inc(vf(v, ssum))

                    @blk.tensor
                    def _(t):
                        t.wait_ge(sdv.h, b2_sdv + 2)
                        ins = None
                        for i in range(NT):
                            ins = t.matmul(ps_ln[:, i * 512:(i + 1) * 512],
                                           ones_r1f[:, :],
                                           ssq[0:1, i * 512:(i + 1) * 512],
                                           start=True, stop=True,
                                           skip_group_check=True)
                        sps.inc(ins)
                        t.wait_ge(sev.h, b2_sev + 2)
                        ins = None
                        for i in range(NT):
                            ins = t.matmul(ps_ln[:, i * 512:(i + 1) * 512],
                                           ones_r1f[:, :],
                                           ssum[0:1, i * 512:(i + 1) * 512],
                                           start=True, stop=True,
                                           skip_group_check=True)
                        sps.inc(ins)

                    @blk.scalar
                    def _(s):
                        s.wait_ge(sps.h, b2_sps + 1)
                        s.copy(rstd_bc[:, :], ps_ln[:, :])
                        sev.inc(sf(s, rstd_bc))
                        s.wait_ge(sps.h, b2_sps + 2)
                        s.copy(mr_bc[:, :], ps_ln[:, :])
                        sev.inc(sf(s, mr_bc))

        emit_proj_ln(woT, bo_col, False, [y1_scr[o] for o in range(NDC)])

        if "y1" in taps or "lnr1" in taps:
            with nc.Block() as blk:
                @blk.gpsimd
                def _(g):
                    g.wait_ge(sscw.h, sscw.n)
                    g.wait_ge(sev.h, sev.n)
                    if "y1" in taps:
                        sst.inc(g.dma_start(out=tap_out["y1"], in_=y1_scr), 16)
                    if "lnr1" in taps:
                        sst.inc(g.dma_start(out=tap_out["lnr1"][0:1, 0:TOK],
                                            in_=ssq[0:1, :]), 16)
                        sst.inc(g.dma_start(out=tap_out["lnr1"][0:1, TOK:2 * TOK],
                                            in_=ssum[0:1, :]), 16)

        if max_phase < 4:
            _finalize_dbg()
            lp.close()
            return nc, tap_out
        with contextlib.ExitStack() as p4p:
            y1r = p4p.enter_context(nc.sbuf_tensor("y1r", [128, 2 * TOK], BF16))
            b_sx2 = [sxp[0].n, sxp[1].n]
            b_sdv = sdv.n
            with nc.Block() as blk:
                @blk.sync
                def _(sy):
                    sy.wait_ge(sev.h, sev.n)
                    sy.wait_ge(sscw.h, sscw.n)
                    for c in range(NDC):
                        if c >= 2:
                            sy.wait_ge(sdv.h, b_sdv + 2 * (c - 1))
                        ins = sy.dma_start(
                            out=y1r[:, (c % 2) * TOK:(c % 2) * TOK + TOK],
                            in_=y1_scr[c])
                        sxp[c % 2].inc(ins, 16)

                @blk.vector
                def _(v):
                    for c in range(NDC):
                        v.wait_ge(sxp[c % 2].h, b_sx2[c % 2] + 16 * (c // 2 + 1))
                        dst = big_a[:, c * TOK:(c + 1) * TOK]
                        v.tensor_mul(dst, y1r[:, (c % 2) * TOK:(c % 2) * TOK + TOK],
                                     rstd_bc[:, :])
                        sdv.inc(v.tensor_sub(dst, dst, mr_bc[:, :]), 2)

        if "y1n" in taps:
            with nc.Block() as blk:
                @blk.gpsimd
                def _(g):
                    g.wait_ge(sdv.h, sdv.n)
                    sst.inc(g.dma_start(out=tap_out["y1n"], in_=big_a[:, :]), 16)

        if max_phase < 5:
            _finalize_dbg()
            lp.close()
            return nc, tap_out
        p4s = contextlib.ExitStack()
        bt_st = p4s.enter_context(nc.sbuf_tensor("bt_st", [128, D], BF16))
        g_sb = p4s.enter_context(nc.sbuf_tensor("g_sb", [128, TOK], BF16))
        with contextlib.ExitStack() as p4a:
            wr_st = p4a.enter_context(nc.sbuf_tensor("wr_st", [128, NDC * 8], BF16))
            at_st = p4a.enter_context(nc.sbuf_tensor("at_st", [128, NDC * 128], BF16))
            lg = p4a.enter_context(nc.sbuf_tensor("lg", [8, TOK], F32))
            lgT = p4a.enter_context(nc.sbuf_tensor("lgT", [128, 16 * 8], F32))
            rsc = p4a.enter_context(nc.sbuf_tensor("rsc", [128, 4 * 16], F32))
            rse = p4a.enter_context(nc.sbuf_tensor("rse", [128, 16 * 8], F32))
            rcl = p4a.enter_context(nc.sbuf_tensor("rcl", [128, 16 * 4], F32))
            rc2 = p4a.enter_context(nc.sbuf_tensor("rc2", [128, 8], F32))
            wfb = p4a.enter_context(nc.sbuf_tensor("wfb", [128, 16 * 8], BF16))
            route_row = p4a.enter_context(nc.sbuf_tensor("route_row", [8, TOK], BF16))
            routebc = p4a.enter_context(nc.sbuf_tensor("routebc", [128, TOK], BF16))
            hevac = p4a.enter_context(nc.sbuf_tensor("hevac", [128, 2 * 512], F32))
            ps_rb = p4a.enter_context(nc.psum_tensor("ps_rb", [128, 2 * 512], F32))
            ps_t = p4a.enter_context(nc.psum_tensor("ps_t", [128, 8], F32))
            ps_rt = p4a.enter_context(nc.psum_tensor("ps_rt", [8, TOK], BF16))

            b_sw, b_sps, b_sev, b_sdv, b_sfin = sw.n, sps.n, sev.n, sdv.n, sfin.n
            y1n_done = sdv.n

            with nc.Block() as blk:
                @blk.gpsimd
                def _(g):
                    sw.inc(g.dma_start(
                        out=wr_st[:, :].rearrange("p (c e) -> p c e", c=NDC),
                        in_=wrT.rearrange("(c p) e -> p c e", p=128)), 16)
                    sw.inc(g.dma_start(
                        out=at_st[:, :].rearrange("p (c r) -> p c r", c=NDC),
                        in_=At_d.rearrange("(c p) r -> p c r", p=128)), 16)
                    sw.inc(g.dma_start(out=bt_st[:, :], in_=Bt_d), 16)

                @blk.tensor
                def _(t):
                    t.wait_ge(sw.h, b_sw + 16)
                    t.wait_ge(sdv.h, y1n_done)
                    for nt in range(NT):
                        if nt >= 2:
                            t.wait_ge(sev.h, b_sev + nt - 1)
                        ins = None
                        for c in range(NDC):
                            ins = t.matmul(
                                ps_rb[0:8, (nt % 2) * 512:(nt % 2) * 512 + 512],
                                wr_st[:, c * 8:c * 8 + 8],
                                big_a[:, c * TOK + nt * 512:c * TOK + nt * 512 + 512],
                                start=(c == 0), stop=(c == NDC - 1))
                        sps.inc(ins)

                @blk.scalar
                def _(s):
                    for nt in range(NT):
                        s.wait_ge(sps.h, b_sps + nt + 1)
                        s.copy(lg[0:8, nt * 512:nt * 512 + 512],
                               ps_rb[0:8, (nt % 2) * 512:(nt % 2) * 512 + 512])
                        sev.inc(sf(s, lg))

                @blk.tensor
                def _(t):
                    for tt in range(16):
                        t.wait_ge(sev.h, b_sev + tt // 4 + 1)
                        if tt >= 1:
                            t.wait_ge(sev.h, b_sev + NT + tt)
                        ins = t.transpose(ps_t[:, :],
                                          lg[0:8, tt * 128:tt * 128 + 128],
                                          ident_f8[:, :])
                        sfin.inc(ins)

                @blk.scalar
                def _(s):
                    for tt in range(16):
                        s.wait_ge(sfin.h, b_sfin + tt + 1)
                        s.copy(lgT[:, tt * 8:tt * 8 + 8], ps_t[:, :])
                        sev.inc(sf(s, lgT))

                def _m1(tt):
                    return rcl[:, tt * 4:tt * 4 + 1]

                def _nm1(tt):
                    return rcl[:, tt * 4 + 1:tt * 4 + 2]

                def _m2(tt):
                    return rcl[:, tt * 4 + 2:tt * 4 + 3]

                @blk.vector
                def _(v):
                    for base in range(0, 16, 2):
                        pair = (base, base + 1)
                        for tt in pair:
                            v.wait_ge(sev.h, b_sev + NT + tt + 1)
                        for tt in pair:
                            v.reduce_max(_m1(tt), lgT[:, tt * 8:tt * 8 + 8], AX.X)
                        for tt in pair:
                            pa = (tt % 2) * 32
                            v.tensor_scalar(rsc[:, pa:pa + 8],
                                            lgT[:, tt * 8:tt * 8 + 8],
                                            _m1(tt), None, ALU.is_equal)
                        for tt in pair:
                            v.tensor_scalar(_nm1(tt), _m1(tt), -1.0, None, ALU.mult)
                        for tt in pair:
                            pa = (tt % 2) * 32
                            v.scalar_tensor_tensor(rsc[:, pa + 8:pa + 16],
                                                   rsc[:, pa:pa + 8], -1e30,
                                                   lgT[:, tt * 8:tt * 8 + 8],
                                                   ALU.mult, ALU.add)
                        for tt in pair:
                            pa = (tt % 2) * 32
                            v.reduce_max(_m2(tt), rsc[:, pa + 8:pa + 16], AX.X)
                        for tt in pair:
                            sdv.inc(vf(v, rcl))

                @blk.scalar
                def _(s):
                    for tt in range(16):
                        s.wait_ge(sdv.h, b_sdv + tt + 1)
                        s.activation(rse[:, tt * 8:tt * 8 + 8],
                                     lgT[:, tt * 8:tt * 8 + 8], AF.Exp,
                                     bias=_nm1(tt), scale=1.0)
                        sev.inc(sf(s, rse))

                @blk.vector
                def _(v):
                    for base in range(0, 16, 2):
                        pair = (base, base + 1)
                        for tt in pair:
                            v.wait_ge(sev.h, b_sev + NT + 16 + tt + 1)
                        for tt in pair:
                            pa = (tt % 2) * 32
                            v.tensor_scalar(rsc[:, pa + 16:pa + 24],
                                            lgT[:, tt * 8:tt * 8 + 8],
                                            _m2(tt), None, ALU.is_ge)
                        for tt in pair:
                            pa = (tt % 2) * 32
                            v.tensor_mul(rsc[:, pa + 16:pa + 24],
                                         rsc[:, pa + 16:pa + 24],
                                         rse[:, tt * 8:tt * 8 + 8])
                        for tt in pair:
                            pa = (tt % 2) * 32
                            v.reduce_sum(rc2[:, (tt % 2) * 4:(tt % 2) * 4 + 1],
                                         rsc[:, pa + 16:pa + 24], AX.X)
                        for tt in pair:
                            v.reciprocal(rc2[:, (tt % 2) * 4 + 1:(tt % 2) * 4 + 2],
                                         rc2[:, (tt % 2) * 4:(tt % 2) * 4 + 1])
                        for tt in pair:
                            v.tensor_scalar(rc2[:, (tt % 2) * 4 + 2:(tt % 2) * 4 + 3],
                                            rc2[:, (tt % 2) * 4 + 1:(tt % 2) * 4 + 2],
                                            2.0, None, ALU.mult)
                        for tt in pair:
                            pa = (tt % 2) * 32
                            v.tensor_scalar(wfb[:, tt * 8:tt * 8 + 8],
                                            rsc[:, pa + 16:pa + 24],
                                            rc2[:, (tt % 2) * 4 + 2:(tt % 2) * 4 + 3],
                                            None, ALU.mult)
                        for tt in pair:
                            sdv.inc(vf(v, wfb))

                @blk.tensor
                def _(t):
                    for tt in range(16):
                        t.wait_ge(sdv.h, b_sdv + 16 + tt + 1)
                        ins = t.transpose(ps_rt[0:8, tt * 128:tt * 128 + 128],
                                          wfb[:, tt * 8:tt * 8 + 8],
                                          ident_b[:, :])
                        sfin.inc(ins)

                @blk.scalar
                def _(s):
                    s.wait_ge(sfin.h, b_sfin + 32)
                    s.copy(route_row[0:8, :], ps_rt[0:8, :])
                    sev.inc(sf(s, route_row))

                @blk.tensor
                def _(t):
                    t.wait_ge(sev.h, b_sev + NT + 32 + 1)
                    for nt in range(NT):
                        if nt >= 2:
                            t.wait_ge(sev.h, b_sev + NT + 32 + 1 + nt - 1)
                        ins = t.matmul(ps_rb[:, (nt % 2) * 512:(nt % 2) * 512 + 512],
                                       ebc[0:8, :],
                                       route_row[0:8, nt * 512:nt * 512 + 512],
                                       start=True, stop=True,
                                       skip_group_check=True)
                        sps.inc(ins)

                @blk.scalar
                def _(s):
                    for nt in range(NT):
                        s.wait_ge(sps.h, b_sps + NT + nt + 1)
                        s.copy(routebc[:, nt * 512:nt * 512 + 512],
                               ps_rb[:, (nt % 2) * 512:(nt % 2) * 512 + 512])
                        sev.inc(sf(s, routebc))

            b_sw2, b_sps2, b_sev2, b_sdv2 = sw.n, sps.n, sev.n, sdv.n
            with nc.Block() as blk:
                @blk.tensor
                def _(t):
                    t.wait_ge(sw.h, b_sw2)
                    t.wait_ge(sev.h, b_sev2)
                    for nt in range(NT):
                        if nt >= 2:
                            t.wait_ge(sev.h, b_sev2 + nt - 1)
                        ins = None
                        for c in range(NDC):
                            ins = t.matmul(
                                ps_rb[:, (nt % 2) * 512:(nt % 2) * 512 + 512],
                                at_st[:, c * 128:c * 128 + 128],
                                big_a[:, c * TOK + nt * 512:c * TOK + nt * 512 + 512],
                                start=(c == 0), stop=(c == NDC - 1))
                        sps.inc(ins)

                @blk.scalar
                def _(s):
                    for nt in range(NT):
                        s.wait_ge(sps.h, b_sps2 + nt + 1)
                        if nt >= 2:
                            s.wait_ge(sdv.h, b_sdv2 + nt - 1)
                        s.copy(hevac[:, (nt % 2) * 512:(nt % 2) * 512 + 512],
                               ps_rb[:, (nt % 2) * 512:(nt % 2) * 512 + 512])
                        sev.inc(sf(s, hevac))

                @blk.vector
                def _(v):
                    for nt in range(NT):
                        v.wait_ge(sev.h, b_sev2 + nt + 1)
                        sdv.inc(v.tensor_mul(
                            g_sb[:, nt * 512:nt * 512 + 512],
                            hevac[:, (nt % 2) * 512:(nt % 2) * 512 + 512],
                            routebc[:, nt * 512:nt * 512 + 512]))

        if "route" in taps or "gsb" in taps:
            with nc.Block() as blk:
                @blk.gpsimd
                def _(g):
                    g.wait_ge(sdv.h, sdv.n)
                    if "gsb" in taps:
                        sst.inc(g.dma_start(out=tap_out["gsb"], in_=g_sb[:, :]), 16)

        if max_phase < 6:
            _finalize_dbg()
            p4s.close()
            lp.close()
            return nc, tap_out
        emit_proj_ln(wbT, bb_col, True, [y2_scr[o] for o in range(NDC)],
                     bt_st=bt_st, g_sb=g_sb)
        p4s.close()

        if "y2" in taps:
            with nc.Block() as blk:
                @blk.gpsimd
                def _(g):
                    g.wait_ge(sscw.h, sscw.n)
                    sst.inc(g.dma_start(out=tap_out["y2"], in_=y2_scr), 16)

        if max_phase < 7:
            _finalize_dbg()
            lp.close()
            return nc, tap_out
        with contextlib.ExitStack() as p5:
            y2r = p5.enter_context(nc.sbuf_tensor("y2r", [128, 2 * TOK], BF16))
            outstg = p5.enter_context(nc.sbuf_tensor("outstg", [128, 2 * TOK], F32))
            b_sx2 = [sxp[0].n, sxp[1].n]
            b_st2 = [sstp[0].n, sstp[1].n]
            b_sdv = sdv.n
            with nc.Block() as blk:
                @blk.sync
                def _(sy):
                    sy.wait_ge(sev.h, sev.n)
                    sy.wait_ge(sscw.h, sscw.n)
                    for c in range(NDC):
                        if c >= 2:
                            sy.wait_ge(sdv.h, b_sdv + 2 * (c - 1))
                        ins = sy.dma_start(
                            out=y2r[:, (c % 2) * TOK:(c % 2) * TOK + TOK],
                            in_=y2_scr[c])
                        sxp[c % 2].inc(ins, 16)
                        if c >= 1:
                            sy.wait_ge(sdv.h, b_sdv + 2 * c)
                            ins = sy.dma_start(
                                out=outT[(c - 1) * 128:c * 128, :],
                                in_=outstg[:, ((c - 1) % 2) * TOK:
                                           ((c - 1) % 2) * TOK + TOK])
                            sstp[(c - 1) % 2].inc(ins, 16)
                    sy.wait_ge(sdv.h, b_sdv + 2 * NDC)
                    ins = sy.dma_start(
                        out=outT[(NDC - 1) * 128:NDC * 128, :],
                        in_=outstg[:, ((NDC - 1) % 2) * TOK:
                                   ((NDC - 1) % 2) * TOK + TOK])
                    sstp[(NDC - 1) % 2].inc(ins, 16)
                    sy.wait_ge(sstp[0].h, sstp[0].n)
                    sy.wait_ge(sstp[1].h, sstp[1].n)

                @blk.vector
                def _(v):
                    for c in range(NDC):
                        v.wait_ge(sxp[c % 2].h, b_sx2[c % 2] + 16 * (c // 2 + 1))
                        if c >= 2:
                            v.wait_ge(sstp[c % 2].h, b_st2[c % 2] + 16 * (c // 2))
                        dst = outstg[:, (c % 2) * TOK:(c % 2) * TOK + TOK]
                        v.tensor_mul(dst, y2r[:, (c % 2) * TOK:(c % 2) * TOK + TOK],
                                     rstd_bc[:, :])
                        sdv.inc(v.tensor_sub(dst, dst, mr_bc[:, :]), 2)

        lp.close()
    return nc, tap_out


def _rope_perm():
    perm = []
    for h in range(NH):
        base = h * HD
        perm += [base + 2 * i for i in range(64)] + [base + 2 * i + 1 for i in range(64)]
    for kh in range(NKV):
        base = NH * HD + kh * HD
        perm += [base + 2 * i for i in range(64)] + [base + 2 * i + 1 for i in range(64)]
    perm += list(range(5120, 6144))
    return np.asarray(perm)


def _core_blocks(c):
    return (0, 3) if c % 2 == 0 else (1, 2)


def make_in_maps(L0_attn_x, L0_attn_w_qkv, L0_attn_b_qkv, L0_attn_w_o, L0_attn_b_o,
                 L0_ffn_W_base, L0_ffn_b_base, L0_ffn_W_router, L0_ffn_A, L0_ffn_B):
    asc = np.ascontiguousarray
    x = np.asarray(L0_attn_x)
    perm = _rope_perm()
    wqkvT = asc(np.asarray(L0_attn_w_qkv)[perm].T)
    bqkv = np.asarray(L0_attn_b_qkv)[perm]
    bq_col = asc(bqkv[:5120].reshape(40, 128).T)
    bv_row = asc(bqkv[5120:].reshape(1, 1024))
    woT = asc(np.asarray(L0_attn_w_o).T)
    bo_col = asc(np.asarray(L0_attn_b_o).reshape(32, 128).T)
    wbT = asc(np.asarray(L0_ffn_W_base).T)
    bb_col = asc(np.asarray(L0_ffn_b_base).reshape(32, 128).T)
    wrT = asc(np.asarray(L0_ffn_W_router).T)
    At = asc(np.asarray(L0_ffn_A).transpose(2, 0, 1).reshape(D, 128))
    Bt = asc(np.asarray(L0_ffn_B).transpose(0, 2, 1).reshape(128, D))

    in_maps = []
    pos_per_core = []
    xT_cache = {}
    for c in range(8):
        b = c // 2
        if b not in xT_cache:
            xT_cache[b] = asc(x[b].T.astype(np.float32))
        blocks = _core_blocks(c)
        pos = np.concatenate([np.arange(1024 * blk, 1024 * blk + 1024)
                              for blk in blocks])
        pos_per_core.append(pos)
        in_maps.append({
            "xT_own": asc(xT_cache[b][:, pos]),
            "xT_full": xT_cache[b],
            "posq": pos.astype(np.float32).reshape(1, TOK),
            "wqkvT": wqkvT, "bq_col_d": bq_col, "bv_row_d": bv_row,
            "woT": woT, "bo_col_d": bo_col,
            "wbT": wbT, "bb_col_d": bb_col,
            "wrT": wrT, "At": At, "Bt": Bt,
        })
    return in_maps, pos_per_core


def kernel(**inputs):
    in_maps, pos_per_core = make_in_maps(**inputs)
    nc, _ = build_program(())
    res = run_bass_kernel_spmd(nc, in_maps, core_ids=list(range(8)))

    out = np.empty((4, S, D), dtype=np.float32)
    for c in range(8):
        b = c // 2
        out[b, pos_per_core[c], :] = res.results[c]["outT"].T
    return out


if __name__ == "__main__":
    nc, _ = build_program(())
    print("instructions:", nc.n_instructions())


# version 08
# speedup vs baseline: 1.1092x; 1.1092x over previous
import contextlib
import math
import os
DBG_NOSTAT = os.environ.get("DBG_NOSTAT", "0") == "1"
DBG_NOROWS = os.environ.get("DBG_NOROWS", "0") == "1"

import numpy as np
import ml_dtypes
import concourse.bass as bass
import concourse.mybir as mybir
from concourse.bass_utils import run_bass_kernel_spmd

F32 = mybir.dt.float32
BF16 = mybir.dt.bfloat16
I32 = mybir.dt.int32
AF = mybir.ActivationFunctionType
ALU = mybir.AluOpType
AX = mybir.AxisListType

D = 4096
NDC = 32
NH = 32
NKV = 8
HD = 128
S = 4096
TOK = 2048
NT = TOK // 512
C_SHIFT = 12.0
SM_SCALE = 1.0 / math.sqrt(HD)
EPS = 1e-5
LN1E4 = math.log(10000.0)
PI = math.pi
LOOP_LEN = (16, 32)
NSLOT = 2


class Sem:
    def __init__(self, nc, es, name):
        self.h = es.enter_context(nc.semaphore(name))
        self.n = 0

    def inc(self, ins, k=1):
        ins.then_inc(self.h, k)
        self.n += k
        return ins


def build_program(taps=(), max_phase=99):
    nc = bass.Bass("TRN2", target_bir_lowering=False,
                   detect_race_conditions=False)

    xT_own = nc.dram_tensor("xT_own", [D, TOK], F32, kind="ExternalInput").ap()
    xT_full = nc.dram_tensor("xT_full", [D, S], F32, kind="ExternalInput").ap()
    posq = nc.dram_tensor("posq", [1, TOK], F32, kind="ExternalInput").ap()
    wqkvT = nc.dram_tensor("wqkvT", [D, 6144], F32, kind="ExternalInput").ap()
    bq_col_d = nc.dram_tensor("bq_col_d", [128, 40], F32, kind="ExternalInput").ap()
    bv_row_d = nc.dram_tensor("bv_row_d", [1, 1024], F32, kind="ExternalInput").ap()
    woT = nc.dram_tensor("woT", [D, D], F32, kind="ExternalInput").ap()
    bo_col_d = nc.dram_tensor("bo_col_d", [128, 32], F32, kind="ExternalInput").ap()
    wbT = nc.dram_tensor("wbT", [D, D], F32, kind="ExternalInput").ap()
    bb_col_d = nc.dram_tensor("bb_col_d", [128, 32], F32, kind="ExternalInput").ap()
    wrT = nc.dram_tensor("wrT", [D, 8], F32, kind="ExternalInput").ap()
    At_d = nc.dram_tensor("At", [D, 128], F32, kind="ExternalInput").ap()
    Bt_d = nc.dram_tensor("Bt", [128, D], F32, kind="ExternalInput").ap()
    outT = nc.dram_tensor("outT", [D, TOK], F32, kind="ExternalOutput").ap()
    hsel = nc.dram_tensor("hsel", [1, 1], F32, kind="ExternalOutput").ap()

    q_scr = nc.dram_tensor("q_scr", [NH, 128, TOK], BF16, kind="Internal").ap()
    k_scr = nc.dram_tensor("k_scr", [NKV, 128, S], BF16, kind="Internal").ap()
    v_scr = nc.dram_tensor("v_scr", [S // 128, 128, NKV * HD], BF16,
                           kind="Internal").ap()
    attn_scr = nc.dram_tensor("attn_scr", [NH, 128, TOK], BF16, kind="Internal").ap()
    y1_scr = nc.dram_tensor("y1_scr", [NDC, 128, TOK], BF16, kind="Internal").ap()
    y2_scr = nc.dram_tensor("y2_scr", [NDC, 128, TOK], BF16, kind="Internal").ap()

    tap_out = {}
    for name, shape in (
        ("qkv_q", (NH, 128, TOK)), ("qkv_k", (NKV, 128, S)),
        ("qkv_v", (S // 128, 128, NKV * HD)),
        ("attn", (128, NDC * TOK)), ("y1", (NDC, 128, TOK)),
        ("y1n", (128, NDC * TOK)), ("y2", (NDC, 128, TOK)),
        ("lnr1", (1, 2 * TOK)), ("route", (8, TOK)),
        ("gsb", (128, TOK)), ("trig", (128, 3 * TOK)), ("p0c", (128, 8)),
        ("trigq", (128, 3 * TOK)),
    ):
        if name in taps:
            tap_out[name] = nc.dram_tensor("tap_" + name, list(shape), F32,
                                           kind="ExternalOutput").ap()

    _uid = [0]

    def _nm(base):
        _uid[0] += 1
        return f"{base}_{_uid[0]}"

    with contextlib.ExitStack() as es:
        iota_p = es.enter_context(nc.sbuf_tensor("iota_p", [128, 1], F32))
        ident_b = es.enter_context(nc.sbuf_tensor("ident_b", [128, 128], BF16))
        ident_f8 = es.enter_context(nc.sbuf_tensor("ident_f8", [8, 8], F32))
        ones_r1 = es.enter_context(nc.sbuf_tensor("ones_r1", [1, 128], BF16))
        ones_r1f = es.enter_context(nc.sbuf_tensor("ones_r1f", [1, 128], F32))
        ones_c1 = es.enter_context(nc.sbuf_tensor("ones_c1", [128, 1], BF16))
        ones_c1f = es.enter_context(nc.sbuf_tensor("ones_c1f", [128, 1], F32))
        negc = es.enter_context(nc.sbuf_tensor("negc", [128, 1], F32))
        invf = es.enter_context(nc.sbuf_tensor("invf", [128, 1], F32))
        epsb = es.enter_context(nc.sbuf_tensor("epsb", [1, 1], F32))
        bq_col = es.enter_context(nc.sbuf_tensor("bq_col", [128, 40], F32))
        bo_col = es.enter_context(nc.sbuf_tensor("bo_col", [128, 32], F32))
        bb_col = es.enter_context(nc.sbuf_tensor("bb_col", [128, 32], F32))
        bv_row = es.enter_context(nc.sbuf_tensor("bv_row", [1, 1024], BF16))
        ebc = es.enter_context(nc.sbuf_tensor("ebc", [8, 128], BF16))
        fslot = es.enter_context(nc.sbuf_tensor("fslot", [1, 8], F32))
        fcol = es.enter_context(nc.sbuf_tensor("fcol", [128, 1], F32))
        lnb = es.enter_context(nc.sbuf_tensor("lnb", [128, 1], F32))
        scol = es.enter_context(nc.sbuf_tensor("scol", [128, 1], F32))

        sx = Sem(nc, es, "sx")
        sw = Sem(nc, es, "sw")
        swc = Sem(nc, es, "swc")
        sps = Sem(nc, es, "sps")
        spst = Sem(nc, es, "spst")
        sev = Sem(nc, es, "sev")
        sdv = Sem(nc, es, "sdv")
        sscw = Sem(nc, es, "sscw")
        sq2 = Sem(nc, es, "sq2")
        skv = Sem(nc, es, "skv")
        ssc = Sem(nc, es, "ssc")
        sex = Sem(nc, es, "sex")
        smk = Sem(nc, es, "smk")
        sav = Sem(nc, es, "sav")
        sfin = Sem(nc, es, "sfin")
        sst = Sem(nc, es, "sst")
        sini = Sem(nc, es, "sini")
        swp = [Sem(nc, es, "swpa"), Sem(nc, es, "swpb")]
        sxp = [Sem(nc, es, "sxpa"), Sem(nc, es, "sxpb")]
        scp = [Sem(nc, es, "scpa"), Sem(nc, es, "scpb")]
        sstp = [Sem(nc, es, "sstpa"), Sem(nc, es, "sstpb")]
        sad = Sem(nc, es, "sad")
        sae = Sem(nc, es, "sae")
        srm = Sem(nc, es, "srm")
        sat = Sem(nc, es, "sat")
        shs = Sem(nc, es, "shs")
        spl = Sem(nc, es, "spl")
        smg = Sem(nc, es, "smg")

        ep = contextlib.ExitStack()
        posq_bc = ep.enter_context(nc.sbuf_tensor("posq_bc", [128, TOK], F32))
        ep1 = contextlib.ExitStack()
        big_a = ep1.enter_context(nc.sbuf_tensor("xbuf", [128, NDC * TOK], BF16))

        def vf(v, ap):
            return v.tensor_copy(fslot[0:1, 0:1], ap[0:1, 0:1])

        def sf(s_, ap):
            return s_.copy(fslot[0:1, 2:3], ap[0:1, 0:1])

        def gf(g_, ap):
            return g_.tensor_copy(fcol[:, 0:1], ap[:, 0:1])

        def _finalize_dbg():
            with nc.Block() as blk:
                @blk.sync
                def _(sy):
                    sy.wait_ge(sscw.h, sscw.n)
                    sy.wait_ge(sst.h, sst.n)
                    sy.wait_ge(sdv.h, sdv.n)

        with contextlib.ExitStack() as p0:
            t0 = p0.enter_context(nc.sbuf_tensor("t0", [128, 128], F32))
            t1 = p0.enter_context(nc.sbuf_tensor("t1", [128, 128], F32))
            iota128 = p0.enter_context(nc.sbuf_tensor("iota128", [128, 128], F32))
            posq_row = p0.enter_context(nc.sbuf_tensor("posq_row", [1, TOK], F32))
            identf_full = p0.enter_context(nc.sbuf_tensor("identf_full",
                                                          [128, 128], F32))
            pq_ps = p0.enter_context(nc.psum_tensor("pq_ps", [128, TOK], F32))

            with nc.Block() as blk:
                @blk.gpsimd
                def _(g):
                    for dst, src in ((posq_row[:, :], posq), (bq_col[:, :], bq_col_d),
                                     (bo_col[:, :], bo_col_d), (bb_col[:, :], bb_col_d),
                                     (bv_row[:, :], bv_row_d)):
                        sx.inc(g.dma_start(out=dst, in_=src), 16)
                    g.iota(iota_p[:, :], [[0, 1]], channel_multiplier=1,
                           allow_small_or_imprecise_dtypes=True)
                    g.iota(iota128[:, :], [[1, 128]], channel_multiplier=0,
                           allow_small_or_imprecise_dtypes=True)
                    g.memset(ones_r1[:, :], 1.0)
                    g.memset(ones_r1f[:, :], 1.0)
                    g.memset(ones_c1[:, :], 1.0)
                    g.memset(ones_c1f[:, :], 1.0)
                    g.memset(epsb[:, :], EPS)
                    g.memset(lnb[:, :], LN1E4)
                    sini.inc(g.memset(negc[:, :], -C_SHIFT))

                @blk.vector
                def _(v):
                    def spacer():
                        v.tensor_copy(scol[:, 0:1], iota_p[:, 0:1])
                    v.wait_ge(sini.h, 1)
                    v.tensor_scalar(identf_full[:, :], iota128[:, :], iota_p[:, 0:1],
                                    None, ALU.is_equal)
                    v.tensor_scalar(t1[0:8, :], iota128[0:8, :], 1.0 / 16.0,
                                    iota_p[0:8, 0:1], ALU.mult, ALU.subtract)
                    v.tensor_copy(ident_b[:, :], identf_full[:, :])
                    v.tensor_scalar(t0[0:8, :], t1[0:8, :], 0.0, None, ALU.is_ge)
                    v.tensor_copy(ident_f8[:, :], identf_full[0:8, 0:8])
                    v.tensor_scalar(t1[0:8, :], t1[0:8, :], 0.999, None, ALU.is_lt)
                    spacer()
                    v.tensor_mul(ebc[0:8, :], t0[0:8, :], t1[0:8, :])
                    sdv.inc(vf(v, ebc))

                b_sdv0 = sdv.n

                @blk.scalar
                def _(s):
                    s.wait_ge(sini.h, 1)
                    s.activation(invf[0:64, :], iota_p[0:64, 0:1], AF.Exp,
                                 bias=0.0, scale=-LN1E4 / 64.0)
                    s.activation(invf[64:128, :], iota_p[64:128, 0:1], AF.Exp,
                                 bias=lnb[64:128, 0:1], scale=-LN1E4 / 64.0)
                    sini.inc(sf(s, invf))

                @blk.tensor
                def _(t):
                    t.wait_ge(sx.h, 80)
                    ins = None
                    for i in range(NT):
                        ins = t.matmul(pq_ps[:, i * 512:(i + 1) * 512], ones_r1f[:, :],
                                       posq_row[:, i * 512:(i + 1) * 512],
                                       start=True, stop=True)
                    sps.inc(ins)

                b_sps0 = sps.n

                @blk.scalar
                def _(s):
                    s.wait_ge(sps.h, b_sps0)
                    s.copy(posq_bc[:, :], pq_ps[:, :])
                    sini.inc(sf(s, posq_bc))

        if "p0c" in taps:
            with nc.Block() as blk:
                @blk.gpsimd
                def _(g):
                    g.wait_ge(sini.h, sini.n)
                    with nc.allow_non_contiguous_dma(reason="debug"):
                        sst.inc(g.dma_start(out=tap_out["p0c"][:, 0:1],
                                            in_=iota_p[:, :],
                                            allow_slow_non_contiguous=True), 16)
                        sst.inc(g.dma_start(out=tap_out["p0c"][:, 4:5],
                                            in_=invf[:, :],
                                            allow_slow_non_contiguous=True), 16)

        cs_scope = contextlib.ExitStack()
        cos_t = cs_scope.enter_context(nc.sbuf_tensor("cos_t", [128, TOK], BF16))
        sin_t = cs_scope.enter_context(nc.sbuf_tensor("sin_t", [128, TOK], BF16))

        def emit_trig(pass_i):
            with contextlib.ExitStack() as ts_:
                ang = ts_.enter_context(nc.sbuf_tensor(_nm("ang"), [128, TOK], F32))
                u = ts_.enter_context(nc.sbuf_tensor(_nm("u"), [128, TOK], F32))
                ki = ts_.enter_context(nc.sbuf_tensor(_nm("ki"), [128, TOK], I32))
                kf = ts_.enter_context(nc.sbuf_tensor(_nm("kf"), [128, TOK], F32))
                b_sev = sev.n
                with nc.Block() as blk:
                    if pass_i < 2:
                        @blk.gpsimd
                        def _(g):
                            g.wait_ge(sini.h, sini.n)
                            sini.inc(g.iota(ang[:, :], [[1, TOK]],
                                            base=pass_i * TOK,
                                            channel_multiplier=0,
                                            allow_small_or_imprecise_dtypes=True))

                    @blk.vector
                    def _(v):
                        v.wait_ge(sini.h, sini.n)
                        src = ang if pass_i < 2 else posq_bc
                        v.tensor_scalar(ang[:, :], src[:, :], invf[:, 0:1],
                                        None, ALU.mult)
                        for which in (0, 1):
                            if which == 0:
                                v.tensor_copy(u[:, :], ang[:, :])
                            else:
                                v.wait_ge(sev.h, b_sev + 1)
                                v.tensor_scalar(u[:, :], ang[:, :], PI / 2.0,
                                                None, ALU.add)
                            v.tensor_scalar(ki[:, :], u[:, :], 1.0 / (2 * PI),
                                            None, ALU.mult)
                            v.tensor_copy(kf[:, :], ki[:, :])
                            v.scalar_tensor_tensor(u[:, :], kf[:, :], -2 * PI,
                                                   u[:, :], ALU.mult, ALU.add)
                            v.tensor_scalar(kf[:, :], u[:, :], PI, None, ALU.is_ge)
                            v.scalar_tensor_tensor(u[:, :], kf[:, :], -2 * PI,
                                                   u[:, :], ALU.mult, ALU.add)
                            v.tensor_scalar(kf[:, :], u[:, :], -PI, None, ALU.is_lt)
                            v.scalar_tensor_tensor(
                                u[:, :], kf[:, :], 2 * PI, u[:, :],
                                ALU.mult, ALU.add)
                            sdv.inc(vf(v, u))

                    b_sdv = sdv.n - 2

                    @blk.scalar
                    def _(s):
                        s.wait_ge(sdv.h, b_sdv + 1)
                        s.activation(sin_t[:, :], u[:, :], AF.Sin,
                                     bias=0.0, scale=1.0)
                        sev.inc(sf(s, sin_t))
                        s.wait_ge(sdv.h, b_sdv + 2)
                        s.activation(cos_t[:, :], u[:, :], AF.Sin,
                                     bias=0.0, scale=1.0)
                        sev.inc(sf(s, cos_t))

        def emit_load_x(src_ap, wait_sev_to):
            with nc.Block() as blk:
                @blk.gpsimd
                def _(g):
                    g.wait_ge(sev.h, wait_sev_to)
                    for i in range(8):
                        ins = g.dma_start(
                            out=big_a[:, :].rearrange("p (c t) -> p c t", c=NDC)
                                [:, 4 * i:4 * (i + 1), :],
                            in_=src_ap[i * 512:(i + 1) * 512, :]
                                .rearrange("(c p) t -> p c t", p=128))
                        sx.inc(ins, 16)

        def emit_qk_loop(jlist, dsts, dst_off):
            b_sx = sx.n
            with contextlib.ExitStack() as ls:
                stage_w = ls.enter_context(
                    nc.sbuf_tensor(_nm("stage_w1"), [128, 2 * NDC * 128], BF16))
                evac = ls.enter_context(nc.sbuf_tensor(_nm("evac1"), [128, 2 * 512], F32))
                ropa = ls.enter_context(nc.sbuf_tensor(_nm("ropa"), [128, 512], F32))
                ropb = ls.enter_context(nc.sbuf_tensor(_nm("ropb"), [128, 512], F32))
                kq_st = ls.enter_context(nc.sbuf_tensor(_nm("kq_st"), [128, 2 * TOK], BF16))
                ps = ls.enter_context(nc.psum_tensor(_nm("p1_ps"), [128, 2 * 512], F32))
                nJ = len(jlist)
                b_sw = [swp[0].n, swp[1].n]
                b_sc = [scp[0].n, scp[1].n]
                b_sps, b_sev, b_sdv = sps.n, sev.n, sdv.n

                with nc.Block() as blk:
                    @blk.gpsimd
                    def _(g):
                        g.wait_ge(sx.h, b_sx)
                        for idx, (jc, _b) in enumerate(jlist):
                            if idx >= 2:
                                g.wait_ge(sps.h, b_sps + NT * (idx - 1))
                            ins = g.dma_start(
                                out=stage_w[:, :]
                                    .rearrange("p (b c j) -> p b c j", b=2, c=NDC)
                                    [:, idx % 2, :, :],
                                in_=wqkvT[:, jc * 128:(jc + 1) * 128]
                                    .rearrange("(c p) j -> p c j", p=128))
                            swp[idx % 2].inc(ins, 16)

                    @blk.tensor
                    def _(t):
                        t.wait_ge(sx.h, b_sx)
                        for idx in range(nJ):
                            t.wait_ge(swp[idx % 2].h,
                                      b_sw[idx % 2] + 16 * (idx // 2 + 1))
                            for nt in range(NT):
                                gi = idx * NT + nt
                                if gi >= 2:
                                    t.wait_ge(sev.h, b_sev + gi - 1)
                                ins = None
                                for c in range(NDC):
                                    ins = t.matmul(
                                        ps[:, (gi % 2) * 512:(gi % 2) * 512 + 512],
                                        stage_w[:, (idx % 2) * NDC * 128 + c * 128:
                                                (idx % 2) * NDC * 128 + c * 128 + 128],
                                        big_a[:, c * TOK + nt * 512:
                                              c * TOK + nt * 512 + 512],
                                        start=(c == 0), stop=(c == NDC - 1))
                                sps.inc(ins)

                    @blk.scalar
                    def _(s):
                        for idx, (jc, bcol) in enumerate(jlist):
                            for nt in range(NT):
                                gi = idx * NT + nt
                                s.wait_ge(sps.h, b_sps + gi + 1)
                                if gi >= 2:
                                    s.wait_ge(sdv.h, b_sdv + gi - 1)
                                sev.inc(s.activation(
                                    evac[:, (gi % 2) * 512:(gi % 2) * 512 + 512],
                                    ps[:, (gi % 2) * 512:(gi % 2) * 512 + 512],
                                    AF.Identity, bias=bq_col[:, bcol:bcol + 1],
                                    scale=1.0))

                    @blk.vector
                    def _(v):
                        for idx in range(nJ):
                            for nt in range(NT):
                                gi = idx * NT + nt
                                v.wait_ge(sev.h, b_sev + gi + 1)
                                if idx >= 2 and nt == 0:
                                    v.wait_ge(scp[idx % 2].h,
                                              b_sc[idx % 2] + 16 * (idx // 2))
                                ev = evac[:, (gi % 2) * 512:(gi % 2) * 512 + 512]
                                co = cos_t[:, nt * 512:nt * 512 + 512]
                                si = sin_t[:, nt * 512:nt * 512 + 512]
                                ks = kq_st[:, (idx % 2) * TOK + nt * 512:
                                           (idx % 2) * TOK + nt * 512 + 512]
                                v.tensor_mul(ropa[:, :], ev, co)
                                v.tensor_mul(ropb[0:64, :], ev[64:128, :],
                                             si[64:128, :])
                                v.tensor_mul(ropb[64:128, :], ev[0:64, :],
                                             si[0:64, :])
                                v.tensor_sub(ks[0:64, :], ropa[0:64, :],
                                             ropb[0:64, :])
                                sdv.inc(v.tensor_add(ks[64:128, :],
                                                     ropb[64:128, :],
                                                     ropa[64:128, :]))

                    @blk.sync
                    def _(sy):
                        for idx in range(nJ):
                            sy.wait_ge(sdv.h, b_sdv + NT * (idx + 1))
                            ins = sy.dma_start(
                                out=dsts[idx][:, dst_off:dst_off + TOK],
                                in_=kq_st[:, (idx % 2) * TOK:(idx % 2) * TOK + TOK])
                            scp[idx % 2].inc(ins, 16)

        def emit_v_loop(pass_i):
            b_sx = sx.n
            with contextlib.ExitStack() as ls:
                stage_wv = ls.enter_context(
                    nc.sbuf_tensor(_nm("stage_wv"), [128, NDC * 512], BF16))
                v_st = ls.enter_context(nc.sbuf_tensor(_nm("v_st"), [128, 8 * 512], BF16))
                ps = ls.enter_context(nc.psum_tensor(_nm("pv_ps"), [128, 2 * 512], F32))
                b_sw, b_sps, b_sev, b_sscw, b_swc = sw.n, sps.n, sev.n, sscw.n, swc.n

                with nc.Block() as blk:
                    @blk.gpsimd
                    def _(g):
                        g.wait_ge(sx.h, b_sx)
                        for jb in range(2):
                            if jb >= 1:
                                g.wait_ge(sps.h, b_sps + 16 * jb)
                            ins = g.dma_start(
                                out=stage_wv[:, :]
                                    .rearrange("p (c j) -> p c j", c=NDC),
                                in_=wqkvT[:, 5120 + jb * 512:5120 + jb * 512 + 512]
                                    .rearrange("(c p) j -> p c j", p=128))
                            sw.inc(ins, 16)

                    @blk.tensor
                    def _(t):
                        t.wait_ge(sx.h, b_sx)
                        for jb in range(2):
                            t.wait_ge(sw.h, b_sw + 16 * (jb + 1))
                            for tt in range(16):
                                gi = jb * 16 + tt
                                if gi >= 2:
                                    t.wait_ge(sev.h, b_sev + gi - 1)
                                for c in range(NDC):
                                    t.matmul(
                                        ps[:, (gi % 2) * 512:(gi % 2) * 512 + 512],
                                        big_a[:, c * TOK + tt * 128:
                                              c * TOK + tt * 128 + 128],
                                        stage_wv[:, c * 512:c * 512 + 512],
                                        start=(c == 0), stop=False)
                                ins = t.matmul(
                                    ps[:, (gi % 2) * 512:(gi % 2) * 512 + 512],
                                    ones_r1[:, :],
                                    bv_row[0:1, jb * 512:jb * 512 + 512],
                                    start=False, stop=True)
                                sps.inc(ins)

                    @blk.scalar
                    def _(s):
                        for jb in range(2):
                            for tt in range(16):
                                gi = jb * 16 + tt
                                s.wait_ge(sps.h, b_sps + gi + 1)
                                half_i = gi // 8
                                if half_i >= 1 and gi % 8 == 0:
                                    s.wait_ge(sscw.h, b_sscw + 16 * half_i)
                                sev.inc(s.copy(
                                    v_st[:, (tt % 8) * 512:(tt % 8) * 512 + 512],
                                    ps[:, (gi % 2) * 512:(gi % 2) * 512 + 512]))

                    @blk.sync
                    def _(sy):
                        for jb in range(2):
                            for h8 in range(2):
                                sy.wait_ge(sev.h, b_sev + jb * 16 + h8 * 8 + 8)
                                t0g = pass_i * 16 + h8 * 8
                                ins = sy.dma_start(
                                    out=v_scr[t0g:t0g + 8, :, jb * 512:jb * 512 + 512]
                                        .rearrange("t p j -> p t j"),
                                    in_=v_st[:, :].rearrange("p (t j) -> p t j", t=8))
                                sscw.inc(ins, 16)

        for pass_i in range(2):
            emit_trig(pass_i)
            if pass_i == 0 and "trig" in taps:
                with nc.Block() as blk:
                    @blk.gpsimd
                    def _(g):
                        g.wait_ge(sev.h, sev.n)
                        sst.inc(g.dma_start(out=tap_out["trig"][:, 0:TOK],
                                            in_=sin_t[:, :]), 16)
                        sst.inc(g.dma_start(out=tap_out["trig"][:, TOK:2 * TOK],
                                            in_=cos_t[:, :]), 16)
                        with nc.allow_non_contiguous_dma(reason="debug tap"):
                            sst.inc(g.dma_start(
                                out=tap_out["trig"][:, 2 * TOK:2 * TOK + 1],
                                in_=invf[:, :],
                                allow_slow_non_contiguous=True), 16)
            emit_load_x(xT_full[:, pass_i * TOK:(pass_i + 1) * TOK], sev.n)
            emit_qk_loop([(32 + jk, 32 + jk) for jk in range(NKV)],
                         [k_scr[jk] for jk in range(NKV)], pass_i * TOK)
            emit_v_loop(pass_i)
        emit_trig(2)
        if "trigq" in taps:
            with nc.Block() as blk:
                @blk.gpsimd
                def _(g):
                    g.wait_ge(sev.h, sev.n)
                    sst.inc(g.dma_start(out=tap_out["trigq"][:, 0:TOK],
                                        in_=sin_t[:, :]), 16)
                    sst.inc(g.dma_start(out=tap_out["trigq"][:, TOK:2 * TOK],
                                        in_=cos_t[:, :]), 16)
                    sst.inc(g.dma_start(out=tap_out["trigq"][:, 2 * TOK:3 * TOK],
                                        in_=posq_bc[:, :]), 16)
        emit_load_x(xT_own, sev.n)
        emit_qk_loop([(jq, jq) for jq in range(NH)],
                     [q_scr[jq] for jq in range(NH)], 0)

        cs_scope.close()

        if "qkv_q" in taps or "qkv_k" in taps or "qkv_v" in taps:
            with nc.Block() as blk:
                @blk.gpsimd
                def _(g):
                    g.wait_ge(sscw.h, sscw.n)
                    for nm, src in (("qkv_q", q_scr), ("qkv_k", k_scr),
                                    ("qkv_v", v_scr)):
                        if nm in taps:
                            sst.inc(g.dma_start(out=tap_out[nm], in_=src), 16)

        if max_phase < 2:
            _finalize_dbg()
            ep.close()
            return nc, tap_out
        ep1.close()
        with contextlib.ExitStack() as p2:
            masks = p2.enter_context(nc.sbuf_tensor("masks", [128, 8 * 1024], BF16))
            k_g = p2.enter_context(nc.sbuf_tensor("k_g", [128, 2 * S], BF16))
            v_g = p2.enter_context(nc.sbuf_tensor("v_g", [128, 2 * S], BF16))
            NEXP = 8
            expb4 = [p2.enter_context(nc.sbuf_tensor(f"expb{j}", [128, 2 * 1024],
                                                     BF16)) for j in range(4)]

            def ebsl(vi, hf=None):
                o = ((vi // 4) % 2) * 1024
                if hf is None:
                    return expb4[vi % 4][:, o:o + 1024]
                return expb4[vi % 4][:, o + hf * 512:o + hf * 512 + 512]
            acc2 = [p2.enter_context(nc.sbuf_tensor(f"acc{i}", [128, 1024], F32))
                    for i in range(3)]
            accb2 = [p2.enter_context(nc.sbuf_tensor(f"accb{i}", [128, 1024], F32))
                     for i in range(2)]
            rrow = p2.enter_context(nc.sbuf_tensor("rrow_t", [1, 1024], F32))
            av_sb = p2.enter_context(nc.sbuf_tensor("av_sb", [128, 2 * 1024], BF16))
            att_st = p2.enter_context(nc.sbuf_tensor("att_st", [128, 2 * 1024], BF16))
            q_t = p2.enter_context(nc.sbuf_tensor("q_t2", [128, 2 * 1024], BF16))
            ps_sc = p2.enter_context(nc.psum_tensor("ps_sc", [128, 3 * 1024], F32))
            ps_av = p2.enter_context(nc.psum_tensor("ps_av", [128, 1024], F32))

            hv = p2.enter_context(nc.sbuf_tensor("hv", [1, 1], F32))

            def emit_p2(LOOP_LEN, varid):
                heads = [(sl, g, qh) for sl in range(NSLOT) for g in range(NKV)
                         for qh in range(4)]
                vstart = {}
                _vv = 0
                for h_ in heads:
                    vstart[h_] = _vv
                    _vv += LOOP_LEN[h_[0]]
                NVIS = _vv
                vhead = [0] * NVIS
                vkk = [0] * NVIS
                for hidx, h_ in enumerate(heads):
                    for kk in range(LOOP_LEN[h_[0]]):
                        vhead[vstart[h_] + kk] = hidx
                        vkk[vstart[h_] + kk] = kk
                K0 = 5
                vslot = [(k % 2) if k < K0 else [2, 0, 1][(k - K0) % 3] for k in vkk]
                vmask = [False] * NVIS
                vpar = [0] * NVIS
                mcnt = [0] * NVIS
                acnt = [[0] * NVIS, [0] * NVIS]
                firstpar = {}
                mc = 0
                ac = [0, 0]
                for vi in range(NVIS):
                    hidx = vhead[vi]
                    sl = heads[hidx][0]
                    L = LOOP_LEN[sl]
                    kk = vkk[vi]
                    vmask[vi] = (kk - (L - 8)) >= 0
                    if sl == 0:
                        par = 0 if (kk % 16) in (0, 2, 4, 6, 9, 11, 13) else 1
                    else:
                        par = 0 if (kk % 8) in (0, 2, 4, 6, 7) else 1
                    vpar[vi] = par
                    if (hidx, par) not in firstpar:
                        firstpar[(hidx, par)] = vi
                    if vmask[vi]:
                        mc += 1
                    ac[par] += 1
                    mcnt[vi] = mc
                    acnt[0][vi] = ac[0]
                    acnt[1][vi] = ac[1]

                b_ssc, b_sex, b_smk, b_sav = ssc.n, sex.n, smk.n, sav.n
                b_sfin, b_sev = sfin.n, sev.n
                b_sq2, b_skv = sq2.n, skv.n
                b_sad, b_spl, b_smg, b_sae = sad.n, spl.n, smg.n, sae.n
                b_srm, b_sdv = srm.n, sdv.n
                b_scp = [scp[0].n, scp[1].n]

                def wait_exp_consumed(eng, vi):
                    pv = vi - NEXP
                    if pv < 0:
                        return
                    eng.wait_ge(sav.h, b_sav + pv + 1)
                    if vpar[pv] == 0:
                        eng.wait_ge(sad.h, b_sad + acnt[0][pv])
                    else:
                        eng.wait_ge(spl.h, b_spl + acnt[1][pv])

                def wait_masked_exp(eng, vi):
                    if vmask[vi]:
                        eng.wait_ge(smk.h, b_smk + mcnt[vi])
                    else:
                        eng.wait_ge(sex.h, b_sex + vi + 1)

                pe_slot_user = {}

                def pe_score_wait(t, slot):
                    u = pe_slot_user.get(slot)
                    if u is None:
                        return
                    if u[0] == "exp":
                        t.wait_ge(sex.h, b_sex + u[1] + 1)
                    else:
                        assert u[0] == "fin"
                        t.wait_ge(srm.h, b_srm + u[1] + 1)

                def pe_av(t, pv):
                    ph = vhead[pv]
                    psl = heads[ph][0]
                    ggi = psl * NKV + heads[ph][1]
                    kkp = vkk[pv]
                    Lp = LOOP_LEN[psl]
                    wait_masked_exp(t, pv)
                    if kkp == 0 and ph >= 1:
                        t.wait_ge(sae.h, b_sae + ph)
                    ins = None
                    for hf in range(2):
                        ins = t.matmul(
                            ps_av[:, hf * 512:hf * 512 + 512],
                            v_g[:, (ggi % 2) * S + kkp * 128:
                                (ggi % 2) * S + kkp * 128 + 128],
                            ebsl(pv, hf),
                            start=(kkp == 0), stop=(kkp == Lp - 1),
                            skip_group_check=True)
                    sav.inc(ins)

                def pe_fin_den(t, h2):
                    t.wait_ge(smg.h, b_smg + h2 + 1)
                    pe_score_wait(t, 2)
                    ins = None
                    for hf in range(2):
                        ins = t.matmul(
                            ps_sc[0:1, 2048 + hf * 512:2048 + hf * 512 + 512],
                            ones_c1f[:, :],
                            acc2[h2 % 3][:, hf * 512:hf * 512 + 512],
                            start=True, stop=True, skip_group_check=True)
                    sfin.inc(ins)

                def pe_fin_bc(t, h2):
                    t.wait_ge(sev.h, b_sev + h2 + 1)
                    ins = None
                    for hf in range(2):
                        ins = t.matmul(
                            ps_sc[:, 2048 + hf * 512:2048 + hf * 512 + 512],
                            ones_r1f[:, :],
                            rrow[0:1, hf * 512:hf * 512 + 512],
                            start=True, stop=True, skip_group_check=True)
                    sfin.inc(ins)
                    pe_slot_user[2] = ("fin", h2)

                def act_avevac(s, h2):
                    s.wait_ge(sav.h, b_sav + vstart[heads[h2]]
                              + LOOP_LEN[heads[h2][0]])
                    if h2 >= 2:
                        s.wait_ge(srm.h, b_srm + h2 - 1)
                    s.copy(av_sb[:, (h2 % 2) * 1024:(h2 % 2) * 1024 + 1024],
                           ps_av[:, :])
                    sae.inc(sf(s, av_sb))

                def act_denrow(s, h2):
                    s.wait_ge(sfin.h, b_sfin + 2 * h2 + 1)
                    s.copy(rrow[0:1, :], ps_sc[0:1, 2048:3072])
                    sev.inc(sf(s, rrow))

                def dve_merge(v, h2):
                    L2 = LOOP_LEN[heads[h2][0]]
                    vs2 = vstart[heads[h2]]
                    v.wait_ge(spl.h, b_spl + acnt[1][vs2 + L2 - 1])
                    smg.inc(v.tensor_add(acc2[h2 % 3][:, :], acc2[h2 % 3][:, :],
                                         accb2[h2 % 2][:, :]))

                def dve_recip_mul(v, h2):
                    v.wait_ge(sfin.h, b_sfin + 2 * h2 + 2)
                    v.reciprocal(acc2[h2 % 3][:, :], ps_sc[:, 2048:3072])
                    v.wait_ge(sae.h, b_sae + h2 + 1)
                    if h2 >= 2:
                        v.wait_ge(scp[h2 % 2].h, b_scp[h2 % 2] + 16 * (h2 // 2))
                    srm.inc(v.tensor_mul(
                        att_st[:, (h2 % 2) * 1024:(h2 % 2) * 1024 + 1024],
                        av_sb[:, (h2 % 2) * 1024:(h2 % 2) * 1024 + 1024],
                        acc2[h2 % 3][:, :]))

                def st_store(sy, ph):
                    psl2, pg2, pqh2 = heads[ph]
                    phead = pg2 * 4 + pqh2
                    sy.wait_ge(srm.h, b_srm + ph + 1)
                    ins = sy.dma_start(
                        out=attn_scr[phead][:, psl2 * 1024:psl2 * 1024 + 1024],
                        in_=att_st[:, (ph % 2) * 1024:(ph % 2) * 1024 + 1024])
                    scp[ph % 2].inc(ins, 16)

                for sl in range(NSLOT):
                    L = LOOP_LEN[sl]
                    with nc.Block() as blk:
                        @blk.vector
                        def _(v, sl=sl, L=L):
                            ins = None
                            for r in range(8):
                                kk0 = float((L - 8 + r) * 128)
                                ins = v.tensor_scalar(
                                    masks[:, r * 1024:(r + 1) * 1024],
                                    posq_bc[:, sl * 1024:(sl + 1) * 1024],
                                    kk0, iota_p[:, 0:1], ALU.subtract, ALU.is_ge)
                            sdv.inc(ins)

                    with nc.Block() as blk:
                        @blk.gpsimd
                        def _(g_, sl=sl, L=L):
                            for g in range(NKV):
                                for qh in range(4):
                                    hidx = sl * 32 + g * 4 + qh
                                    vs = vstart[(sl, g, qh)]
                                    for kk in range(L):
                                        vi = vs + kk
                                        if vpar[vi] != 1:
                                            continue
                                        wait_masked_exp(g_, vi)
                                        eb = ebsl(vi)
                                        if vi == firstpar[(hidx, 1)]:
                                            if hidx >= 2:
                                                g_.wait_ge(smg.h, b_smg + hidx - 1)
                                            spl.inc(g_.tensor_copy(
                                                accb2[hidx % 2][:, :], eb))
                                        else:
                                            spl.inc(g_.tensor_tensor(
                                                accb2[hidx % 2][:, :],
                                                accb2[hidx % 2][:, :], eb, ALU.add))

                        @blk.sync
                        def _(sy, sl=sl, L=L):
                            for g in range(NKV):
                                for qh in range(4):
                                    hidx = sl * 32 + g * 4 + qh
                                    head = g * 4 + qh
                                    if hidx >= 2:
                                        prev = heads[hidx - 2]
                                        sy.wait_ge(ssc.h, b_ssc + vstart[prev]
                                                   + LOOP_LEN[prev[0]])
                                    ins = sy.dma_start(
                                        out=q_t[:, (hidx % 2) * 1024:
                                                (hidx % 2) * 1024 + 1024],
                                        in_=q_scr[head][:, sl * 1024:sl * 1024 + 1024])
                                    sq2.inc(ins, 16)
                                    if hidx >= 2:
                                        st_store(sy, hidx - 2)
                            if sl == NSLOT - 1:
                                st_store(sy, 62)
                                st_store(sy, 63)

                        @blk.tensor
                        def _(t, sl=sl, L=L):
                            t.wait_ge(sdv.h, b_sdv + sl + 1)
                            for g in range(NKV):
                                gidx = sl * NKV + g
                                for qh in range(4):
                                    hidx = sl * 32 + g * 4 + qh
                                    vs = vstart[(sl, g, qh)]
                                    for kk in range(L):
                                        vi = vs + kk
                                        slot = vslot[vi]
                                        if kk == 0:
                                            t.wait_ge(skv.h, b_skv + 32 * (gidx + 1))
                                            t.wait_ge(sq2.h, b_sq2 + 16 * (hidx + 1))
                                        pe_score_wait(t, slot)
                                        ins = None
                                        for hf in range(2):
                                            ins = t.matmul(
                                                ps_sc[:, slot * 1024 + hf * 512:
                                                      slot * 1024 + hf * 512 + 512],
                                                k_g[:, (gidx % 2) * S + kk * 128:
                                                    (gidx % 2) * S + kk * 128 + 128],
                                                q_t[:, (hidx % 2) * 1024 + hf * 512:
                                                    (hidx % 2) * 1024 + hf * 512 + 512],
                                                start=True, stop=True,
                                                skip_group_check=True)
                                        ssc.inc(ins)
                                        pe_slot_user[slot] = ("exp", vi)
                                        if vi >= 2:
                                            pe_av(t, vi - 2)
                                        if hidx >= 2:
                                            if kk == 0:
                                                pe_fin_den(t, hidx - 2)
                                            elif kk == 1:
                                                pe_fin_bc(t, hidx - 2)
                            if sl == NSLOT - 1:
                                pe_av(t, NVIS - 2)
                                pe_av(t, NVIS - 1)
                                pe_fin_den(t, 62)
                                pe_fin_bc(t, 62)
                                pe_fin_den(t, 63)
                                pe_fin_bc(t, 63)

                        @blk.scalar
                        def _(s, sl=sl, L=L):
                            def kv_dma(gp):
                                g2 = gp % NKV
                                Lg = LOOP_LEN[gp // NKV]
                                if gp >= 2:
                                    s.wait_ge(sae.h, b_sae + 4 * (gp - 1))
                                ins = s.dma_start(
                                    out=k_g[:, (gp % 2) * S:(gp % 2) * S + Lg * 128],
                                    in_=k_scr[g2][:, 0:Lg * 128])
                                skv.inc(ins, 16)
                                ins = s.dma_start(
                                    out=v_g[:, (gp % 2) * S:(gp % 2) * S + Lg * 128]
                                        .rearrange("p (t h) -> p t h", t=Lg),
                                    in_=v_scr[0:Lg, :, g2 * 128:(g2 + 1) * 128]
                                        .rearrange("t p h -> p t h"))
                                skv.inc(ins, 16)

                            if sl == 0:
                                kv_dma(0)
                                kv_dma(1)
                            for g in range(NKV):
                                for qh in range(4):
                                    hidx = sl * 32 + g * 4 + qh
                                    vs = vstart[(sl, g, qh)]
                                    for kk in range(L):
                                        vi = vs + kk
                                        slot = vslot[vi]
                                        s.wait_ge(ssc.h, b_ssc + vi + 1)
                                        wait_exp_consumed(s, vi)
                                        sex.inc(s.activation(
                                            ebsl(vi),
                                            ps_sc[:, slot * 1024:slot * 1024 + 1024],
                                            AF.Exp, bias=negc[:, 0:1],
                                            scale=SM_SCALE))
                                        if kk == 0:
                                            if hidx >= 1:
                                                act_avevac(s, hidx - 1)
                                            if hidx % 4 == 0 and 2 <= hidx // 4 + 1 <= 15:
                                                kv_dma(hidx // 4 + 1)
                                        elif kk == 1 and hidx >= 2:
                                            act_denrow(s, hidx - 2)
                            if sl == NSLOT - 1:
                                act_avevac(s, 63)
                                act_denrow(s, 62)
                                act_denrow(s, 63)

                        @blk.vector
                        def _(v, sl=sl, L=L):
                            def dve_add(vj):
                                if vj < 0 or vpar[vj] != 0:
                                    return
                                hj = vhead[vj]
                                eb = ebsl(vj)
                                if not vmask[vj]:
                                    v.wait_ge(sex.h, b_sex + vj + 1)
                                if vj == firstpar[(hj, 0)]:
                                    sad.inc(v.tensor_copy(acc2[hj % 3][:, :], eb))
                                else:
                                    sad.inc(v.tensor_add(acc2[hj % 3][:, :],
                                                         acc2[hj % 3][:, :], eb))

                            for g in range(NKV):
                                for qh in range(4):
                                    hidx = sl * 32 + g * 4 + qh
                                    vs = vstart[(sl, g, qh)]
                                    for kk in range(L):
                                        vi = vs + kk
                                        if vmask[vi]:
                                            v.wait_ge(sex.h, b_sex + vi + 1)
                                            rel = kk - (L - 8)
                                            eb = ebsl(vi)
                                            smk.inc(v.tensor_mul(
                                                eb, eb,
                                                masks[:, rel * 1024:rel * 1024 + 1024]))
                                        dve_add(vi - 2)
                                        if kk == 1 and hidx >= 1:
                                            dve_merge(v, hidx - 1)
                                        elif kk == 2 and hidx >= 2:
                                            dve_recip_mul(v, hidx - 2)
                            if sl == NSLOT - 1:
                                dve_add(NVIS - 2)
                                dve_add(NVIS - 1)
                                dve_merge(v, 63)
                                dve_recip_mul(v, 62)
                                dve_recip_mul(v, 63)

                assert ssc.n == b_ssc + NVIS and sex.n == b_sex + NVIS
                assert sav.n == b_sav + NVIS
                assert sad.n == b_sad + acnt[0][-1] and spl.n == b_spl + acnt[1][-1]
                assert sfin.n == b_sfin + 2 * 64 and sev.n == b_sev + 64
                assert sdv.n == b_sdv + NSLOT and smg.n == b_smg + 64
                assert sae.n == b_sae + 64 and srm.n == b_srm + 64
                with nc.Block() as blk:
                    @blk.gpsimd
                    def _(g_):
                        shs.inc(g_.memset(hv[:, :], float(varid)))

            engines5 = [nc.tensor, nc.vector, nc.scalar, nc.sync, nc.gpsimd]
            sems_p2 = (ssc, sex, smk, sav, sad, spl, smg, sfin, sev, sae,
                       srm, sdv, sq2, skv, scp[0], scp[1], shs)
            pre = {s_: s_.n for s_ in sems_p2}
            for arm in nc.switch_core_id(engines=engines5, n=128):
                nc.br("p2_var0" if arm.logical % 2 == 0 else "p2_var1")
            nc.br("p2_join")
            ve = [e_.engine for e_ in engines5]
            with nc.body("p2_var0", valid_engines=ve):
                emit_p2((8, 32), 0)
                nc.br("p2_join")
            post0 = {s_: s_.n for s_ in sems_p2}
            for s_, v_ in pre.items():
                s_.n = v_
            with nc.body("p2_var1", valid_engines=ve):
                emit_p2((16, 24), 1)
                nc.br("p2_join")
            post1 = {s_: s_.n for s_ in sems_p2}
            for s_ in (smg, sfin, sev, sae, srm, sdv, sq2, skv, scp[0],
                       scp[1], shs):
                assert post0[s_] == post1[s_], s_.h
            for s_, v_ in post0.items():
                s_.n = v_
            nc.switch_bb("p2_join")
            with nc.Block() as blk:
                @blk.sync
                def _(sy):
                    sy.wait_ge(shs.h, shs.n)
                    sst.inc(sy.dma_start(out=hsel, in_=hv[:, :]), 16)




        ep.close()

        lp = contextlib.ExitStack()
        big_a = lp.enter_context(nc.sbuf_tensor("big_a", [128, NDC * TOK], BF16))
        rstd_bc = lp.enter_context(nc.sbuf_tensor("rstd_bc", [128, TOK], BF16))
        mr_bc = lp.enter_context(nc.sbuf_tensor("mr_bc", [128, TOK], BF16))
        ssum = lp.enter_context(nc.sbuf_tensor("ssum", [1, TOK], F32))
        ssq = lp.enter_context(nc.sbuf_tensor("ssq", [1, TOK], F32))

        with nc.Block() as blk:
            @blk.sync
            def _(sy):
                sy.wait_ge(scp[0].h, scp[0].n)
                sy.wait_ge(scp[1].h, scp[1].n)
                for hc in range(NH):
                    ins = sy.dma_start(out=big_a[:, hc * TOK:(hc + 1) * TOK],
                                       in_=attn_scr[hc])
                    sat.inc(ins, 16)

            @blk.tensor
            def _(t):
                t.wait_ge(sat.h, sat.n)

        if "attn" in taps:
            with nc.Block() as blk:
                @blk.gpsimd
                def _(g):
                    g.wait_ge(sat.h, sat.n)
                    sst.inc(g.dma_start(out=tap_out["attn"], in_=big_a[:, :]), 16)

        if max_phase < 3:
            _finalize_dbg()
            lp.close()
            return nc, tap_out

        def emit_proj_ln(wT_ap, b_col, lora, scr_out, bt_st=None, g_sb=None):
            with contextlib.ExitStack() as ls:
                stage_w = ls.enter_context(
                    nc.sbuf_tensor(_nm("stage_w3"), [128, 2 * NDC * 128], BF16))
                x_st = (None if lora else
                        ls.enter_context(nc.sbuf_tensor(_nm("x_st"), [128, 2 * TOK], F32)))
                evac = ls.enter_context(nc.sbuf_tensor(_nm("evac3"), [128, 2 * 512], F32))
                sqb = ls.enter_context(nc.sbuf_tensor(_nm("sqb"), [128, 2 * 512], BF16))
                ystg = ls.enter_context(nc.sbuf_tensor(_nm("ystg"), [128, TOK], BF16))
                ps = ls.enter_context(nc.psum_tensor(_nm("p3_ps"), [128, 2 * 512], F32))
                ps_st = ls.enter_context(nc.psum_tensor(_nm("p3_st"), [1, 1024], F32))

                b_sw = [swp[0].n, swp[1].n]
                b_sx2 = [sxp[0].n, sxp[1].n]
                b_sps, b_spst = sps.n, spst.n
                b_sev, b_sdv, b_sscw, b_sfin = sev.n, sdv.n, sscw.n, sfin.n
                NG = NDC * NT

                def emit_stat_mm(t, gi):
                    if DBG_NOSTAT:
                        return
                    t.wait_ge(sev.h, b_sev + 2 * gi + 2)
                    if gi >= 1:
                        t.wait_ge(sfin.h, b_sfin + gi)
                    nt = gi % NT
                    t.matmul(ps_st[0:1, 0:512], ones_c1[:, :],
                             ystg[:, nt * 512:nt * 512 + 512],
                             start=True, stop=True, skip_group_check=True)
                    ins = t.matmul(ps_st[0:1, 512:1024], ones_c1[:, :],
                                   sqb[:, (gi % 2) * 512:(gi % 2) * 512 + 512],
                                   start=True, stop=True, skip_group_check=True)
                    spst.inc(ins)

                with nc.Block() as blk:
                    @blk.gpsimd
                    def _(g):
                        for o in range(NDC):
                            if o >= 2:
                                g.wait_ge(sps.h, b_sps + NT * (o - 1))
                            ins = g.dma_start(
                                out=stage_w[:, :]
                                    .rearrange("p (b c j) -> p b c j", b=2, c=NDC)
                                    [:, o % 2, :, :],
                                in_=wT_ap[:, o * 128:(o + 1) * 128]
                                    .rearrange("(c p) j -> p c j", p=128))
                            swp[o % 2].inc(ins, 16)


                    @blk.tensor
                    def _(t):
                        for o in range(NDC):
                            t.wait_ge(swp[o % 2].h, b_sw[o % 2] + 16 * (o // 2 + 1))
                            for nt in range(NT):
                                gi = o * NT + nt
                                if gi >= 2:
                                    t.wait_ge(sev.h, b_sev + 2 * (gi - 1))
                                ins = None
                                for c in range(NDC):
                                    ins = t.matmul(
                                        ps[:, (gi % 2) * 512:(gi % 2) * 512 + 512],
                                        stage_w[:, (o % 2) * NDC * 128 + c * 128:
                                                (o % 2) * NDC * 128 + c * 128 + 128],
                                        big_a[:, c * TOK + nt * 512:
                                              c * TOK + nt * 512 + 512],
                                        start=(c == 0),
                                        stop=(c == NDC - 1 and not lora))
                                if lora:
                                    ins = t.matmul(
                                        ps[:, (gi % 2) * 512:(gi % 2) * 512 + 512],
                                        bt_st[:, o * 128:o * 128 + 128],
                                        g_sb[:, nt * 512:nt * 512 + 512],
                                        start=False, stop=True,
                                        skip_group_check=True)
                                sps.inc(ins)
                                if gi >= 2:
                                    emit_stat_mm(t, gi - 2)
                        emit_stat_mm(t, NG - 2)
                        emit_stat_mm(t, NG - 1)

                    @blk.scalar
                    def _(s):
                        for o in range(NDC):
                            for nt in range(NT):
                                gi = o * NT + nt
                                s.wait_ge(sps.h, b_sps + gi + 1)
                                if gi >= 2:
                                    s.wait_ge(sdv.h, b_sdv + gi - 1)
                                sev.inc(s.activation(
                                    evac[:, (gi % 2) * 512:(gi % 2) * 512 + 512],
                                    ps[:, (gi % 2) * 512:(gi % 2) * 512 + 512],
                                    AF.Identity, bias=b_col[:, o:o + 1], scale=1.0))
                                s.wait_ge(sdv.h, b_sdv + gi + 1)
                                if gi >= 2 and not DBG_NOSTAT:
                                    s.wait_ge(spst.h, b_spst + gi - 1)
                                sev.inc(s.activation(
                                    sqb[:, (gi % 2) * 512:(gi % 2) * 512 + 512],
                                    ystg[:, nt * 512:nt * 512 + 512],
                                    AF.Square, bias=0.0, scale=1.0))

                    @blk.vector
                    def _(v):
                        def rowadd(gi2):
                            if DBG_NOSTAT:
                                return
                            nt2 = gi2 % NT
                            v.wait_ge(spst.h, b_spst + gi2 + 1)
                            if gi2 < NT:
                                v.tensor_copy(ssum[0:1, nt2 * 512:nt2 * 512 + 512],
                                              ps_st[0:1, 0:512])
                                ins = v.tensor_copy(
                                    ssq[0:1, nt2 * 512:nt2 * 512 + 512],
                                    ps_st[0:1, 512:1024])
                            else:
                                v.tensor_add(ssum[0:1, nt2 * 512:nt2 * 512 + 512],
                                             ssum[0:1, nt2 * 512:nt2 * 512 + 512],
                                             ps_st[0:1, 0:512])
                                ins = v.tensor_add(
                                    ssq[0:1, nt2 * 512:nt2 * 512 + 512],
                                    ssq[0:1, nt2 * 512:nt2 * 512 + 512],
                                    ps_st[0:1, 512:1024])
                            sfin.inc(ins)

                        for o in range(NDC):
                            for nt in range(NT):
                                gi = o * NT + nt
                                v.wait_ge(sev.h, b_sev + 2 * gi + 1)
                                if nt == 0 and o >= 1:
                                    v.wait_ge(sscw.h, b_sscw + 16 * o)
                                src1 = (x_st[:, (o % 2) * TOK + nt * 512:
                                             (o % 2) * TOK + nt * 512 + 512]
                                        if not lora else
                                        big_a[:, o * TOK + nt * 512:
                                              o * TOK + nt * 512 + 512])
                                if not lora:
                                    v.wait_ge(sxp[o % 2].h,
                                              b_sx2[o % 2] + 16 * (o // 2 + 1))
                                sdv.inc(v.tensor_add(
                                    ystg[:, nt * 512:nt * 512 + 512],
                                    evac[:, (gi % 2) * 512:(gi % 2) * 512 + 512],
                                    src1))
                                if gi >= 2:
                                    rowadd(gi - 2)
                        rowadd(NG - 2)
                        rowadd(NG - 1)

                    @blk.sync
                    def _(sy):
                        for o in range(NDC):
                            if not lora:
                                if o >= 2:
                                    sy.wait_ge(sdv.h, b_sdv + NT * (o - 1))
                                ins = sy.dma_start(
                                    out=x_st[:, (o % 2) * TOK:(o % 2) * TOK + TOK],
                                    in_=xT_own[o * 128:(o + 1) * 128, :])
                                sxp[o % 2].inc(ins, 16)
                            if o >= 1:
                                sy.wait_ge(sdv.h, b_sdv + NT * o)
                                ins = sy.dma_start(out=scr_out[o - 1], in_=ystg[:, :])
                                sscw.inc(ins, 16)
                        sy.wait_ge(sdv.h, b_sdv + NT * NDC)
                        ins = sy.dma_start(out=scr_out[NDC - 1], in_=ystg[:, :])
                        sscw.inc(ins, 16)

            if DBG_NOROWS:
                return
            with contextlib.ExitStack() as ls2:
                ps_ln = ls2.enter_context(nc.psum_tensor(_nm("ps_ln"), [128, TOK], F32))
                b2_sdv, b2_sev, b2_sps, b2_sfin = sdv.n, sev.n, sps.n, sfin.n
                with nc.Block() as blk:
                    @blk.vector
                    def _(v):
                        v.wait_ge(sfin.h, b2_sfin)
                        v.tensor_scalar(ssum[0:1, :], ssum[0:1, :], 1.0 / D,
                                        None, ALU.mult)
                        v.tensor_scalar(ssq[0:1, :], ssq[0:1, :], 1.0 / D,
                                        None, ALU.mult)
                        v.tensor_mul(ps_ln[0:1, 0:TOK], ssum[0:1, :], ssum[0:1, :])
                        v.tensor_sub(ssq[0:1, :], ssq[0:1, :], ps_ln[0:1, 0:TOK])
                        sdv.inc(vf(v, ssq))

                    @blk.scalar
                    def _(s):
                        s.wait_ge(sdv.h, b2_sdv + 1)
                        s.activation(ps_ln[0:1, 0:TOK], ssq[0:1, :],
                                     AF.Sqrt, bias=epsb[0:1, 0:1], scale=1.0)
                        sev.inc(sf(s, ps_ln))

                    @blk.vector
                    def _(v):
                        v.wait_ge(sev.h, b2_sev + 1)
                        v.reciprocal(ssq[0:1, :], ps_ln[0:1, 0:TOK])
                        v.tensor_mul(ssum[0:1, :], ssum[0:1, :], ssq[0:1, :])
                        sdv.inc(vf(v, ssum))

                    @blk.tensor
                    def _(t):
                        t.wait_ge(sdv.h, b2_sdv + 2)
                        ins = None
                        for i in range(NT):
                            ins = t.matmul(ps_ln[:, i * 512:(i + 1) * 512],
                                           ones_r1f[:, :],
                                           ssq[0:1, i * 512:(i + 1) * 512],
                                           start=True, stop=True,
                                           skip_group_check=True)
                        sps.inc(ins)
                        t.wait_ge(sev.h, b2_sev + 2)
                        ins = None
                        for i in range(NT):
                            ins = t.matmul(ps_ln[:, i * 512:(i + 1) * 512],
                                           ones_r1f[:, :],
                                           ssum[0:1, i * 512:(i + 1) * 512],
                                           start=True, stop=True,
                                           skip_group_check=True)
                        sps.inc(ins)

                    @blk.scalar
                    def _(s):
                        s.wait_ge(sps.h, b2_sps + 1)
                        s.copy(rstd_bc[:, :], ps_ln[:, :])
                        sev.inc(sf(s, rstd_bc))
                        s.wait_ge(sps.h, b2_sps + 2)
                        s.copy(mr_bc[:, :], ps_ln[:, :])
                        sev.inc(sf(s, mr_bc))

        emit_proj_ln(woT, bo_col, False, [y1_scr[o] for o in range(NDC)])

        if "y1" in taps or "lnr1" in taps:
            with nc.Block() as blk:
                @blk.gpsimd
                def _(g):
                    g.wait_ge(sscw.h, sscw.n)
                    g.wait_ge(sev.h, sev.n)
                    if "y1" in taps:
                        sst.inc(g.dma_start(out=tap_out["y1"], in_=y1_scr), 16)
                    if "lnr1" in taps:
                        sst.inc(g.dma_start(out=tap_out["lnr1"][0:1, 0:TOK],
                                            in_=ssq[0:1, :]), 16)
                        sst.inc(g.dma_start(out=tap_out["lnr1"][0:1, TOK:2 * TOK],
                                            in_=ssum[0:1, :]), 16)

        if max_phase < 4:
            _finalize_dbg()
            lp.close()
            return nc, tap_out
        with contextlib.ExitStack() as p4p:
            y1r = p4p.enter_context(nc.sbuf_tensor("y1r", [128, 2 * TOK], BF16))
            b_sx2 = [sxp[0].n, sxp[1].n]
            b_sdv = sdv.n
            with nc.Block() as blk:
                @blk.sync
                def _(sy):
                    sy.wait_ge(sev.h, sev.n)
                    sy.wait_ge(sscw.h, sscw.n)
                    for c in range(NDC):
                        if c >= 2:
                            sy.wait_ge(sdv.h, b_sdv + 2 * (c - 1))
                        ins = sy.dma_start(
                            out=y1r[:, (c % 2) * TOK:(c % 2) * TOK + TOK],
                            in_=y1_scr[c])
                        sxp[c % 2].inc(ins, 16)

                @blk.vector
                def _(v):
                    for c in range(NDC):
                        v.wait_ge(sxp[c % 2].h, b_sx2[c % 2] + 16 * (c // 2 + 1))
                        dst = big_a[:, c * TOK:(c + 1) * TOK]
                        v.tensor_mul(dst, y1r[:, (c % 2) * TOK:(c % 2) * TOK + TOK],
                                     rstd_bc[:, :])
                        sdv.inc(v.tensor_sub(dst, dst, mr_bc[:, :]), 2)

        if "y1n" in taps:
            with nc.Block() as blk:
                @blk.gpsimd
                def _(g):
                    g.wait_ge(sdv.h, sdv.n)
                    sst.inc(g.dma_start(out=tap_out["y1n"], in_=big_a[:, :]), 16)

        if max_phase < 5:
            _finalize_dbg()
            lp.close()
            return nc, tap_out
        p4s = contextlib.ExitStack()
        bt_st = p4s.enter_context(nc.sbuf_tensor("bt_st", [128, D], BF16))
        g_sb = p4s.enter_context(nc.sbuf_tensor("g_sb", [128, TOK], BF16))
        with contextlib.ExitStack() as p4a:
            wr_st = p4a.enter_context(nc.sbuf_tensor("wr_st", [128, NDC * 8], BF16))
            at_st = p4a.enter_context(nc.sbuf_tensor("at_st", [128, NDC * 128], BF16))
            lg = p4a.enter_context(nc.sbuf_tensor("lg", [8, TOK], F32))
            lgT = p4a.enter_context(nc.sbuf_tensor("lgT", [128, 16 * 8], F32))
            rsc = p4a.enter_context(nc.sbuf_tensor("rsc", [128, 4 * 16], F32))
            rse = p4a.enter_context(nc.sbuf_tensor("rse", [128, 16 * 8], F32))
            rcl = p4a.enter_context(nc.sbuf_tensor("rcl", [128, 16 * 4], F32))
            rc2 = p4a.enter_context(nc.sbuf_tensor("rc2", [128, 8], F32))
            wfb = p4a.enter_context(nc.sbuf_tensor("wfb", [128, 16 * 8], BF16))
            route_row = p4a.enter_context(nc.sbuf_tensor("route_row", [8, TOK], BF16))
            routebc = p4a.enter_context(nc.sbuf_tensor("routebc", [128, TOK], BF16))
            hevac = p4a.enter_context(nc.sbuf_tensor("hevac", [128, 2 * 512], F32))
            ps_rb = p4a.enter_context(nc.psum_tensor("ps_rb", [128, 2 * 512], F32))
            ps_t = p4a.enter_context(nc.psum_tensor("ps_t", [128, 8], F32))
            ps_rt = p4a.enter_context(nc.psum_tensor("ps_rt", [8, TOK], BF16))

            b_sw, b_sps, b_sev, b_sdv, b_sfin = sw.n, sps.n, sev.n, sdv.n, sfin.n
            y1n_done = sdv.n

            with nc.Block() as blk:
                @blk.gpsimd
                def _(g):
                    sw.inc(g.dma_start(
                        out=wr_st[:, :].rearrange("p (c e) -> p c e", c=NDC),
                        in_=wrT.rearrange("(c p) e -> p c e", p=128)), 16)
                    sw.inc(g.dma_start(
                        out=at_st[:, :].rearrange("p (c r) -> p c r", c=NDC),
                        in_=At_d.rearrange("(c p) r -> p c r", p=128)), 16)
                    sw.inc(g.dma_start(out=bt_st[:, :], in_=Bt_d), 16)

                @blk.tensor
                def _(t):
                    t.wait_ge(sw.h, b_sw + 16)
                    t.wait_ge(sdv.h, y1n_done)
                    for nt in range(NT):
                        if nt >= 2:
                            t.wait_ge(sev.h, b_sev + nt - 1)
                        ins = None
                        for c in range(NDC):
                            ins = t.matmul(
                                ps_rb[0:8, (nt % 2) * 512:(nt % 2) * 512 + 512],
                                wr_st[:, c * 8:c * 8 + 8],
                                big_a[:, c * TOK + nt * 512:c * TOK + nt * 512 + 512],
                                start=(c == 0), stop=(c == NDC - 1))
                        sps.inc(ins)

                @blk.scalar
                def _(s):
                    for nt in range(NT):
                        s.wait_ge(sps.h, b_sps + nt + 1)
                        s.copy(lg[0:8, nt * 512:nt * 512 + 512],
                               ps_rb[0:8, (nt % 2) * 512:(nt % 2) * 512 + 512])
                        sev.inc(sf(s, lg))

                @blk.tensor
                def _(t):
                    for tt in range(16):
                        t.wait_ge(sev.h, b_sev + tt // 4 + 1)
                        if tt >= 1:
                            t.wait_ge(sev.h, b_sev + NT + tt)
                        ins = t.transpose(ps_t[:, :],
                                          lg[0:8, tt * 128:tt * 128 + 128],
                                          ident_f8[:, :])
                        sfin.inc(ins)

                @blk.scalar
                def _(s):
                    for tt in range(16):
                        s.wait_ge(sfin.h, b_sfin + tt + 1)
                        s.copy(lgT[:, tt * 8:tt * 8 + 8], ps_t[:, :])
                        sev.inc(sf(s, lgT))

                def _m1(tt):
                    return rcl[:, tt * 4:tt * 4 + 1]

                def _nm1(tt):
                    return rcl[:, tt * 4 + 1:tt * 4 + 2]

                def _m2(tt):
                    return rcl[:, tt * 4 + 2:tt * 4 + 3]

                @blk.vector
                def _(v):
                    for base in range(0, 16, 2):
                        pair = (base, base + 1)
                        for tt in pair:
                            v.wait_ge(sev.h, b_sev + NT + tt + 1)
                        for tt in pair:
                            v.reduce_max(_m1(tt), lgT[:, tt * 8:tt * 8 + 8], AX.X)
                        for tt in pair:
                            pa = (tt % 2) * 32
                            v.tensor_scalar(rsc[:, pa:pa + 8],
                                            lgT[:, tt * 8:tt * 8 + 8],
                                            _m1(tt), None, ALU.is_equal)
                        for tt in pair:
                            v.tensor_scalar(_nm1(tt), _m1(tt), -1.0, None, ALU.mult)
                        for tt in pair:
                            pa = (tt % 2) * 32
                            v.scalar_tensor_tensor(rsc[:, pa + 8:pa + 16],
                                                   rsc[:, pa:pa + 8], -1e30,
                                                   lgT[:, tt * 8:tt * 8 + 8],
                                                   ALU.mult, ALU.add)
                        for tt in pair:
                            pa = (tt % 2) * 32
                            v.reduce_max(_m2(tt), rsc[:, pa + 8:pa + 16], AX.X)
                        for tt in pair:
                            sdv.inc(vf(v, rcl))

                @blk.scalar
                def _(s):
                    for tt in range(16):
                        s.wait_ge(sdv.h, b_sdv + tt + 1)
                        s.activation(rse[:, tt * 8:tt * 8 + 8],
                                     lgT[:, tt * 8:tt * 8 + 8], AF.Exp,
                                     bias=_nm1(tt), scale=1.0)
                        sev.inc(sf(s, rse))

                @blk.vector
                def _(v):
                    for base in range(0, 16, 2):
                        pair = (base, base + 1)
                        for tt in pair:
                            v.wait_ge(sev.h, b_sev + NT + 16 + tt + 1)
                        for tt in pair:
                            pa = (tt % 2) * 32
                            v.tensor_scalar(rsc[:, pa + 16:pa + 24],
                                            lgT[:, tt * 8:tt * 8 + 8],
                                            _m2(tt), None, ALU.is_ge)
                        for tt in pair:
                            pa = (tt % 2) * 32
                            v.tensor_mul(rsc[:, pa + 16:pa + 24],
                                         rsc[:, pa + 16:pa + 24],
                                         rse[:, tt * 8:tt * 8 + 8])
                        for tt in pair:
                            pa = (tt % 2) * 32
                            v.reduce_sum(rc2[:, (tt % 2) * 4:(tt % 2) * 4 + 1],
                                         rsc[:, pa + 16:pa + 24], AX.X)
                        for tt in pair:
                            v.reciprocal(rc2[:, (tt % 2) * 4 + 1:(tt % 2) * 4 + 2],
                                         rc2[:, (tt % 2) * 4:(tt % 2) * 4 + 1])
                        for tt in pair:
                            v.tensor_scalar(rc2[:, (tt % 2) * 4 + 2:(tt % 2) * 4 + 3],
                                            rc2[:, (tt % 2) * 4 + 1:(tt % 2) * 4 + 2],
                                            2.0, None, ALU.mult)
                        for tt in pair:
                            pa = (tt % 2) * 32
                            v.tensor_scalar(wfb[:, tt * 8:tt * 8 + 8],
                                            rsc[:, pa + 16:pa + 24],
                                            rc2[:, (tt % 2) * 4 + 2:(tt % 2) * 4 + 3],
                                            None, ALU.mult)
                        for tt in pair:
                            sdv.inc(vf(v, wfb))

                @blk.tensor
                def _(t):
                    for tt in range(16):
                        t.wait_ge(sdv.h, b_sdv + 16 + tt + 1)
                        ins = t.transpose(ps_rt[0:8, tt * 128:tt * 128 + 128],
                                          wfb[:, tt * 8:tt * 8 + 8],
                                          ident_b[:, :])
                        sfin.inc(ins)

                @blk.scalar
                def _(s):
                    s.wait_ge(sfin.h, b_sfin + 32)
                    s.copy(route_row[0:8, :], ps_rt[0:8, :])
                    sev.inc(sf(s, route_row))

                @blk.tensor
                def _(t):
                    t.wait_ge(sev.h, b_sev + NT + 32 + 1)
                    for nt in range(NT):
                        if nt >= 2:
                            t.wait_ge(sev.h, b_sev + NT + 32 + 1 + nt - 1)
                        ins = t.matmul(ps_rb[:, (nt % 2) * 512:(nt % 2) * 512 + 512],
                                       ebc[0:8, :],
                                       route_row[0:8, nt * 512:nt * 512 + 512],
                                       start=True, stop=True,
                                       skip_group_check=True)
                        sps.inc(ins)

                @blk.scalar
                def _(s):
                    for nt in range(NT):
                        s.wait_ge(sps.h, b_sps + NT + nt + 1)
                        s.copy(routebc[:, nt * 512:nt * 512 + 512],
                               ps_rb[:, (nt % 2) * 512:(nt % 2) * 512 + 512])
                        sev.inc(sf(s, routebc))

            b_sw2, b_sps2, b_sev2, b_sdv2 = sw.n, sps.n, sev.n, sdv.n
            with nc.Block() as blk:
                @blk.tensor
                def _(t):
                    t.wait_ge(sw.h, b_sw2)
                    t.wait_ge(sev.h, b_sev2)
                    for nt in range(NT):
                        if nt >= 2:
                            t.wait_ge(sev.h, b_sev2 + nt - 1)
                        ins = None
                        for c in range(NDC):
                            ins = t.matmul(
                                ps_rb[:, (nt % 2) * 512:(nt % 2) * 512 + 512],
                                at_st[:, c * 128:c * 128 + 128],
                                big_a[:, c * TOK + nt * 512:c * TOK + nt * 512 + 512],
                                start=(c == 0), stop=(c == NDC - 1))
                        sps.inc(ins)

                @blk.scalar
                def _(s):
                    for nt in range(NT):
                        s.wait_ge(sps.h, b_sps2 + nt + 1)
                        if nt >= 2:
                            s.wait_ge(sdv.h, b_sdv2 + nt - 1)
                        s.copy(hevac[:, (nt % 2) * 512:(nt % 2) * 512 + 512],
                               ps_rb[:, (nt % 2) * 512:(nt % 2) * 512 + 512])
                        sev.inc(sf(s, hevac))

                @blk.vector
                def _(v):
                    for nt in range(NT):
                        v.wait_ge(sev.h, b_sev2 + nt + 1)
                        sdv.inc(v.tensor_mul(
                            g_sb[:, nt * 512:nt * 512 + 512],
                            hevac[:, (nt % 2) * 512:(nt % 2) * 512 + 512],
                            routebc[:, nt * 512:nt * 512 + 512]))

        if "route" in taps or "gsb" in taps:
            with nc.Block() as blk:
                @blk.gpsimd
                def _(g):
                    g.wait_ge(sdv.h, sdv.n)
                    if "gsb" in taps:
                        sst.inc(g.dma_start(out=tap_out["gsb"], in_=g_sb[:, :]), 16)

        if max_phase < 6:
            _finalize_dbg()
            p4s.close()
            lp.close()
            return nc, tap_out
        emit_proj_ln(wbT, bb_col, True, [y2_scr[o] for o in range(NDC)],
                     bt_st=bt_st, g_sb=g_sb)
        p4s.close()

        if "y2" in taps:
            with nc.Block() as blk:
                @blk.gpsimd
                def _(g):
                    g.wait_ge(sscw.h, sscw.n)
                    sst.inc(g.dma_start(out=tap_out["y2"], in_=y2_scr), 16)

        if max_phase < 7:
            _finalize_dbg()
            lp.close()
            return nc, tap_out
        with contextlib.ExitStack() as p5:
            y2r = p5.enter_context(nc.sbuf_tensor("y2r", [128, 2 * TOK], BF16))
            outstg = p5.enter_context(nc.sbuf_tensor("outstg", [128, 2 * TOK], F32))
            b_sx2 = [sxp[0].n, sxp[1].n]
            b_st2 = [sstp[0].n, sstp[1].n]
            b_sdv = sdv.n
            with nc.Block() as blk:
                @blk.sync
                def _(sy):
                    sy.wait_ge(sev.h, sev.n)
                    sy.wait_ge(sscw.h, sscw.n)
                    for c in range(NDC):
                        if c >= 2:
                            sy.wait_ge(sdv.h, b_sdv + 2 * (c - 1))
                        ins = sy.dma_start(
                            out=y2r[:, (c % 2) * TOK:(c % 2) * TOK + TOK],
                            in_=y2_scr[c])
                        sxp[c % 2].inc(ins, 16)
                        if c >= 1:
                            sy.wait_ge(sdv.h, b_sdv + 2 * c)
                            ins = sy.dma_start(
                                out=outT[(c - 1) * 128:c * 128, :],
                                in_=outstg[:, ((c - 1) % 2) * TOK:
                                           ((c - 1) % 2) * TOK + TOK])
                            sstp[(c - 1) % 2].inc(ins, 16)
                    sy.wait_ge(sdv.h, b_sdv + 2 * NDC)
                    ins = sy.dma_start(
                        out=outT[(NDC - 1) * 128:NDC * 128, :],
                        in_=outstg[:, ((NDC - 1) % 2) * TOK:
                                   ((NDC - 1) % 2) * TOK + TOK])
                    sstp[(NDC - 1) % 2].inc(ins, 16)
                    sy.wait_ge(sstp[0].h, sstp[0].n)
                    sy.wait_ge(sstp[1].h, sstp[1].n)
                    sy.wait_ge(sst.h, sst.n)

                @blk.vector
                def _(v):
                    for c in range(NDC):
                        v.wait_ge(sxp[c % 2].h, b_sx2[c % 2] + 16 * (c // 2 + 1))
                        if c >= 2:
                            v.wait_ge(sstp[c % 2].h, b_st2[c % 2] + 16 * (c // 2))
                        dst = outstg[:, (c % 2) * TOK:(c % 2) * TOK + TOK]
                        v.tensor_mul(dst, y2r[:, (c % 2) * TOK:(c % 2) * TOK + TOK],
                                     rstd_bc[:, :])
                        sdv.inc(v.tensor_sub(dst, dst, mr_bc[:, :]), 2)

        lp.close()
    return nc, tap_out


def _rope_perm():
    perm = []
    for h in range(NH):
        base = h * HD
        perm += [base + 2 * i for i in range(64)] + [base + 2 * i + 1 for i in range(64)]
    for kh in range(NKV):
        base = NH * HD + kh * HD
        perm += [base + 2 * i for i in range(64)] + [base + 2 * i + 1 for i in range(64)]
    perm += list(range(5120, 6144))
    return np.asarray(perm)


def _core_blocks(c):
    return (0, 3) if c % 2 == 0 else (1, 2)


def make_in_maps(L0_attn_x, L0_attn_w_qkv, L0_attn_b_qkv, L0_attn_w_o, L0_attn_b_o,
                 L0_ffn_W_base, L0_ffn_b_base, L0_ffn_W_router, L0_ffn_A, L0_ffn_B):
    asc = np.ascontiguousarray
    x = np.asarray(L0_attn_x)
    perm = _rope_perm()
    wqkvT = asc(np.asarray(L0_attn_w_qkv)[perm].T)
    bqkv = np.asarray(L0_attn_b_qkv)[perm]
    bq_col = asc(bqkv[:5120].reshape(40, 128).T)
    bv_row = asc(bqkv[5120:].reshape(1, 1024))
    woT = asc(np.asarray(L0_attn_w_o).T)
    bo_col = asc(np.asarray(L0_attn_b_o).reshape(32, 128).T)
    wbT = asc(np.asarray(L0_ffn_W_base).T)
    bb_col = asc(np.asarray(L0_ffn_b_base).reshape(32, 128).T)
    wrT = asc(np.asarray(L0_ffn_W_router).T)
    At = asc(np.asarray(L0_ffn_A).transpose(2, 0, 1).reshape(D, 128))
    Bt = asc(np.asarray(L0_ffn_B).transpose(0, 2, 1).reshape(128, D))

    in_maps = []
    pos_per_core = []
    xT_cache = {}
    for c in range(8):
        b = c // 2
        if b not in xT_cache:
            xT_cache[b] = asc(x[b].T.astype(np.float32))
        blocks = _core_blocks(c)
        pos = np.concatenate([np.arange(1024 * blk, 1024 * blk + 1024)
                              for blk in blocks])
        pos_per_core.append(pos)
        in_maps.append({
            "xT_own": asc(xT_cache[b][:, pos]),
            "xT_full": xT_cache[b],
            "posq": pos.astype(np.float32).reshape(1, TOK),
            "wqkvT": wqkvT, "bq_col_d": bq_col, "bv_row_d": bv_row,
            "woT": woT, "bo_col_d": bo_col,
            "wbT": wbT, "bb_col_d": bb_col,
            "wrT": wrT, "At": At, "Bt": Bt,
        })
    return in_maps, pos_per_core


def kernel(**inputs):
    in_maps, pos_per_core = make_in_maps(**inputs)
    nc, _ = build_program(())
    res = run_bass_kernel_spmd(nc, in_maps, core_ids=list(range(8)))

    out = np.empty((4, S, D), dtype=np.float32)
    for c in range(8):
        b = c // 2
        hv = int(res.results[c]["hsel"][0, 0])
        if hv != c % 2:
            raise RuntimeError(
                f"core {c} ran switch variant {hv}; physical-core mapping "
                f"changed — per-core loop lengths do not match its data")
        out[b, pos_per_core[c], :] = res.results[c]["outT"].T
    return out


if __name__ == "__main__":
    nc, _ = build_program(())
    print("instructions:", nc.n_instructions())


# version 09
# speedup vs baseline: 1.1199x; 1.0096x over previous
import contextlib
import math
import os
DBG_NOSTAT = os.environ.get("DBG_NOSTAT", "0") == "1"
DBG_NOROWS = os.environ.get("DBG_NOROWS", "0") == "1"

import numpy as np
import ml_dtypes
import concourse.bass as bass
import concourse.mybir as mybir
from concourse.bass_utils import run_bass_kernel_spmd

F32 = mybir.dt.float32
BF16 = mybir.dt.bfloat16
I32 = mybir.dt.int32
AF = mybir.ActivationFunctionType
ALU = mybir.AluOpType
AX = mybir.AxisListType

D = 4096
NDC = 32
NH = 32
NKV = 8
HD = 128
S = 4096
TOK = 2048
NT = TOK // 512
C_SHIFT = 12.0
SM_SCALE = 1.0 / math.sqrt(HD)
EPS = 1e-5
LN1E4 = math.log(10000.0)
PI = math.pi
LOOP_LEN = (16, 32)
NSLOT = 2


class Sem:
    def __init__(self, nc, es, name):
        self.h = es.enter_context(nc.semaphore(name))
        self.n = 0

    def inc(self, ins, k=1):
        ins.then_inc(self.h, k)
        self.n += k
        return ins


def build_program(taps=(), max_phase=99):
    nc = bass.Bass("TRN2", target_bir_lowering=False,
                   detect_race_conditions=False)

    xT_own = nc.dram_tensor("xT_own", [D, TOK], F32, kind="ExternalInput").ap()
    xT_full = nc.dram_tensor("xT_full", [D, S], F32, kind="ExternalInput").ap()
    posq = nc.dram_tensor("posq", [1, TOK], F32, kind="ExternalInput").ap()
    wqkvT = nc.dram_tensor("wqkvT", [D, 6144], F32, kind="ExternalInput").ap()
    bq_col_d = nc.dram_tensor("bq_col_d", [128, 40], F32, kind="ExternalInput").ap()
    bv_row_d = nc.dram_tensor("bv_row_d", [1, 1024], F32, kind="ExternalInput").ap()
    woT = nc.dram_tensor("woT", [D, D], F32, kind="ExternalInput").ap()
    bo_col_d = nc.dram_tensor("bo_col_d", [128, 32], F32, kind="ExternalInput").ap()
    wbT = nc.dram_tensor("wbT", [D, D], F32, kind="ExternalInput").ap()
    bb_col_d = nc.dram_tensor("bb_col_d", [128, 32], F32, kind="ExternalInput").ap()
    wrT = nc.dram_tensor("wrT", [D, 8], F32, kind="ExternalInput").ap()
    At_d = nc.dram_tensor("At", [D, 128], F32, kind="ExternalInput").ap()
    Bt_d = nc.dram_tensor("Bt", [128, D], F32, kind="ExternalInput").ap()
    outT = nc.dram_tensor("outT", [D, TOK], F32, kind="ExternalOutput").ap()
    hsel = nc.dram_tensor("hsel", [1, 1], F32, kind="ExternalOutput").ap()

    q_scr = nc.dram_tensor("q_scr", [NH, 128, TOK], BF16, kind="Internal").ap()
    k_scr = nc.dram_tensor("k_scr", [NKV, 128, S], BF16, kind="Internal").ap()
    v_scr = nc.dram_tensor("v_scr", [S // 128, 128, NKV * HD], BF16,
                           kind="Internal").ap()
    attn_scr = nc.dram_tensor("attn_scr", [NH, 128, TOK], BF16, kind="Internal").ap()
    y1_scr = nc.dram_tensor("y1_scr", [NDC, 128, TOK], BF16, kind="Internal").ap()
    y2_scr = nc.dram_tensor("y2_scr", [NDC, 128, TOK], BF16, kind="Internal").ap()

    tap_out = {}
    for name, shape in (
        ("qkv_q", (NH, 128, TOK)), ("qkv_k", (NKV, 128, S)),
        ("qkv_v", (S // 128, 128, NKV * HD)),
        ("attn", (128, NDC * TOK)), ("y1", (NDC, 128, TOK)),
        ("y1n", (128, NDC * TOK)), ("y2", (NDC, 128, TOK)),
        ("lnr1", (1, 2 * TOK)), ("route", (8, TOK)),
        ("gsb", (128, TOK)), ("trig", (128, 3 * TOK)), ("p0c", (128, 8)),
        ("trigq", (128, 3 * TOK)),
    ):
        if name in taps:
            tap_out[name] = nc.dram_tensor("tap_" + name, list(shape), F32,
                                           kind="ExternalOutput").ap()

    _uid = [0]

    def _nm(base):
        _uid[0] += 1
        return f"{base}_{_uid[0]}"

    with contextlib.ExitStack() as es:
        iota_p = es.enter_context(nc.sbuf_tensor("iota_p", [128, 1], F32))
        ident_b = es.enter_context(nc.sbuf_tensor("ident_b", [128, 128], BF16))
        ident_f8 = es.enter_context(nc.sbuf_tensor("ident_f8", [8, 8], F32))
        ones_r1 = es.enter_context(nc.sbuf_tensor("ones_r1", [1, 128], BF16))
        ones_r1f = es.enter_context(nc.sbuf_tensor("ones_r1f", [1, 128], F32))
        ones_c1 = es.enter_context(nc.sbuf_tensor("ones_c1", [128, 1], BF16))
        ones_c1f = es.enter_context(nc.sbuf_tensor("ones_c1f", [128, 1], F32))
        negc = es.enter_context(nc.sbuf_tensor("negc", [128, 1], F32))
        invf = es.enter_context(nc.sbuf_tensor("invf", [128, 1], F32))
        epsb = es.enter_context(nc.sbuf_tensor("epsb", [1, 1], F32))
        bq_col = es.enter_context(nc.sbuf_tensor("bq_col", [128, 40], F32))
        bo_col = es.enter_context(nc.sbuf_tensor("bo_col", [128, 32], F32))
        bb_col = es.enter_context(nc.sbuf_tensor("bb_col", [128, 32], F32))
        bv_row = es.enter_context(nc.sbuf_tensor("bv_row", [1, 1024], BF16))
        ebc = es.enter_context(nc.sbuf_tensor("ebc", [8, 128], BF16))
        fslot = es.enter_context(nc.sbuf_tensor("fslot", [1, 8], F32))
        fcol = es.enter_context(nc.sbuf_tensor("fcol", [128, 1], F32))
        lnb = es.enter_context(nc.sbuf_tensor("lnb", [128, 1], F32))
        scol = es.enter_context(nc.sbuf_tensor("scol", [128, 1], F32))

        sx = Sem(nc, es, "sx")
        sw = Sem(nc, es, "sw")
        swc = Sem(nc, es, "swc")
        sps = Sem(nc, es, "sps")
        spst = Sem(nc, es, "spst")
        sev = Sem(nc, es, "sev")
        sdv = Sem(nc, es, "sdv")
        sscw = Sem(nc, es, "sscw")
        sq2 = Sem(nc, es, "sq2")
        skv = Sem(nc, es, "skv")
        ssc = Sem(nc, es, "ssc")
        sex = Sem(nc, es, "sex")
        smk = Sem(nc, es, "smk")
        sav = Sem(nc, es, "sav")
        sfin = Sem(nc, es, "sfin")
        sst = Sem(nc, es, "sst")
        sini = Sem(nc, es, "sini")
        swp = [Sem(nc, es, "swpa"), Sem(nc, es, "swpb")]
        sxp = [Sem(nc, es, "sxpa"), Sem(nc, es, "sxpb")]
        scp = [Sem(nc, es, "scpa"), Sem(nc, es, "scpb")]
        sstp = [Sem(nc, es, "sstpa"), Sem(nc, es, "sstpb")]
        sad = Sem(nc, es, "sad")
        sae = Sem(nc, es, "sae")
        srm = Sem(nc, es, "srm")
        sat = Sem(nc, es, "sat")
        shs = Sem(nc, es, "shs")
        srr = Sem(nc, es, "srr")
        sbe = Sem(nc, es, "sbe")
        spl = Sem(nc, es, "spl")
        smg = Sem(nc, es, "smg")

        ep = contextlib.ExitStack()
        posq_bc = ep.enter_context(nc.sbuf_tensor("posq_bc", [128, TOK], F32))
        ep1 = contextlib.ExitStack()
        big_a = ep1.enter_context(nc.sbuf_tensor("xbuf", [128, NDC * TOK], BF16))

        def vf(v, ap):
            return v.tensor_copy(fslot[0:1, 0:1], ap[0:1, 0:1])

        def sf(s_, ap):
            return s_.copy(fslot[0:1, 2:3], ap[0:1, 0:1])

        def gf(g_, ap):
            return g_.tensor_copy(fcol[:, 0:1], ap[:, 0:1])

        def _finalize_dbg():
            with nc.Block() as blk:
                @blk.sync
                def _(sy):
                    sy.wait_ge(sscw.h, sscw.n)
                    sy.wait_ge(sst.h, sst.n)
                    sy.wait_ge(sdv.h, sdv.n)

        with contextlib.ExitStack() as p0:
            t0 = p0.enter_context(nc.sbuf_tensor("t0", [128, 128], F32))
            t1 = p0.enter_context(nc.sbuf_tensor("t1", [128, 128], F32))
            iota128 = p0.enter_context(nc.sbuf_tensor("iota128", [128, 128], F32))
            posq_row = p0.enter_context(nc.sbuf_tensor("posq_row", [1, TOK], F32))
            identf_full = p0.enter_context(nc.sbuf_tensor("identf_full",
                                                          [128, 128], F32))
            pq_ps = p0.enter_context(nc.psum_tensor("pq_ps", [128, TOK], F32))

            with nc.Block() as blk:
                @blk.gpsimd
                def _(g):
                    for dst, src in ((posq_row[:, :], posq), (bq_col[:, :], bq_col_d),
                                     (bo_col[:, :], bo_col_d), (bb_col[:, :], bb_col_d),
                                     (bv_row[:, :], bv_row_d)):
                        sx.inc(g.dma_start(out=dst, in_=src), 16)
                    g.iota(iota_p[:, :], [[0, 1]], channel_multiplier=1,
                           allow_small_or_imprecise_dtypes=True)
                    g.iota(iota128[:, :], [[1, 128]], channel_multiplier=0,
                           allow_small_or_imprecise_dtypes=True)
                    g.memset(ones_r1[:, :], 1.0)
                    g.memset(ones_r1f[:, :], 1.0)
                    g.memset(ones_c1[:, :], 1.0)
                    g.memset(ones_c1f[:, :], 1.0)
                    g.memset(epsb[:, :], EPS)
                    g.memset(lnb[:, :], LN1E4)
                    sini.inc(g.memset(negc[:, :], -C_SHIFT))

                @blk.vector
                def _(v):
                    def spacer():
                        v.tensor_copy(scol[:, 0:1], iota_p[:, 0:1])
                    v.wait_ge(sini.h, 1)
                    v.tensor_scalar(identf_full[:, :], iota128[:, :], iota_p[:, 0:1],
                                    None, ALU.is_equal)
                    v.tensor_scalar(t1[0:8, :], iota128[0:8, :], 1.0 / 16.0,
                                    iota_p[0:8, 0:1], ALU.mult, ALU.subtract)
                    v.tensor_copy(ident_b[:, :], identf_full[:, :])
                    v.tensor_scalar(t0[0:8, :], t1[0:8, :], 0.0, None, ALU.is_ge)
                    v.tensor_copy(ident_f8[:, :], identf_full[0:8, 0:8])
                    v.tensor_scalar(t1[0:8, :], t1[0:8, :], 0.999, None, ALU.is_lt)
                    spacer()
                    v.tensor_mul(ebc[0:8, :], t0[0:8, :], t1[0:8, :])
                    sdv.inc(vf(v, ebc))

                b_sdv0 = sdv.n

                @blk.scalar
                def _(s):
                    s.wait_ge(sini.h, 1)
                    s.activation(invf[0:64, :], iota_p[0:64, 0:1], AF.Exp,
                                 bias=0.0, scale=-LN1E4 / 64.0)
                    s.activation(invf[64:128, :], iota_p[64:128, 0:1], AF.Exp,
                                 bias=lnb[64:128, 0:1], scale=-LN1E4 / 64.0)
                    sini.inc(sf(s, invf))

                @blk.tensor
                def _(t):
                    t.wait_ge(sx.h, 80)
                    ins = None
                    for i in range(NT):
                        ins = t.matmul(pq_ps[:, i * 512:(i + 1) * 512], ones_r1f[:, :],
                                       posq_row[:, i * 512:(i + 1) * 512],
                                       start=True, stop=True)
                    sps.inc(ins)

                b_sps0 = sps.n

                @blk.scalar
                def _(s):
                    s.wait_ge(sps.h, b_sps0)
                    s.copy(posq_bc[:, :], pq_ps[:, :])
                    sini.inc(sf(s, posq_bc))

        if "p0c" in taps:
            with nc.Block() as blk:
                @blk.gpsimd
                def _(g):
                    g.wait_ge(sini.h, sini.n)
                    with nc.allow_non_contiguous_dma(reason="debug"):
                        sst.inc(g.dma_start(out=tap_out["p0c"][:, 0:1],
                                            in_=iota_p[:, :],
                                            allow_slow_non_contiguous=True), 16)
                        sst.inc(g.dma_start(out=tap_out["p0c"][:, 4:5],
                                            in_=invf[:, :],
                                            allow_slow_non_contiguous=True), 16)

        cs_scope = contextlib.ExitStack()
        cos_t = cs_scope.enter_context(nc.sbuf_tensor("cos_t", [128, TOK], BF16))
        sin_t = cs_scope.enter_context(nc.sbuf_tensor("sin_t", [128, TOK], BF16))

        def emit_trig(pass_i):
            with contextlib.ExitStack() as ts_:
                ang = ts_.enter_context(nc.sbuf_tensor(_nm("ang"), [128, TOK], F32))
                u = ts_.enter_context(nc.sbuf_tensor(_nm("u"), [128, TOK], F32))
                ki = ts_.enter_context(nc.sbuf_tensor(_nm("ki"), [128, TOK], I32))
                kf = ts_.enter_context(nc.sbuf_tensor(_nm("kf"), [128, TOK], F32))
                b_sev = sev.n
                with nc.Block() as blk:
                    if pass_i < 2:
                        @blk.gpsimd
                        def _(g):
                            g.wait_ge(sini.h, sini.n)
                            sini.inc(g.iota(ang[:, :], [[1, TOK]],
                                            base=pass_i * TOK,
                                            channel_multiplier=0,
                                            allow_small_or_imprecise_dtypes=True))

                    @blk.vector
                    def _(v):
                        v.wait_ge(sini.h, sini.n)
                        src = ang if pass_i < 2 else posq_bc
                        v.tensor_scalar(ang[:, :], src[:, :], invf[:, 0:1],
                                        None, ALU.mult)
                        for which in (0, 1):
                            if which == 0:
                                v.tensor_copy(u[:, :], ang[:, :])
                            else:
                                v.wait_ge(sev.h, b_sev + 1)
                                v.tensor_scalar(u[:, :], ang[:, :], PI / 2.0,
                                                None, ALU.add)
                            v.tensor_scalar(ki[:, :], u[:, :], 1.0 / (2 * PI),
                                            None, ALU.mult)
                            v.tensor_copy(kf[:, :], ki[:, :])
                            v.scalar_tensor_tensor(u[:, :], kf[:, :], -2 * PI,
                                                   u[:, :], ALU.mult, ALU.add)
                            v.tensor_scalar(kf[:, :], u[:, :], PI, None, ALU.is_ge)
                            v.scalar_tensor_tensor(u[:, :], kf[:, :], -2 * PI,
                                                   u[:, :], ALU.mult, ALU.add)
                            v.tensor_scalar(kf[:, :], u[:, :], -PI, None, ALU.is_lt)
                            v.scalar_tensor_tensor(
                                u[:, :], kf[:, :], 2 * PI, u[:, :],
                                ALU.mult, ALU.add)
                            sdv.inc(vf(v, u))

                    b_sdv = sdv.n - 2

                    @blk.scalar
                    def _(s):
                        s.wait_ge(sdv.h, b_sdv + 1)
                        s.activation(sin_t[:, :], u[:, :], AF.Sin,
                                     bias=0.0, scale=1.0)
                        sev.inc(sf(s, sin_t))
                        s.wait_ge(sdv.h, b_sdv + 2)
                        s.activation(cos_t[:, :], u[:, :], AF.Sin,
                                     bias=0.0, scale=1.0)
                        sev.inc(sf(s, cos_t))

        def emit_load_x(src_ap, wait_sev_to):
            with nc.Block() as blk:
                @blk.gpsimd
                def _(g):
                    g.wait_ge(sev.h, wait_sev_to)
                    for i in range(8):
                        ins = g.dma_start(
                            out=big_a[:, :].rearrange("p (c t) -> p c t", c=NDC)
                                [:, 4 * i:4 * (i + 1), :],
                            in_=src_ap[i * 512:(i + 1) * 512, :]
                                .rearrange("(c p) t -> p c t", p=128))
                        sx.inc(ins, 16)

        def emit_qk_loop(jlist, dsts, dst_off):
            b_sx = sx.n
            with contextlib.ExitStack() as ls:
                stage_w = ls.enter_context(
                    nc.sbuf_tensor(_nm("stage_w1"), [128, 2 * NDC * 128], BF16))
                evac = ls.enter_context(nc.sbuf_tensor(_nm("evac1"), [128, 2 * 512], F32))
                ropa = ls.enter_context(nc.sbuf_tensor(_nm("ropa"), [128, 512], F32))
                ropb = ls.enter_context(nc.sbuf_tensor(_nm("ropb"), [128, 512], F32))
                kq_st = ls.enter_context(nc.sbuf_tensor(_nm("kq_st"), [128, 2 * TOK], BF16))
                ps = ls.enter_context(nc.psum_tensor(_nm("p1_ps"), [128, 2 * 512], F32))
                nJ = len(jlist)
                b_sw = [swp[0].n, swp[1].n]
                b_sc = [scp[0].n, scp[1].n]
                b_sps, b_sev, b_sdv = sps.n, sev.n, sdv.n

                with nc.Block() as blk:
                    @blk.gpsimd
                    def _(g):
                        g.wait_ge(sx.h, b_sx)
                        for idx, (jc, _b) in enumerate(jlist):
                            if idx >= 2:
                                g.wait_ge(sps.h, b_sps + NT * (idx - 1))
                            ins = g.dma_start(
                                out=stage_w[:, :]
                                    .rearrange("p (b c j) -> p b c j", b=2, c=NDC)
                                    [:, idx % 2, :, :],
                                in_=wqkvT[:, jc * 128:(jc + 1) * 128]
                                    .rearrange("(c p) j -> p c j", p=128))
                            swp[idx % 2].inc(ins, 16)

                    @blk.tensor
                    def _(t):
                        t.wait_ge(sx.h, b_sx)
                        for idx in range(nJ):
                            t.wait_ge(swp[idx % 2].h,
                                      b_sw[idx % 2] + 16 * (idx // 2 + 1))
                            for nt in range(NT):
                                gi = idx * NT + nt
                                if gi >= 2:
                                    t.wait_ge(sev.h, b_sev + gi - 1)
                                ins = None
                                for c in range(NDC):
                                    ins = t.matmul(
                                        ps[:, (gi % 2) * 512:(gi % 2) * 512 + 512],
                                        stage_w[:, (idx % 2) * NDC * 128 + c * 128:
                                                (idx % 2) * NDC * 128 + c * 128 + 128],
                                        big_a[:, c * TOK + nt * 512:
                                              c * TOK + nt * 512 + 512],
                                        start=(c == 0), stop=(c == NDC - 1))
                                sps.inc(ins)

                    @blk.scalar
                    def _(s):
                        for idx, (jc, bcol) in enumerate(jlist):
                            for nt in range(NT):
                                gi = idx * NT + nt
                                s.wait_ge(sps.h, b_sps + gi + 1)
                                if gi >= 2:
                                    s.wait_ge(sdv.h, b_sdv + gi - 1)
                                sev.inc(s.activation(
                                    evac[:, (gi % 2) * 512:(gi % 2) * 512 + 512],
                                    ps[:, (gi % 2) * 512:(gi % 2) * 512 + 512],
                                    AF.Identity, bias=bq_col[:, bcol:bcol + 1],
                                    scale=1.0))

                    @blk.vector
                    def _(v):
                        for idx in range(nJ):
                            for nt in range(NT):
                                gi = idx * NT + nt
                                v.wait_ge(sev.h, b_sev + gi + 1)
                                if idx >= 2 and nt == 0:
                                    v.wait_ge(scp[idx % 2].h,
                                              b_sc[idx % 2] + 16 * (idx // 2))
                                ev = evac[:, (gi % 2) * 512:(gi % 2) * 512 + 512]
                                co = cos_t[:, nt * 512:nt * 512 + 512]
                                si = sin_t[:, nt * 512:nt * 512 + 512]
                                ks = kq_st[:, (idx % 2) * TOK + nt * 512:
                                           (idx % 2) * TOK + nt * 512 + 512]
                                v.tensor_mul(ropa[:, :], ev, co)
                                v.tensor_mul(ropb[0:64, :], ev[64:128, :],
                                             si[64:128, :])
                                v.tensor_mul(ropb[64:128, :], ev[0:64, :],
                                             si[0:64, :])
                                v.tensor_sub(ks[0:64, :], ropa[0:64, :],
                                             ropb[0:64, :])
                                sdv.inc(v.tensor_add(ks[64:128, :],
                                                     ropb[64:128, :],
                                                     ropa[64:128, :]))

                    @blk.sync
                    def _(sy):
                        for idx in range(nJ):
                            sy.wait_ge(sdv.h, b_sdv + NT * (idx + 1))
                            ins = sy.dma_start(
                                out=dsts[idx][:, dst_off:dst_off + TOK],
                                in_=kq_st[:, (idx % 2) * TOK:(idx % 2) * TOK + TOK])
                            scp[idx % 2].inc(ins, 16)

        def emit_v_loop(pass_i):
            b_sx = sx.n
            with contextlib.ExitStack() as ls:
                stage_wv = ls.enter_context(
                    nc.sbuf_tensor(_nm("stage_wv"), [128, NDC * 512], BF16))
                v_st = ls.enter_context(nc.sbuf_tensor(_nm("v_st"), [128, 8 * 512], BF16))
                ps = ls.enter_context(nc.psum_tensor(_nm("pv_ps"), [128, 2 * 512], F32))
                b_sw, b_sps, b_sev, b_sscw, b_swc = sw.n, sps.n, sev.n, sscw.n, swc.n

                with nc.Block() as blk:
                    @blk.gpsimd
                    def _(g):
                        g.wait_ge(sx.h, b_sx)
                        for jb in range(2):
                            if jb >= 1:
                                g.wait_ge(sps.h, b_sps + 16 * jb)
                            ins = g.dma_start(
                                out=stage_wv[:, :]
                                    .rearrange("p (c j) -> p c j", c=NDC),
                                in_=wqkvT[:, 5120 + jb * 512:5120 + jb * 512 + 512]
                                    .rearrange("(c p) j -> p c j", p=128))
                            sw.inc(ins, 16)

                    @blk.tensor
                    def _(t):
                        t.wait_ge(sx.h, b_sx)
                        for jb in range(2):
                            t.wait_ge(sw.h, b_sw + 16 * (jb + 1))
                            for tt in range(16):
                                gi = jb * 16 + tt
                                if gi >= 2:
                                    t.wait_ge(sev.h, b_sev + gi - 1)
                                for c in range(NDC):
                                    t.matmul(
                                        ps[:, (gi % 2) * 512:(gi % 2) * 512 + 512],
                                        big_a[:, c * TOK + tt * 128:
                                              c * TOK + tt * 128 + 128],
                                        stage_wv[:, c * 512:c * 512 + 512],
                                        start=(c == 0), stop=False)
                                ins = t.matmul(
                                    ps[:, (gi % 2) * 512:(gi % 2) * 512 + 512],
                                    ones_r1[:, :],
                                    bv_row[0:1, jb * 512:jb * 512 + 512],
                                    start=False, stop=True)
                                sps.inc(ins)

                    @blk.scalar
                    def _(s):
                        for jb in range(2):
                            for tt in range(16):
                                gi = jb * 16 + tt
                                s.wait_ge(sps.h, b_sps + gi + 1)
                                half_i = gi // 8
                                if half_i >= 1 and gi % 8 == 0:
                                    s.wait_ge(sscw.h, b_sscw + 16 * half_i)
                                sev.inc(s.copy(
                                    v_st[:, (tt % 8) * 512:(tt % 8) * 512 + 512],
                                    ps[:, (gi % 2) * 512:(gi % 2) * 512 + 512]))

                    @blk.sync
                    def _(sy):
                        for jb in range(2):
                            for h8 in range(2):
                                sy.wait_ge(sev.h, b_sev + jb * 16 + h8 * 8 + 8)
                                t0g = pass_i * 16 + h8 * 8
                                ins = sy.dma_start(
                                    out=v_scr[t0g:t0g + 8, :, jb * 512:jb * 512 + 512]
                                        .rearrange("t p j -> p t j"),
                                    in_=v_st[:, :].rearrange("p (t j) -> p t j", t=8))
                                sscw.inc(ins, 16)

        for pass_i in range(2):
            emit_trig(pass_i)
            if pass_i == 0 and "trig" in taps:
                with nc.Block() as blk:
                    @blk.gpsimd
                    def _(g):
                        g.wait_ge(sev.h, sev.n)
                        sst.inc(g.dma_start(out=tap_out["trig"][:, 0:TOK],
                                            in_=sin_t[:, :]), 16)
                        sst.inc(g.dma_start(out=tap_out["trig"][:, TOK:2 * TOK],
                                            in_=cos_t[:, :]), 16)
                        with nc.allow_non_contiguous_dma(reason="debug tap"):
                            sst.inc(g.dma_start(
                                out=tap_out["trig"][:, 2 * TOK:2 * TOK + 1],
                                in_=invf[:, :],
                                allow_slow_non_contiguous=True), 16)
            emit_load_x(xT_full[:, pass_i * TOK:(pass_i + 1) * TOK], sev.n)
            emit_qk_loop([(32 + jk, 32 + jk) for jk in range(NKV)],
                         [k_scr[jk] for jk in range(NKV)], pass_i * TOK)
            emit_v_loop(pass_i)
        emit_trig(2)
        if "trigq" in taps:
            with nc.Block() as blk:
                @blk.gpsimd
                def _(g):
                    g.wait_ge(sev.h, sev.n)
                    sst.inc(g.dma_start(out=tap_out["trigq"][:, 0:TOK],
                                        in_=sin_t[:, :]), 16)
                    sst.inc(g.dma_start(out=tap_out["trigq"][:, TOK:2 * TOK],
                                        in_=cos_t[:, :]), 16)
                    sst.inc(g.dma_start(out=tap_out["trigq"][:, 2 * TOK:3 * TOK],
                                        in_=posq_bc[:, :]), 16)
        emit_load_x(xT_own, sev.n)
        emit_qk_loop([(jq, jq) for jq in range(NH)],
                     [q_scr[jq] for jq in range(NH)], 0)

        cs_scope.close()

        if "qkv_q" in taps or "qkv_k" in taps or "qkv_v" in taps:
            with nc.Block() as blk:
                @blk.gpsimd
                def _(g):
                    g.wait_ge(sscw.h, sscw.n)
                    for nm, src in (("qkv_q", q_scr), ("qkv_k", k_scr),
                                    ("qkv_v", v_scr)):
                        if nm in taps:
                            sst.inc(g.dma_start(out=tap_out[nm], in_=src), 16)

        if max_phase < 2:
            _finalize_dbg()
            ep.close()
            return nc, tap_out
        ep1.close()
        with contextlib.ExitStack() as p2:
            masks = p2.enter_context(nc.sbuf_tensor("masks", [128, 8 * 1024], BF16))
            k_g = p2.enter_context(nc.sbuf_tensor("k_g", [128, 2 * S], BF16))
            v_g = p2.enter_context(nc.sbuf_tensor("v_g", [128, 2 * S], BF16))
            NEXP = 8
            expb4 = [p2.enter_context(nc.sbuf_tensor(f"expb{j}", [128, 2 * 1024],
                                                     BF16)) for j in range(4)]

            def ebsl(vi, hf=None):
                o = ((vi // 4) % 2) * 1024
                if hf is None:
                    return expb4[vi % 4][:, o:o + 1024]
                return expb4[vi % 4][:, o + hf * 512:o + hf * 512 + 512]
            acc2 = [p2.enter_context(nc.sbuf_tensor(f"acc{i}", [128, 1024], F32))
                    for i in range(3)]
            accb2 = [p2.enter_context(nc.sbuf_tensor(f"accb{i}", [128, 1024], F32))
                     for i in range(2)]
            rrow = p2.enter_context(nc.sbuf_tensor("rrow_t", [1, 1024], F32))
            av_sb = p2.enter_context(nc.sbuf_tensor("av_sb", [128, 2 * 1024], BF16))
            att_st = p2.enter_context(nc.sbuf_tensor("att_st", [128, 2 * 1024], BF16))
            bc_sb = p2.enter_context(nc.sbuf_tensor("bc_sb", [128, 2 * 1024], F32))
            q_t = p2.enter_context(nc.sbuf_tensor("q_t2", [128, 2 * 1024], BF16))
            ps_sc = p2.enter_context(nc.psum_tensor("ps_sc", [128, 3 * 1024], F32))
            ps_av = p2.enter_context(nc.psum_tensor("ps_av", [128, 1024], F32))

            hv = p2.enter_context(nc.sbuf_tensor("hv", [1, 1], F32))

            def emit_p2(LOOP_LEN, varid):
                heads = [(sl, g, qh) for sl in range(NSLOT) for g in range(NKV)
                         for qh in range(4)]
                vstart = {}
                _vv = 0
                for h_ in heads:
                    vstart[h_] = _vv
                    _vv += LOOP_LEN[h_[0]]
                NVIS = _vv
                vhead = [0] * NVIS
                vkk = [0] * NVIS
                for hidx, h_ in enumerate(heads):
                    for kk in range(LOOP_LEN[h_[0]]):
                        vhead[vstart[h_] + kk] = hidx
                        vkk[vstart[h_] + kk] = kk
                K0 = 6
                vslot = [(k % 2) if k < K0 else [2, 0, 1][(k - K0) % 3] for k in vkk]
                vmask = [False] * NVIS
                vpar = [0] * NVIS
                mcnt = [0] * NVIS
                acnt = [[0] * NVIS, [0] * NVIS]
                firstpar = {}
                mc = 0
                ac = [0, 0]
                for vi in range(NVIS):
                    hidx = vhead[vi]
                    sl = heads[hidx][0]
                    L = LOOP_LEN[sl]
                    kk = vkk[vi]
                    vmask[vi] = (kk - (L - 8)) >= 0
                    if sl == 0:
                        par = 0 if (kk % 16) in (0, 2, 4, 6, 9, 11, 13) else 1
                    else:
                        par = 0 if (kk % 8) in (0, 2, 4, 6, 7) else 1
                    vpar[vi] = par
                    if (hidx, par) not in firstpar:
                        firstpar[(hidx, par)] = vi
                    if vmask[vi]:
                        mc += 1
                    ac[par] += 1
                    mcnt[vi] = mc
                    acnt[0][vi] = ac[0]
                    acnt[1][vi] = ac[1]

                b_ssc, b_sex, b_smk, b_sav = ssc.n, sex.n, smk.n, sav.n
                b_sfin, b_sev = sfin.n, sev.n
                b_sq2, b_skv = sq2.n, skv.n
                b_sad, b_spl, b_smg, b_sae = sad.n, spl.n, smg.n, sae.n
                b_srm, b_sdv = srm.n, sdv.n
                b_srr, b_sbe = srr.n, sbe.n
                b_scp = [scp[0].n, scp[1].n]

                def wait_exp_consumed(eng, vi):
                    pv = vi - NEXP
                    if pv < 0:
                        return
                    eng.wait_ge(sav.h, b_sav + pv + 1)
                    if vpar[pv] == 0:
                        eng.wait_ge(sad.h, b_sad + acnt[0][pv])
                    else:
                        eng.wait_ge(spl.h, b_spl + acnt[1][pv])

                def wait_masked_exp(eng, vi):
                    if vmask[vi]:
                        eng.wait_ge(smk.h, b_smk + mcnt[vi])
                    else:
                        eng.wait_ge(sex.h, b_sex + vi + 1)

                pe_slot_user = {}

                def pe_score_wait(t, slot):
                    u = pe_slot_user.get(slot)
                    if u is None:
                        return
                    if u[0] == "exp":
                        t.wait_ge(sex.h, b_sex + u[1] + 1)
                    else:
                        assert u[0] == "fin"
                        t.wait_ge(sbe.h, b_sbe + u[1] + 1)

                def pe_av(t, pv):
                    ph = vhead[pv]
                    psl = heads[ph][0]
                    ggi = psl * NKV + heads[ph][1]
                    kkp = vkk[pv]
                    Lp = LOOP_LEN[psl]
                    wait_masked_exp(t, pv)
                    if kkp == 0 and ph >= 1:
                        t.wait_ge(sae.h, b_sae + ph)
                    ins = None
                    for hf in range(2):
                        ins = t.matmul(
                            ps_av[:, hf * 512:hf * 512 + 512],
                            v_g[:, (ggi % 2) * S + kkp * 128:
                                (ggi % 2) * S + kkp * 128 + 128],
                            ebsl(pv, hf),
                            start=(kkp == 0), stop=(kkp == Lp - 1),
                            skip_group_check=True)
                    sav.inc(ins)

                def pe_fin_den(t, h2):
                    t.wait_ge(smg.h, b_smg + h2 + 1)
                    pe_score_wait(t, 2)
                    ins = None
                    for hf in range(2):
                        ins = t.matmul(
                            ps_sc[0:1, 2048 + hf * 512:2048 + hf * 512 + 512],
                            ones_c1f[:, :],
                            acc2[h2 % 3][:, hf * 512:hf * 512 + 512],
                            start=True, stop=True, skip_group_check=True)
                    sfin.inc(ins)

                def pe_fin_bc(t, h2):
                    t.wait_ge(srr.h, b_srr + h2 + 1)
                    ins = None
                    for hf in range(2):
                        ins = t.matmul(
                            ps_sc[:, 2048 + hf * 512:2048 + hf * 512 + 512],
                            ones_r1f[:, :],
                            rrow[0:1, hf * 512:hf * 512 + 512],
                            start=True, stop=True, skip_group_check=True)
                    sfin.inc(ins)
                    pe_slot_user[2] = ("fin", h2)

                def act_avevac(s, h2):
                    s.wait_ge(sav.h, b_sav + vstart[heads[h2]]
                              + LOOP_LEN[heads[h2][0]])
                    if h2 >= 2:
                        s.wait_ge(srm.h, b_srm + h2 - 1)
                    s.copy(av_sb[:, (h2 % 2) * 1024:(h2 % 2) * 1024 + 1024],
                           ps_av[:, :])
                    sae.inc(sf(s, av_sb))

                def act_denrow(s, h2):
                    s.wait_ge(sfin.h, b_sfin + 2 * h2 + 1)
                    s.copy(rrow[0:1, :], ps_sc[0:1, 2048:3072])
                    sev.inc(sf(s, rrow))

                def act_bcevac(s, h2):
                    s.wait_ge(sfin.h, b_sfin + 2 * h2 + 2)
                    if h2 >= 2:
                        s.wait_ge(srm.h, b_srm + h2 - 1)
                    s.copy(bc_sb[:, (h2 % 2) * 1024:(h2 % 2) * 1024 + 1024],
                           ps_sc[:, 2048:3072])
                    sbe.inc(sf(s, bc_sb))

                def dve_merge(v, h2):
                    L2 = LOOP_LEN[heads[h2][0]]
                    vs2 = vstart[heads[h2]]
                    v.wait_ge(spl.h, b_spl + acnt[1][vs2 + L2 - 1])
                    smg.inc(v.tensor_add(acc2[h2 % 3][:, :], acc2[h2 % 3][:, :],
                                         accb2[h2 % 2][:, :]))

                def dve_recip_row(v, h2):
                    v.wait_ge(sev.h, b_sev + h2 + 1)
                    srr.inc(v.reciprocal(rrow[0:1, :], rrow[0:1, :]))

                def dve_mul(v, h2):
                    v.wait_ge(sbe.h, b_sbe + h2 + 1)
                    v.wait_ge(sae.h, b_sae + h2 + 1)
                    if h2 >= 2:
                        v.wait_ge(scp[h2 % 2].h, b_scp[h2 % 2] + 16 * (h2 // 2))
                    srm.inc(v.tensor_mul(
                        att_st[:, (h2 % 2) * 1024:(h2 % 2) * 1024 + 1024],
                        av_sb[:, (h2 % 2) * 1024:(h2 % 2) * 1024 + 1024],
                        bc_sb[:, (h2 % 2) * 1024:(h2 % 2) * 1024 + 1024]))

                def st_store(sy, ph):
                    psl2, pg2, pqh2 = heads[ph]
                    phead = pg2 * 4 + pqh2
                    sy.wait_ge(srm.h, b_srm + ph + 1)
                    ins = sy.dma_start(
                        out=attn_scr[phead][:, psl2 * 1024:psl2 * 1024 + 1024],
                        in_=att_st[:, (ph % 2) * 1024:(ph % 2) * 1024 + 1024])
                    scp[ph % 2].inc(ins, 16)

                for sl in range(NSLOT):
                    L = LOOP_LEN[sl]
                    with nc.Block() as blk:
                        @blk.vector
                        def _(v, sl=sl, L=L):
                            ins = None
                            for r in range(8):
                                kk0 = float((L - 8 + r) * 128)
                                ins = v.tensor_scalar(
                                    masks[:, r * 1024:(r + 1) * 1024],
                                    posq_bc[:, sl * 1024:(sl + 1) * 1024],
                                    kk0, iota_p[:, 0:1], ALU.subtract, ALU.is_ge)
                            sdv.inc(ins)

                    with nc.Block() as blk:
                        @blk.gpsimd
                        def _(g_, sl=sl, L=L):
                            for g in range(NKV):
                                for qh in range(4):
                                    hidx = sl * 32 + g * 4 + qh
                                    vs = vstart[(sl, g, qh)]
                                    for kk in range(L):
                                        vi = vs + kk
                                        if vpar[vi] != 1:
                                            continue
                                        wait_masked_exp(g_, vi)
                                        eb = ebsl(vi)
                                        if vi == firstpar[(hidx, 1)]:
                                            if hidx >= 2:
                                                g_.wait_ge(smg.h, b_smg + hidx - 1)
                                            spl.inc(g_.tensor_copy(
                                                accb2[hidx % 2][:, :], eb))
                                        else:
                                            spl.inc(g_.tensor_tensor(
                                                accb2[hidx % 2][:, :],
                                                accb2[hidx % 2][:, :], eb, ALU.add))

                        @blk.sync
                        def _(sy, sl=sl, L=L):
                            for g in range(NKV):
                                for qh in range(4):
                                    hidx = sl * 32 + g * 4 + qh
                                    head = g * 4 + qh
                                    if hidx >= 2:
                                        prev = heads[hidx - 2]
                                        sy.wait_ge(ssc.h, b_ssc + vstart[prev]
                                                   + LOOP_LEN[prev[0]])
                                    ins = sy.dma_start(
                                        out=q_t[:, (hidx % 2) * 1024:
                                                (hidx % 2) * 1024 + 1024],
                                        in_=q_scr[head][:, sl * 1024:sl * 1024 + 1024])
                                    sq2.inc(ins, 16)
                                    if hidx >= 2:
                                        st_store(sy, hidx - 2)
                            if sl == NSLOT - 1:
                                st_store(sy, 62)
                                st_store(sy, 63)

                        @blk.tensor
                        def _(t, sl=sl, L=L):
                            t.wait_ge(sdv.h, b_sdv + sl + 1)
                            for g in range(NKV):
                                gidx = sl * NKV + g
                                for qh in range(4):
                                    hidx = sl * 32 + g * 4 + qh
                                    vs = vstart[(sl, g, qh)]
                                    for kk in range(L):
                                        vi = vs + kk
                                        slot = vslot[vi]
                                        if kk == 0:
                                            t.wait_ge(skv.h, b_skv + 32 * (gidx + 1))
                                            t.wait_ge(sq2.h, b_sq2 + 16 * (hidx + 1))
                                        pe_score_wait(t, slot)
                                        ins = None
                                        for hf in range(2):
                                            ins = t.matmul(
                                                ps_sc[:, slot * 1024 + hf * 512:
                                                      slot * 1024 + hf * 512 + 512],
                                                k_g[:, (gidx % 2) * S + kk * 128:
                                                    (gidx % 2) * S + kk * 128 + 128],
                                                q_t[:, (hidx % 2) * 1024 + hf * 512:
                                                    (hidx % 2) * 1024 + hf * 512 + 512],
                                                start=True, stop=True,
                                                skip_group_check=True)
                                        ssc.inc(ins)
                                        pe_slot_user[slot] = ("exp", vi)
                                        if vi >= 2:
                                            pe_av(t, vi - 2)
                                        if hidx >= 2:
                                            if kk == 0:
                                                pe_fin_den(t, hidx - 2)
                                            elif kk == 3:
                                                pe_fin_bc(t, hidx - 2)
                            if sl == NSLOT - 1:
                                pe_av(t, NVIS - 2)
                                pe_av(t, NVIS - 1)
                                pe_fin_den(t, 62)
                                pe_fin_bc(t, 62)
                                pe_fin_den(t, 63)
                                pe_fin_bc(t, 63)

                        @blk.scalar
                        def _(s, sl=sl, L=L):
                            def kv_dma(gp):
                                g2 = gp % NKV
                                Lg = LOOP_LEN[gp // NKV]
                                if gp >= 2:
                                    s.wait_ge(sae.h, b_sae + 4 * (gp - 1))
                                ins = s.dma_start(
                                    out=k_g[:, (gp % 2) * S:(gp % 2) * S + Lg * 128],
                                    in_=k_scr[g2][:, 0:Lg * 128])
                                skv.inc(ins, 16)
                                ins = s.dma_start(
                                    out=v_g[:, (gp % 2) * S:(gp % 2) * S + Lg * 128]
                                        .rearrange("p (t h) -> p t h", t=Lg),
                                    in_=v_scr[0:Lg, :, g2 * 128:(g2 + 1) * 128]
                                        .rearrange("t p h -> p t h"))
                                skv.inc(ins, 16)

                            if sl == 0:
                                kv_dma(0)
                                kv_dma(1)
                            for g in range(NKV):
                                for qh in range(4):
                                    hidx = sl * 32 + g * 4 + qh
                                    vs = vstart[(sl, g, qh)]
                                    for kk in range(L):
                                        vi = vs + kk
                                        slot = vslot[vi]
                                        s.wait_ge(ssc.h, b_ssc + vi + 1)
                                        wait_exp_consumed(s, vi)
                                        sex.inc(s.activation(
                                            ebsl(vi),
                                            ps_sc[:, slot * 1024:slot * 1024 + 1024],
                                            AF.Exp, bias=negc[:, 0:1],
                                            scale=SM_SCALE))
                                        if kk == 0:
                                            if hidx >= 1:
                                                act_avevac(s, hidx - 1)
                                            if hidx % 4 == 0 and 2 <= hidx // 4 + 1 <= 15:
                                                kv_dma(hidx // 4 + 1)
                                        elif kk == 1 and hidx >= 2:
                                            act_denrow(s, hidx - 2)
                                        elif kk == 4 and hidx >= 2:
                                            act_bcevac(s, hidx - 2)
                            if sl == NSLOT - 1:
                                act_avevac(s, 63)
                                act_denrow(s, 62)
                                act_bcevac(s, 62)
                                act_denrow(s, 63)
                                act_bcevac(s, 63)

                        @blk.vector
                        def _(v, sl=sl, L=L):
                            def dve_add(vj):
                                if vj < 0 or vpar[vj] != 0:
                                    return
                                hj = vhead[vj]
                                eb = ebsl(vj)
                                if not vmask[vj]:
                                    v.wait_ge(sex.h, b_sex + vj + 1)
                                if vj == firstpar[(hj, 0)]:
                                    sad.inc(v.tensor_copy(acc2[hj % 3][:, :], eb))
                                else:
                                    sad.inc(v.tensor_add(acc2[hj % 3][:, :],
                                                         acc2[hj % 3][:, :], eb))

                            for g in range(NKV):
                                for qh in range(4):
                                    hidx = sl * 32 + g * 4 + qh
                                    vs = vstart[(sl, g, qh)]
                                    for kk in range(L):
                                        vi = vs + kk
                                        if vmask[vi]:
                                            v.wait_ge(sex.h, b_sex + vi + 1)
                                            rel = kk - (L - 8)
                                            eb = ebsl(vi)
                                            smk.inc(v.tensor_mul(
                                                eb, eb,
                                                masks[:, rel * 1024:rel * 1024 + 1024]))
                                        dve_add(vi - 2)
                                        if kk == 1 and hidx >= 1:
                                            dve_merge(v, hidx - 1)
                                        elif kk == 2 and hidx >= 2:
                                            dve_recip_row(v, hidx - 2)
                                        elif kk == 5 and hidx >= 2:
                                            dve_mul(v, hidx - 2)
                            if sl == NSLOT - 1:
                                dve_add(NVIS - 2)
                                dve_add(NVIS - 1)
                                dve_merge(v, 63)
                                dve_recip_row(v, 62)
                                dve_mul(v, 62)
                                dve_recip_row(v, 63)
                                dve_mul(v, 63)

                assert ssc.n == b_ssc + NVIS and sex.n == b_sex + NVIS
                assert sav.n == b_sav + NVIS
                assert sad.n == b_sad + acnt[0][-1] and spl.n == b_spl + acnt[1][-1]
                assert sfin.n == b_sfin + 2 * 64 and sev.n == b_sev + 64
                assert sdv.n == b_sdv + NSLOT and smg.n == b_smg + 64
                assert sae.n == b_sae + 64 and srm.n == b_srm + 64
                assert srr.n == b_srr + 64 and sbe.n == b_sbe + 64
                with nc.Block() as blk:
                    @blk.gpsimd
                    def _(g_):
                        shs.inc(g_.memset(hv[:, :], float(varid)))

            engines5 = [nc.tensor, nc.vector, nc.scalar, nc.sync, nc.gpsimd]
            sems_p2 = (ssc, sex, smk, sav, sad, spl, smg, sfin, sev, sae,
                       srm, sdv, sq2, skv, scp[0], scp[1], shs, srr, sbe)
            pre = {s_: s_.n for s_ in sems_p2}
            for arm in nc.switch_core_id(engines=engines5, n=128):
                nc.br("p2_var0" if arm.logical % 2 == 0 else "p2_var1")
            nc.br("p2_join")
            ve = [e_.engine for e_ in engines5]
            with nc.body("p2_var0", valid_engines=ve):
                emit_p2((8, 32), 0)
                nc.br("p2_join")
            post0 = {s_: s_.n for s_ in sems_p2}
            for s_, v_ in pre.items():
                s_.n = v_
            with nc.body("p2_var1", valid_engines=ve):
                emit_p2((16, 24), 1)
                nc.br("p2_join")
            post1 = {s_: s_.n for s_ in sems_p2}
            for s_ in (smg, sfin, sev, sae, srm, sdv, sq2, skv, scp[0],
                       scp[1], shs, srr, sbe):
                assert post0[s_] == post1[s_], s_.h
            for s_, v_ in post0.items():
                s_.n = v_
            nc.switch_bb("p2_join")
            with nc.Block() as blk:
                @blk.sync
                def _(sy):
                    sy.wait_ge(shs.h, shs.n)
                    sst.inc(sy.dma_start(out=hsel, in_=hv[:, :]), 16)




        ep.close()

        lp = contextlib.ExitStack()
        big_a = lp.enter_context(nc.sbuf_tensor("big_a", [128, NDC * TOK], BF16))
        rstd_bc = lp.enter_context(nc.sbuf_tensor("rstd_bc", [128, TOK], BF16))
        mr_bc = lp.enter_context(nc.sbuf_tensor("mr_bc", [128, TOK], BF16))
        ssum = lp.enter_context(nc.sbuf_tensor("ssum", [1, TOK], F32))
        ssq = lp.enter_context(nc.sbuf_tensor("ssq", [1, TOK], F32))

        with nc.Block() as blk:
            @blk.sync
            def _(sy):
                sy.wait_ge(scp[0].h, scp[0].n)
                sy.wait_ge(scp[1].h, scp[1].n)
                for hc in range(NH):
                    ins = sy.dma_start(out=big_a[:, hc * TOK:(hc + 1) * TOK],
                                       in_=attn_scr[hc])
                    sat.inc(ins, 16)

            @blk.tensor
            def _(t):
                t.wait_ge(sat.h, sat.n)

        if "attn" in taps:
            with nc.Block() as blk:
                @blk.gpsimd
                def _(g):
                    g.wait_ge(sat.h, sat.n)
                    sst.inc(g.dma_start(out=tap_out["attn"], in_=big_a[:, :]), 16)

        if max_phase < 3:
            _finalize_dbg()
            lp.close()
            return nc, tap_out

        def emit_proj_ln(wT_ap, b_col, lora, scr_out, bt_st=None, g_sb=None):
            with contextlib.ExitStack() as ls:
                stage_w = ls.enter_context(
                    nc.sbuf_tensor(_nm("stage_w3"), [128, 2 * NDC * 128], BF16))
                x_st = (None if lora else
                        ls.enter_context(nc.sbuf_tensor(_nm("x_st"), [128, 2 * TOK], F32)))
                evac = ls.enter_context(nc.sbuf_tensor(_nm("evac3"), [128, 2 * 512], F32))
                sqb = ls.enter_context(nc.sbuf_tensor(_nm("sqb"), [128, 2 * 512], BF16))
                ystg = ls.enter_context(nc.sbuf_tensor(_nm("ystg"), [128, TOK], BF16))
                ps = ls.enter_context(nc.psum_tensor(_nm("p3_ps"), [128, 2 * 512], F32))
                ps_st = ls.enter_context(nc.psum_tensor(_nm("p3_st"), [1, 1024], F32))

                b_sw = [swp[0].n, swp[1].n]
                b_sx2 = [sxp[0].n, sxp[1].n]
                b_sps, b_spst = sps.n, spst.n
                b_sev, b_sdv, b_sscw, b_sfin = sev.n, sdv.n, sscw.n, sfin.n
                NG = NDC * NT

                def emit_stat_mm(t, gi):
                    if DBG_NOSTAT:
                        return
                    t.wait_ge(sev.h, b_sev + 2 * gi + 2)
                    if gi >= 1:
                        t.wait_ge(sfin.h, b_sfin + gi)
                    nt = gi % NT
                    t.matmul(ps_st[0:1, 0:512], ones_c1[:, :],
                             ystg[:, nt * 512:nt * 512 + 512],
                             start=True, stop=True, skip_group_check=True)
                    ins = t.matmul(ps_st[0:1, 512:1024], ones_c1[:, :],
                                   sqb[:, (gi % 2) * 512:(gi % 2) * 512 + 512],
                                   start=True, stop=True, skip_group_check=True)
                    spst.inc(ins)

                with nc.Block() as blk:
                    @blk.gpsimd
                    def _(g):
                        for o in range(NDC):
                            if o >= 2:
                                g.wait_ge(sps.h, b_sps + NT * (o - 1))
                            ins = g.dma_start(
                                out=stage_w[:, :]
                                    .rearrange("p (b c j) -> p b c j", b=2, c=NDC)
                                    [:, o % 2, :, :],
                                in_=wT_ap[:, o * 128:(o + 1) * 128]
                                    .rearrange("(c p) j -> p c j", p=128))
                            swp[o % 2].inc(ins, 16)


                    @blk.tensor
                    def _(t):
                        for o in range(NDC):
                            t.wait_ge(swp[o % 2].h, b_sw[o % 2] + 16 * (o // 2 + 1))
                            for nt in range(NT):
                                gi = o * NT + nt
                                if gi >= 2:
                                    t.wait_ge(sev.h, b_sev + 2 * (gi - 1))
                                ins = None
                                for c in range(NDC):
                                    ins = t.matmul(
                                        ps[:, (gi % 2) * 512:(gi % 2) * 512 + 512],
                                        stage_w[:, (o % 2) * NDC * 128 + c * 128:
                                                (o % 2) * NDC * 128 + c * 128 + 128],
                                        big_a[:, c * TOK + nt * 512:
                                              c * TOK + nt * 512 + 512],
                                        start=(c == 0),
                                        stop=(c == NDC - 1 and not lora))
                                if lora:
                                    ins = t.matmul(
                                        ps[:, (gi % 2) * 512:(gi % 2) * 512 + 512],
                                        bt_st[:, o * 128:o * 128 + 128],
                                        g_sb[:, nt * 512:nt * 512 + 512],
                                        start=False, stop=True,
                                        skip_group_check=True)
                                sps.inc(ins)
                                if gi >= 2:
                                    emit_stat_mm(t, gi - 2)
                        emit_stat_mm(t, NG - 2)
                        emit_stat_mm(t, NG - 1)

                    @blk.scalar
                    def _(s):
                        for o in range(NDC):
                            for nt in range(NT):
                                gi = o * NT + nt
                                s.wait_ge(sps.h, b_sps + gi + 1)
                                if gi >= 2:
                                    s.wait_ge(sdv.h, b_sdv + gi - 1)
                                sev.inc(s.activation(
                                    evac[:, (gi % 2) * 512:(gi % 2) * 512 + 512],
                                    ps[:, (gi % 2) * 512:(gi % 2) * 512 + 512],
                                    AF.Identity, bias=b_col[:, o:o + 1], scale=1.0))
                                s.wait_ge(sdv.h, b_sdv + gi + 1)
                                if gi >= 2 and not DBG_NOSTAT:
                                    s.wait_ge(spst.h, b_spst + gi - 1)
                                sev.inc(s.activation(
                                    sqb[:, (gi % 2) * 512:(gi % 2) * 512 + 512],
                                    ystg[:, nt * 512:nt * 512 + 512],
                                    AF.Square, bias=0.0, scale=1.0))

                    @blk.vector
                    def _(v):
                        def rowadd(gi2):
                            if DBG_NOSTAT:
                                return
                            nt2 = gi2 % NT
                            v.wait_ge(spst.h, b_spst + gi2 + 1)
                            if gi2 < NT:
                                v.tensor_copy(ssum[0:1, nt2 * 512:nt2 * 512 + 512],
                                              ps_st[0:1, 0:512])
                                ins = v.tensor_copy(
                                    ssq[0:1, nt2 * 512:nt2 * 512 + 512],
                                    ps_st[0:1, 512:1024])
                            else:
                                v.tensor_add(ssum[0:1, nt2 * 512:nt2 * 512 + 512],
                                             ssum[0:1, nt2 * 512:nt2 * 512 + 512],
                                             ps_st[0:1, 0:512])
                                ins = v.tensor_add(
                                    ssq[0:1, nt2 * 512:nt2 * 512 + 512],
                                    ssq[0:1, nt2 * 512:nt2 * 512 + 512],
                                    ps_st[0:1, 512:1024])
                            sfin.inc(ins)

                        for o in range(NDC):
                            for nt in range(NT):
                                gi = o * NT + nt
                                v.wait_ge(sev.h, b_sev + 2 * gi + 1)
                                if nt == 0 and o >= 1:
                                    v.wait_ge(sscw.h, b_sscw + 16 * o)
                                src1 = (x_st[:, (o % 2) * TOK + nt * 512:
                                             (o % 2) * TOK + nt * 512 + 512]
                                        if not lora else
                                        big_a[:, o * TOK + nt * 512:
                                              o * TOK + nt * 512 + 512])
                                if not lora:
                                    v.wait_ge(sxp[o % 2].h,
                                              b_sx2[o % 2] + 16 * (o // 2 + 1))
                                sdv.inc(v.tensor_add(
                                    ystg[:, nt * 512:nt * 512 + 512],
                                    evac[:, (gi % 2) * 512:(gi % 2) * 512 + 512],
                                    src1))
                                if gi >= 2:
                                    rowadd(gi - 2)
                        rowadd(NG - 2)
                        rowadd(NG - 1)

                    @blk.sync
                    def _(sy):
                        for o in range(NDC):
                            if not lora:
                                if o >= 2:
                                    sy.wait_ge(sdv.h, b_sdv + NT * (o - 1))
                                ins = sy.dma_start(
                                    out=x_st[:, (o % 2) * TOK:(o % 2) * TOK + TOK],
                                    in_=xT_own[o * 128:(o + 1) * 128, :])
                                sxp[o % 2].inc(ins, 16)
                            if o >= 1:
                                sy.wait_ge(sdv.h, b_sdv + NT * o)
                                ins = sy.dma_start(out=scr_out[o - 1], in_=ystg[:, :])
                                sscw.inc(ins, 16)
                        sy.wait_ge(sdv.h, b_sdv + NT * NDC)
                        ins = sy.dma_start(out=scr_out[NDC - 1], in_=ystg[:, :])
                        sscw.inc(ins, 16)

            if DBG_NOROWS:
                return
            with contextlib.ExitStack() as ls2:
                ps_ln = ls2.enter_context(nc.psum_tensor(_nm("ps_ln"), [128, TOK], F32))
                b2_sdv, b2_sev, b2_sps, b2_sfin = sdv.n, sev.n, sps.n, sfin.n
                with nc.Block() as blk:
                    @blk.vector
                    def _(v):
                        v.wait_ge(sfin.h, b2_sfin)
                        v.tensor_scalar(ssum[0:1, :], ssum[0:1, :], 1.0 / D,
                                        None, ALU.mult)
                        v.tensor_scalar(ssq[0:1, :], ssq[0:1, :], 1.0 / D,
                                        None, ALU.mult)
                        v.tensor_mul(ps_ln[0:1, 0:TOK], ssum[0:1, :], ssum[0:1, :])
                        v.tensor_sub(ssq[0:1, :], ssq[0:1, :], ps_ln[0:1, 0:TOK])
                        sdv.inc(vf(v, ssq))

                    @blk.scalar
                    def _(s):
                        s.wait_ge(sdv.h, b2_sdv + 1)
                        s.activation(ps_ln[0:1, 0:TOK], ssq[0:1, :],
                                     AF.Sqrt, bias=epsb[0:1, 0:1], scale=1.0)
                        sev.inc(sf(s, ps_ln))

                    @blk.vector
                    def _(v):
                        v.wait_ge(sev.h, b2_sev + 1)
                        v.reciprocal(ssq[0:1, :], ps_ln[0:1, 0:TOK])
                        v.tensor_mul(ssum[0:1, :], ssum[0:1, :], ssq[0:1, :])
                        sdv.inc(vf(v, ssum))

                    @blk.tensor
                    def _(t):
                        t.wait_ge(sdv.h, b2_sdv + 2)
                        ins = None
                        for i in range(NT):
                            ins = t.matmul(ps_ln[:, i * 512:(i + 1) * 512],
                                           ones_r1f[:, :],
                                           ssq[0:1, i * 512:(i + 1) * 512],
                                           start=True, stop=True,
                                           skip_group_check=True)
                        sps.inc(ins)
                        t.wait_ge(sev.h, b2_sev + 2)
                        ins = None
                        for i in range(NT):
                            ins = t.matmul(ps_ln[:, i * 512:(i + 1) * 512],
                                           ones_r1f[:, :],
                                           ssum[0:1, i * 512:(i + 1) * 512],
                                           start=True, stop=True,
                                           skip_group_check=True)
                        sps.inc(ins)

                    @blk.scalar
                    def _(s):
                        s.wait_ge(sps.h, b2_sps + 1)
                        s.copy(rstd_bc[:, :], ps_ln[:, :])
                        sev.inc(sf(s, rstd_bc))
                        s.wait_ge(sps.h, b2_sps + 2)
                        s.copy(mr_bc[:, :], ps_ln[:, :])
                        sev.inc(sf(s, mr_bc))

        emit_proj_ln(woT, bo_col, False, [y1_scr[o] for o in range(NDC)])

        if "y1" in taps or "lnr1" in taps:
            with nc.Block() as blk:
                @blk.gpsimd
                def _(g):
                    g.wait_ge(sscw.h, sscw.n)
                    g.wait_ge(sev.h, sev.n)
                    if "y1" in taps:
                        sst.inc(g.dma_start(out=tap_out["y1"], in_=y1_scr), 16)
                    if "lnr1" in taps:
                        sst.inc(g.dma_start(out=tap_out["lnr1"][0:1, 0:TOK],
                                            in_=ssq[0:1, :]), 16)
                        sst.inc(g.dma_start(out=tap_out["lnr1"][0:1, TOK:2 * TOK],
                                            in_=ssum[0:1, :]), 16)

        if max_phase < 4:
            _finalize_dbg()
            lp.close()
            return nc, tap_out
        with contextlib.ExitStack() as p4p:
            y1r = p4p.enter_context(nc.sbuf_tensor("y1r", [128, 2 * TOK], BF16))
            b_sx2 = [sxp[0].n, sxp[1].n]
            b_sdv = sdv.n
            with nc.Block() as blk:
                @blk.sync
                def _(sy):
                    sy.wait_ge(sev.h, sev.n)
                    sy.wait_ge(sscw.h, sscw.n)
                    for c in range(NDC):
                        if c >= 2:
                            sy.wait_ge(sdv.h, b_sdv + 2 * (c - 1))
                        ins = sy.dma_start(
                            out=y1r[:, (c % 2) * TOK:(c % 2) * TOK + TOK],
                            in_=y1_scr[c])
                        sxp[c % 2].inc(ins, 16)

                @blk.vector
                def _(v):
                    for c in range(NDC):
                        v.wait_ge(sxp[c % 2].h, b_sx2[c % 2] + 16 * (c // 2 + 1))
                        dst = big_a[:, c * TOK:(c + 1) * TOK]
                        v.tensor_mul(dst, y1r[:, (c % 2) * TOK:(c % 2) * TOK + TOK],
                                     rstd_bc[:, :])
                        sdv.inc(v.tensor_sub(dst, dst, mr_bc[:, :]), 2)

        if "y1n" in taps:
            with nc.Block() as blk:
                @blk.gpsimd
                def _(g):
                    g.wait_ge(sdv.h, sdv.n)
                    sst.inc(g.dma_start(out=tap_out["y1n"], in_=big_a[:, :]), 16)

        if max_phase < 5:
            _finalize_dbg()
            lp.close()
            return nc, tap_out
        p4s = contextlib.ExitStack()
        bt_st = p4s.enter_context(nc.sbuf_tensor("bt_st", [128, D], BF16))
        g_sb = p4s.enter_context(nc.sbuf_tensor("g_sb", [128, TOK], BF16))
        with contextlib.ExitStack() as p4a:
            wr_st = p4a.enter_context(nc.sbuf_tensor("wr_st", [128, NDC * 8], BF16))
            at_st = p4a.enter_context(nc.sbuf_tensor("at_st", [128, NDC * 128], BF16))
            lg = p4a.enter_context(nc.sbuf_tensor("lg", [8, TOK], F32))
            lgT = p4a.enter_context(nc.sbuf_tensor("lgT", [128, 16 * 8], F32))
            rsc = p4a.enter_context(nc.sbuf_tensor("rsc", [128, 4 * 16], F32))
            rse = p4a.enter_context(nc.sbuf_tensor("rse", [128, 16 * 8], F32))
            rcl = p4a.enter_context(nc.sbuf_tensor("rcl", [128, 16 * 4], F32))
            rc2 = p4a.enter_context(nc.sbuf_tensor("rc2", [128, 8], F32))
            wfb = p4a.enter_context(nc.sbuf_tensor("wfb", [128, 16 * 8], BF16))
            route_row = p4a.enter_context(nc.sbuf_tensor("route_row", [8, TOK], BF16))
            routebc = p4a.enter_context(nc.sbuf_tensor("routebc", [128, TOK], BF16))
            hevac = p4a.enter_context(nc.sbuf_tensor("hevac", [128, 2 * 512], F32))
            ps_rb = p4a.enter_context(nc.psum_tensor("ps_rb", [128, 2 * 512], F32))
            ps_t = p4a.enter_context(nc.psum_tensor("ps_t", [128, 8], F32))
            ps_rt = p4a.enter_context(nc.psum_tensor("ps_rt", [8, TOK], BF16))

            b_sw, b_sps, b_sev, b_sdv, b_sfin = sw.n, sps.n, sev.n, sdv.n, sfin.n
            y1n_done = sdv.n

            with nc.Block() as blk:
                @blk.gpsimd
                def _(g):
                    sw.inc(g.dma_start(
                        out=wr_st[:, :].rearrange("p (c e) -> p c e", c=NDC),
                        in_=wrT.rearrange("(c p) e -> p c e", p=128)), 16)
                    sw.inc(g.dma_start(
                        out=at_st[:, :].rearrange("p (c r) -> p c r", c=NDC),
                        in_=At_d.rearrange("(c p) r -> p c r", p=128)), 16)
                    sw.inc(g.dma_start(out=bt_st[:, :], in_=Bt_d), 16)

                @blk.tensor
                def _(t):
                    t.wait_ge(sw.h, b_sw + 16)
                    t.wait_ge(sdv.h, y1n_done)
                    for nt in range(NT):
                        if nt >= 2:
                            t.wait_ge(sev.h, b_sev + nt - 1)
                        ins = None
                        for c in range(NDC):
                            ins = t.matmul(
                                ps_rb[0:8, (nt % 2) * 512:(nt % 2) * 512 + 512],
                                wr_st[:, c * 8:c * 8 + 8],
                                big_a[:, c * TOK + nt * 512:c * TOK + nt * 512 + 512],
                                start=(c == 0), stop=(c == NDC - 1))
                        sps.inc(ins)

                @blk.scalar
                def _(s):
                    for nt in range(NT):
                        s.wait_ge(sps.h, b_sps + nt + 1)
                        s.copy(lg[0:8, nt * 512:nt * 512 + 512],
                               ps_rb[0:8, (nt % 2) * 512:(nt % 2) * 512 + 512])
                        sev.inc(sf(s, lg))

                @blk.tensor
                def _(t):
                    for tt in range(16):
                        t.wait_ge(sev.h, b_sev + tt // 4 + 1)
                        if tt >= 1:
                            t.wait_ge(sev.h, b_sev + NT + tt)
                        ins = t.transpose(ps_t[:, :],
                                          lg[0:8, tt * 128:tt * 128 + 128],
                                          ident_f8[:, :])
                        sfin.inc(ins)

                @blk.scalar
                def _(s):
                    for tt in range(16):
                        s.wait_ge(sfin.h, b_sfin + tt + 1)
                        s.copy(lgT[:, tt * 8:tt * 8 + 8], ps_t[:, :])
                        sev.inc(sf(s, lgT))

                def _m1(tt):
                    return rcl[:, tt * 4:tt * 4 + 1]

                def _nm1(tt):
                    return rcl[:, tt * 4 + 1:tt * 4 + 2]

                def _m2(tt):
                    return rcl[:, tt * 4 + 2:tt * 4 + 3]

                @blk.vector
                def _(v):
                    for base in range(0, 16, 2):
                        pair = (base, base + 1)
                        for tt in pair:
                            v.wait_ge(sev.h, b_sev + NT + tt + 1)
                        for tt in pair:
                            v.reduce_max(_m1(tt), lgT[:, tt * 8:tt * 8 + 8], AX.X)
                        for tt in pair:
                            pa = (tt % 2) * 32
                            v.tensor_scalar(rsc[:, pa:pa + 8],
                                            lgT[:, tt * 8:tt * 8 + 8],
                                            _m1(tt), None, ALU.is_equal)
                        for tt in pair:
                            v.tensor_scalar(_nm1(tt), _m1(tt), -1.0, None, ALU.mult)
                        for tt in pair:
                            pa = (tt % 2) * 32
                            v.scalar_tensor_tensor(rsc[:, pa + 8:pa + 16],
                                                   rsc[:, pa:pa + 8], -1e30,
                                                   lgT[:, tt * 8:tt * 8 + 8],
                                                   ALU.mult, ALU.add)
                        for tt in pair:
                            pa = (tt % 2) * 32
                            v.reduce_max(_m2(tt), rsc[:, pa + 8:pa + 16], AX.X)
                        for tt in pair:
                            sdv.inc(vf(v, rcl))

                @blk.scalar
                def _(s):
                    for tt in range(16):
                        s.wait_ge(sdv.h, b_sdv + tt + 1)
                        s.activation(rse[:, tt * 8:tt * 8 + 8],
                                     lgT[:, tt * 8:tt * 8 + 8], AF.Exp,
                                     bias=_nm1(tt), scale=1.0)
                        sev.inc(sf(s, rse))

                @blk.vector
                def _(v):
                    for base in range(0, 16, 2):
                        pair = (base, base + 1)
                        for tt in pair:
                            v.wait_ge(sev.h, b_sev + NT + 16 + tt + 1)
                        for tt in pair:
                            pa = (tt % 2) * 32
                            v.tensor_scalar(rsc[:, pa + 16:pa + 24],
                                            lgT[:, tt * 8:tt * 8 + 8],
                                            _m2(tt), None, ALU.is_ge)
                        for tt in pair:
                            pa = (tt % 2) * 32
                            v.tensor_mul(rsc[:, pa + 16:pa + 24],
                                         rsc[:, pa + 16:pa + 24],
                                         rse[:, tt * 8:tt * 8 + 8])
                        for tt in pair:
                            pa = (tt % 2) * 32
                            v.reduce_sum(rc2[:, (tt % 2) * 4:(tt % 2) * 4 + 1],
                                         rsc[:, pa + 16:pa + 24], AX.X)
                        for tt in pair:
                            v.reciprocal(rc2[:, (tt % 2) * 4 + 1:(tt % 2) * 4 + 2],
                                         rc2[:, (tt % 2) * 4:(tt % 2) * 4 + 1])
                        for tt in pair:
                            v.tensor_scalar(rc2[:, (tt % 2) * 4 + 2:(tt % 2) * 4 + 3],
                                            rc2[:, (tt % 2) * 4 + 1:(tt % 2) * 4 + 2],
                                            2.0, None, ALU.mult)
                        for tt in pair:
                            pa = (tt % 2) * 32
                            v.tensor_scalar(wfb[:, tt * 8:tt * 8 + 8],
                                            rsc[:, pa + 16:pa + 24],
                                            rc2[:, (tt % 2) * 4 + 2:(tt % 2) * 4 + 3],
                                            None, ALU.mult)
                        for tt in pair:
                            sdv.inc(vf(v, wfb))

                @blk.tensor
                def _(t):
                    for tt in range(16):
                        t.wait_ge(sdv.h, b_sdv + 16 + tt + 1)
                        ins = t.transpose(ps_rt[0:8, tt * 128:tt * 128 + 128],
                                          wfb[:, tt * 8:tt * 8 + 8],
                                          ident_b[:, :])
                        sfin.inc(ins)

                @blk.scalar
                def _(s):
                    s.wait_ge(sfin.h, b_sfin + 32)
                    s.copy(route_row[0:8, :], ps_rt[0:8, :])
                    sev.inc(sf(s, route_row))

                @blk.tensor
                def _(t):
                    t.wait_ge(sev.h, b_sev + NT + 32 + 1)
                    for nt in range(NT):
                        if nt >= 2:
                            t.wait_ge(sev.h, b_sev + NT + 32 + 1 + nt - 1)
                        ins = t.matmul(ps_rb[:, (nt % 2) * 512:(nt % 2) * 512 + 512],
                                       ebc[0:8, :],
                                       route_row[0:8, nt * 512:nt * 512 + 512],
                                       start=True, stop=True,
                                       skip_group_check=True)
                        sps.inc(ins)

                @blk.scalar
                def _(s):
                    for nt in range(NT):
                        s.wait_ge(sps.h, b_sps + NT + nt + 1)
                        s.copy(routebc[:, nt * 512:nt * 512 + 512],
                               ps_rb[:, (nt % 2) * 512:(nt % 2) * 512 + 512])
                        sev.inc(sf(s, routebc))

            b_sw2, b_sps2, b_sev2, b_sdv2 = sw.n, sps.n, sev.n, sdv.n
            with nc.Block() as blk:
                @blk.tensor
                def _(t):
                    t.wait_ge(sw.h, b_sw2)
                    t.wait_ge(sev.h, b_sev2)
                    for nt in range(NT):
                        if nt >= 2:
                            t.wait_ge(sev.h, b_sev2 + nt - 1)
                        ins = None
                        for c in range(NDC):
                            ins = t.matmul(
                                ps_rb[:, (nt % 2) * 512:(nt % 2) * 512 + 512],
                                at_st[:, c * 128:c * 128 + 128],
                                big_a[:, c * TOK + nt * 512:c * TOK + nt * 512 + 512],
                                start=(c == 0), stop=(c == NDC - 1))
                        sps.inc(ins)

                @blk.scalar
                def _(s):
                    for nt in range(NT):
                        s.wait_ge(sps.h, b_sps2 + nt + 1)
                        if nt >= 2:
                            s.wait_ge(sdv.h, b_sdv2 + nt - 1)
                        s.copy(hevac[:, (nt % 2) * 512:(nt % 2) * 512 + 512],
                               ps_rb[:, (nt % 2) * 512:(nt % 2) * 512 + 512])
                        sev.inc(sf(s, hevac))

                @blk.vector
                def _(v):
                    for nt in range(NT):
                        v.wait_ge(sev.h, b_sev2 + nt + 1)
                        sdv.inc(v.tensor_mul(
                            g_sb[:, nt * 512:nt * 512 + 512],
                            hevac[:, (nt % 2) * 512:(nt % 2) * 512 + 512],
                            routebc[:, nt * 512:nt * 512 + 512]))

        if "route" in taps or "gsb" in taps:
            with nc.Block() as blk:
                @blk.gpsimd
                def _(g):
                    g.wait_ge(sdv.h, sdv.n)
                    if "gsb" in taps:
                        sst.inc(g.dma_start(out=tap_out["gsb"], in_=g_sb[:, :]), 16)

        if max_phase < 6:
            _finalize_dbg()
            p4s.close()
            lp.close()
            return nc, tap_out
        emit_proj_ln(wbT, bb_col, True, [y2_scr[o] for o in range(NDC)],
                     bt_st=bt_st, g_sb=g_sb)
        p4s.close()

        if "y2" in taps:
            with nc.Block() as blk:
                @blk.gpsimd
                def _(g):
                    g.wait_ge(sscw.h, sscw.n)
                    sst.inc(g.dma_start(out=tap_out["y2"], in_=y2_scr), 16)

        if max_phase < 7:
            _finalize_dbg()
            lp.close()
            return nc, tap_out
        with contextlib.ExitStack() as p5:
            y2r = p5.enter_context(nc.sbuf_tensor("y2r", [128, 2 * TOK], BF16))
            outstg = p5.enter_context(nc.sbuf_tensor("outstg", [128, 2 * TOK], F32))
            b_sx2 = [sxp[0].n, sxp[1].n]
            b_st2 = [sstp[0].n, sstp[1].n]
            b_sdv = sdv.n
            with nc.Block() as blk:
                @blk.sync
                def _(sy):
                    sy.wait_ge(sev.h, sev.n)
                    sy.wait_ge(sscw.h, sscw.n)
                    for c in range(NDC):
                        if c >= 2:
                            sy.wait_ge(sdv.h, b_sdv + 2 * (c - 1))
                        ins = sy.dma_start(
                            out=y2r[:, (c % 2) * TOK:(c % 2) * TOK + TOK],
                            in_=y2_scr[c])
                        sxp[c % 2].inc(ins, 16)
                        if c >= 1:
                            sy.wait_ge(sdv.h, b_sdv + 2 * c)
                            ins = sy.dma_start(
                                out=outT[(c - 1) * 128:c * 128, :],
                                in_=outstg[:, ((c - 1) % 2) * TOK:
                                           ((c - 1) % 2) * TOK + TOK])
                            sstp[(c - 1) % 2].inc(ins, 16)
                    sy.wait_ge(sdv.h, b_sdv + 2 * NDC)
                    ins = sy.dma_start(
                        out=outT[(NDC - 1) * 128:NDC * 128, :],
                        in_=outstg[:, ((NDC - 1) % 2) * TOK:
                                   ((NDC - 1) % 2) * TOK + TOK])
                    sstp[(NDC - 1) % 2].inc(ins, 16)
                    sy.wait_ge(sstp[0].h, sstp[0].n)
                    sy.wait_ge(sstp[1].h, sstp[1].n)
                    sy.wait_ge(sst.h, sst.n)

                @blk.vector
                def _(v):
                    for c in range(NDC):
                        v.wait_ge(sxp[c % 2].h, b_sx2[c % 2] + 16 * (c // 2 + 1))
                        if c >= 2:
                            v.wait_ge(sstp[c % 2].h, b_st2[c % 2] + 16 * (c // 2))
                        dst = outstg[:, (c % 2) * TOK:(c % 2) * TOK + TOK]
                        v.tensor_mul(dst, y2r[:, (c % 2) * TOK:(c % 2) * TOK + TOK],
                                     rstd_bc[:, :])
                        sdv.inc(v.tensor_sub(dst, dst, mr_bc[:, :]), 2)

        lp.close()
    return nc, tap_out


def _rope_perm():
    perm = []
    for h in range(NH):
        base = h * HD
        perm += [base + 2 * i for i in range(64)] + [base + 2 * i + 1 for i in range(64)]
    for kh in range(NKV):
        base = NH * HD + kh * HD
        perm += [base + 2 * i for i in range(64)] + [base + 2 * i + 1 for i in range(64)]
    perm += list(range(5120, 6144))
    return np.asarray(perm)


def _core_blocks(c):
    return (0, 3) if c % 2 == 0 else (1, 2)


def make_in_maps(L0_attn_x, L0_attn_w_qkv, L0_attn_b_qkv, L0_attn_w_o, L0_attn_b_o,
                 L0_ffn_W_base, L0_ffn_b_base, L0_ffn_W_router, L0_ffn_A, L0_ffn_B):
    asc = np.ascontiguousarray
    x = np.asarray(L0_attn_x)
    perm = _rope_perm()
    wqkvT = asc(np.asarray(L0_attn_w_qkv)[perm].T)
    bqkv = np.asarray(L0_attn_b_qkv)[perm]
    bq_col = asc(bqkv[:5120].reshape(40, 128).T)
    bv_row = asc(bqkv[5120:].reshape(1, 1024))
    woT = asc(np.asarray(L0_attn_w_o).T)
    bo_col = asc(np.asarray(L0_attn_b_o).reshape(32, 128).T)
    wbT = asc(np.asarray(L0_ffn_W_base).T)
    bb_col = asc(np.asarray(L0_ffn_b_base).reshape(32, 128).T)
    wrT = asc(np.asarray(L0_ffn_W_router).T)
    At = asc(np.asarray(L0_ffn_A).transpose(2, 0, 1).reshape(D, 128))
    Bt = asc(np.asarray(L0_ffn_B).transpose(0, 2, 1).reshape(128, D))

    in_maps = []
    pos_per_core = []
    xT_cache = {}
    for c in range(8):
        b = c // 2
        if b not in xT_cache:
            xT_cache[b] = asc(x[b].T.astype(np.float32))
        blocks = _core_blocks(c)
        pos = np.concatenate([np.arange(1024 * blk, 1024 * blk + 1024)
                              for blk in blocks])
        pos_per_core.append(pos)
        in_maps.append({
            "xT_own": asc(xT_cache[b][:, pos]),
            "xT_full": xT_cache[b],
            "posq": pos.astype(np.float32).reshape(1, TOK),
            "wqkvT": wqkvT, "bq_col_d": bq_col, "bv_row_d": bv_row,
            "woT": woT, "bo_col_d": bo_col,
            "wbT": wbT, "bb_col_d": bb_col,
            "wrT": wrT, "At": At, "Bt": Bt,
        })
    return in_maps, pos_per_core


def kernel(**inputs):
    in_maps, pos_per_core = make_in_maps(**inputs)
    nc, _ = build_program(())
    res = run_bass_kernel_spmd(nc, in_maps, core_ids=list(range(8)))

    out = np.empty((4, S, D), dtype=np.float32)
    for c in range(8):
        b = c // 2
        hv = int(res.results[c]["hsel"][0, 0])
        if hv != c % 2:
            raise RuntimeError(
                f"core {c} ran switch variant {hv}; physical-core mapping "
                f"changed — per-core loop lengths do not match its data")
        out[b, pos_per_core[c], :] = res.results[c]["outT"].T
    return out


if __name__ == "__main__":
    nc, _ = build_program(())
    print("instructions:", nc.n_instructions())


# version 12
# speedup vs baseline: 1.1341x; 1.0127x over previous
import contextlib
import math
import os
DBG_NOSTAT = os.environ.get("DBG_NOSTAT", "0") == "1"
DBG_NOROWS = os.environ.get("DBG_NOROWS", "0") == "1"

import numpy as np
import ml_dtypes
import concourse.bass as bass
import concourse.mybir as mybir
from concourse.bass_utils import run_bass_kernel_spmd

F32 = mybir.dt.float32
BF16 = mybir.dt.bfloat16
I32 = mybir.dt.int32
AF = mybir.ActivationFunctionType
ALU = mybir.AluOpType
AX = mybir.AxisListType

D = 4096
NDC = 32
NH = 32
NKV = 8
HD = 128
S = 4096
TOK = 2048
NT = TOK // 512
C_SHIFT = 12.0
SM_SCALE = 1.0 / math.sqrt(HD)
EPS = 1e-5
LN1E4 = math.log(10000.0)
PI = math.pi
LOOP_LEN = (16, 32)
NSLOT = 2


class Sem:
    def __init__(self, nc, es, name):
        self.h = es.enter_context(nc.semaphore(name))
        self.n = 0

    def inc(self, ins, k=1):
        ins.then_inc(self.h, k)
        self.n += k
        return ins


def build_program(taps=(), max_phase=99):
    nc = bass.Bass("TRN2", target_bir_lowering=False,
                   detect_race_conditions=False)

    xT_own = nc.dram_tensor("xT_own", [D, TOK], F32, kind="ExternalInput").ap()
    xT_full = nc.dram_tensor("xT_full", [D, S], F32, kind="ExternalInput").ap()
    posq = nc.dram_tensor("posq", [1, TOK], F32, kind="ExternalInput").ap()
    wqkvT = nc.dram_tensor("wqkvT", [D, 6144], F32, kind="ExternalInput").ap()
    bq_col_d = nc.dram_tensor("bq_col_d", [128, 40], F32, kind="ExternalInput").ap()
    bv_row_d = nc.dram_tensor("bv_row_d", [1, 1024], F32, kind="ExternalInput").ap()
    woT = nc.dram_tensor("woT", [D, D], F32, kind="ExternalInput").ap()
    bo_col_d = nc.dram_tensor("bo_col_d", [128, 32], F32, kind="ExternalInput").ap()
    wbT = nc.dram_tensor("wbT", [D, D], F32, kind="ExternalInput").ap()
    bb_col_d = nc.dram_tensor("bb_col_d", [128, 32], F32, kind="ExternalInput").ap()
    wrT = nc.dram_tensor("wrT", [D, 8], F32, kind="ExternalInput").ap()
    At_d = nc.dram_tensor("At", [D, 128], F32, kind="ExternalInput").ap()
    Bt_d = nc.dram_tensor("Bt", [128, D], F32, kind="ExternalInput").ap()
    outT = nc.dram_tensor("outT", [D, TOK], F32, kind="ExternalOutput").ap()
    hsel = nc.dram_tensor("hsel", [1, 1], F32, kind="ExternalOutput").ap()

    q_scr = nc.dram_tensor("q_scr", [NH, 128, TOK], BF16, kind="Internal").ap()
    k_scr = nc.dram_tensor("k_scr", [NKV, 128, S], BF16, kind="Internal").ap()
    v_scr = nc.dram_tensor("v_scr", [S // 128, 128, NKV * HD], BF16,
                           kind="Internal").ap()
    attn_scr = nc.dram_tensor("attn_scr", [NH, 128, TOK], BF16, kind="Internal").ap()
    y1_scr = nc.dram_tensor("y1_scr", [NDC, 128, TOK], BF16, kind="Internal").ap()
    y2_scr = nc.dram_tensor("y2_scr", [NDC, 128, TOK], BF16, kind="Internal").ap()

    tap_out = {}
    for name, shape in (
        ("qkv_q", (NH, 128, TOK)), ("qkv_k", (NKV, 128, S)),
        ("qkv_v", (S // 128, 128, NKV * HD)),
        ("attn", (128, NDC * TOK)), ("y1", (NDC, 128, TOK)),
        ("y1n", (128, NDC * TOK)), ("y2", (NDC, 128, TOK)),
        ("lnr1", (1, 2 * TOK)), ("route", (8, TOK)),
        ("gsb", (128, TOK)), ("trig", (128, 3 * TOK)), ("p0c", (128, 8)),
        ("trigq", (128, 3 * TOK)),
    ):
        if name in taps:
            tap_out[name] = nc.dram_tensor("tap_" + name, list(shape), F32,
                                           kind="ExternalOutput").ap()

    _uid = [0]

    def _nm(base):
        _uid[0] += 1
        return f"{base}_{_uid[0]}"

    with contextlib.ExitStack() as es:
        iota_p = es.enter_context(nc.sbuf_tensor("iota_p", [128, 1], F32))
        ident_b = es.enter_context(nc.sbuf_tensor("ident_b", [128, 128], BF16))
        ident_f8 = es.enter_context(nc.sbuf_tensor("ident_f8", [8, 8], F32))
        ones_r1 = es.enter_context(nc.sbuf_tensor("ones_r1", [1, 128], BF16))
        ones_r1f = es.enter_context(nc.sbuf_tensor("ones_r1f", [1, 128], F32))
        ones_c1 = es.enter_context(nc.sbuf_tensor("ones_c1", [128, 1], BF16))
        ones_c1f = es.enter_context(nc.sbuf_tensor("ones_c1f", [128, 1], F32))
        negc = es.enter_context(nc.sbuf_tensor("negc", [128, 1], F32))
        invf = es.enter_context(nc.sbuf_tensor("invf", [128, 1], F32))
        epsb = es.enter_context(nc.sbuf_tensor("epsb", [1, 1], F32))
        bq_col = es.enter_context(nc.sbuf_tensor("bq_col", [128, 40], F32))
        bo_col = es.enter_context(nc.sbuf_tensor("bo_col", [128, 32], F32))
        bb_col = es.enter_context(nc.sbuf_tensor("bb_col", [128, 32], F32))
        bv_row = es.enter_context(nc.sbuf_tensor("bv_row", [1, 1024], BF16))
        ebc = es.enter_context(nc.sbuf_tensor("ebc", [8, 128], BF16))
        fslot = es.enter_context(nc.sbuf_tensor("fslot", [1, 8], F32))
        fcol = es.enter_context(nc.sbuf_tensor("fcol", [128, 1], F32))
        lnb = es.enter_context(nc.sbuf_tensor("lnb", [128, 1], F32))
        scol = es.enter_context(nc.sbuf_tensor("scol", [128, 1], F32))

        sx = Sem(nc, es, "sx")
        sw = Sem(nc, es, "sw")
        swc = Sem(nc, es, "swc")
        sps = Sem(nc, es, "sps")
        spst = Sem(nc, es, "spst")
        sev = Sem(nc, es, "sev")
        sdv = Sem(nc, es, "sdv")
        sscw = Sem(nc, es, "sscw")
        sq2 = Sem(nc, es, "sq2")
        skv = Sem(nc, es, "skv")
        ssc = Sem(nc, es, "ssc")
        sex = Sem(nc, es, "sex")
        smk = Sem(nc, es, "smk")
        sav = Sem(nc, es, "sav")
        sfin = Sem(nc, es, "sfin")
        sst = Sem(nc, es, "sst")
        sini = Sem(nc, es, "sini")
        swp = [Sem(nc, es, "swpa"), Sem(nc, es, "swpb")]
        sxp = [Sem(nc, es, "sxpa"), Sem(nc, es, "sxpb")]
        scp = [Sem(nc, es, "scpa"), Sem(nc, es, "scpb")]
        sstp = [Sem(nc, es, "sstpa"), Sem(nc, es, "sstpb")]
        sad = Sem(nc, es, "sad")
        sae = Sem(nc, es, "sae")
        srm = Sem(nc, es, "srm")
        sat = Sem(nc, es, "sat")
        shs = Sem(nc, es, "shs")
        srr = Sem(nc, es, "srr")
        sbe = Sem(nc, es, "sbe")
        spl = Sem(nc, es, "spl")
        smg = Sem(nc, es, "smg")

        ep = contextlib.ExitStack()
        posq_bc = ep.enter_context(nc.sbuf_tensor("posq_bc", [128, TOK], F32))
        ep1 = contextlib.ExitStack()
        big_a = ep1.enter_context(nc.sbuf_tensor("xbuf", [128, NDC * TOK], BF16))

        def vf(v, ap):
            return v.tensor_copy(fslot[0:1, 0:1], ap[0:1, 0:1])

        def sf(s_, ap):
            return s_.copy(fslot[0:1, 2:3], ap[0:1, 0:1])

        def gf(g_, ap):
            return g_.tensor_copy(fcol[:, 0:1], ap[:, 0:1])

        def _finalize_dbg():
            with nc.Block() as blk:
                @blk.sync
                def _(sy):
                    sy.wait_ge(sscw.h, sscw.n)
                    sy.wait_ge(sst.h, sst.n)
                    sy.wait_ge(sdv.h, sdv.n)

        with contextlib.ExitStack() as p0:
            t0 = p0.enter_context(nc.sbuf_tensor("t0", [128, 128], F32))
            t1 = p0.enter_context(nc.sbuf_tensor("t1", [128, 128], F32))
            iota128 = p0.enter_context(nc.sbuf_tensor("iota128", [128, 128], F32))
            posq_row = p0.enter_context(nc.sbuf_tensor("posq_row", [1, TOK], F32))
            identf_full = p0.enter_context(nc.sbuf_tensor("identf_full",
                                                          [128, 128], F32))
            pq_ps = p0.enter_context(nc.psum_tensor("pq_ps", [128, TOK], F32))

            with nc.Block() as blk:
                @blk.gpsimd
                def _(g):
                    for dst, src in ((posq_row[:, :], posq), (bq_col[:, :], bq_col_d),
                                     (bo_col[:, :], bo_col_d), (bb_col[:, :], bb_col_d),
                                     (bv_row[:, :], bv_row_d)):
                        sx.inc(g.dma_start(out=dst, in_=src), 16)
                    g.iota(iota_p[:, :], [[0, 1]], channel_multiplier=1,
                           allow_small_or_imprecise_dtypes=True)
                    g.iota(iota128[:, :], [[1, 128]], channel_multiplier=0,
                           allow_small_or_imprecise_dtypes=True)
                    g.memset(ones_r1[:, :], 1.0)
                    g.memset(ones_r1f[:, :], 1.0)
                    g.memset(ones_c1[:, :], 1.0)
                    g.memset(ones_c1f[:, :], 1.0)
                    g.memset(epsb[:, :], EPS)
                    g.memset(lnb[:, :], LN1E4)
                    sini.inc(g.memset(negc[:, :], -C_SHIFT))

                @blk.vector
                def _(v):
                    def spacer():
                        v.tensor_copy(scol[:, 0:1], iota_p[:, 0:1])
                    v.wait_ge(sini.h, 1)
                    v.tensor_scalar(identf_full[:, :], iota128[:, :], iota_p[:, 0:1],
                                    None, ALU.is_equal)
                    v.tensor_scalar(t1[0:8, :], iota128[0:8, :], 1.0 / 16.0,
                                    iota_p[0:8, 0:1], ALU.mult, ALU.subtract)
                    v.tensor_copy(ident_b[:, :], identf_full[:, :])
                    v.tensor_scalar(t0[0:8, :], t1[0:8, :], 0.0, None, ALU.is_ge)
                    v.tensor_copy(ident_f8[:, :], identf_full[0:8, 0:8])
                    v.tensor_scalar(t1[0:8, :], t1[0:8, :], 0.999, None, ALU.is_lt)
                    spacer()
                    v.tensor_mul(ebc[0:8, :], t0[0:8, :], t1[0:8, :])
                    sdv.inc(vf(v, ebc))

                b_sdv0 = sdv.n

                @blk.scalar
                def _(s):
                    s.wait_ge(sini.h, 1)
                    s.activation(invf[0:64, :], iota_p[0:64, 0:1], AF.Exp,
                                 bias=0.0, scale=-LN1E4 / 64.0)
                    s.activation(invf[64:128, :], iota_p[64:128, 0:1], AF.Exp,
                                 bias=lnb[64:128, 0:1], scale=-LN1E4 / 64.0)
                    sini.inc(sf(s, invf))

                @blk.tensor
                def _(t):
                    t.wait_ge(sx.h, 80)
                    ins = None
                    for i in range(NT):
                        ins = t.matmul(pq_ps[:, i * 512:(i + 1) * 512], ones_r1f[:, :],
                                       posq_row[:, i * 512:(i + 1) * 512],
                                       start=True, stop=True)
                    sps.inc(ins)

                b_sps0 = sps.n

                @blk.scalar
                def _(s):
                    s.wait_ge(sps.h, b_sps0)
                    s.copy(posq_bc[:, :], pq_ps[:, :])
                    sini.inc(sf(s, posq_bc))

        if "p0c" in taps:
            with nc.Block() as blk:
                @blk.gpsimd
                def _(g):
                    g.wait_ge(sini.h, sini.n)
                    with nc.allow_non_contiguous_dma(reason="debug"):
                        sst.inc(g.dma_start(out=tap_out["p0c"][:, 0:1],
                                            in_=iota_p[:, :],
                                            allow_slow_non_contiguous=True), 16)
                        sst.inc(g.dma_start(out=tap_out["p0c"][:, 4:5],
                                            in_=invf[:, :],
                                            allow_slow_non_contiguous=True), 16)

        cs_scope = contextlib.ExitStack()
        cos_t = cs_scope.enter_context(nc.sbuf_tensor("cos_t", [128, TOK], BF16))
        sin_t = cs_scope.enter_context(nc.sbuf_tensor("sin_t", [128, TOK], BF16))

        def emit_trig(pass_i):
            with contextlib.ExitStack() as ts_:
                ang = ts_.enter_context(nc.sbuf_tensor(_nm("ang"), [128, TOK], F32))
                u = ts_.enter_context(nc.sbuf_tensor(_nm("u"), [128, TOK], F32))
                ki = ts_.enter_context(nc.sbuf_tensor(_nm("ki"), [128, TOK], I32))
                kf = ts_.enter_context(nc.sbuf_tensor(_nm("kf"), [128, TOK], F32))
                b_sev = sev.n
                with nc.Block() as blk:
                    if pass_i < 2:
                        @blk.gpsimd
                        def _(g):
                            g.wait_ge(sini.h, sini.n)
                            sini.inc(g.iota(ang[:, :], [[1, TOK]],
                                            base=pass_i * TOK,
                                            channel_multiplier=0,
                                            allow_small_or_imprecise_dtypes=True))

                    @blk.vector
                    def _(v):
                        v.wait_ge(sini.h, sini.n)
                        src = ang if pass_i < 2 else posq_bc
                        v.tensor_scalar(ang[:, :], src[:, :], invf[:, 0:1],
                                        None, ALU.mult)
                        for which in (0, 1):
                            if which == 0:
                                v.tensor_copy(u[:, :], ang[:, :])
                            else:
                                v.wait_ge(sev.h, b_sev + 1)
                                v.tensor_scalar(u[:, :], ang[:, :], PI / 2.0,
                                                None, ALU.add)
                            v.tensor_scalar(ki[:, :], u[:, :], 1.0 / (2 * PI),
                                            None, ALU.mult)
                            v.tensor_copy(kf[:, :], ki[:, :])
                            v.scalar_tensor_tensor(u[:, :], kf[:, :], -2 * PI,
                                                   u[:, :], ALU.mult, ALU.add)
                            v.tensor_scalar(kf[:, :], u[:, :], PI, None, ALU.is_ge)
                            v.scalar_tensor_tensor(u[:, :], kf[:, :], -2 * PI,
                                                   u[:, :], ALU.mult, ALU.add)
                            v.tensor_scalar(kf[:, :], u[:, :], -PI, None, ALU.is_lt)
                            v.scalar_tensor_tensor(
                                u[:, :], kf[:, :], 2 * PI, u[:, :],
                                ALU.mult, ALU.add)
                            sdv.inc(vf(v, u))

                    b_sdv = sdv.n - 2

                    @blk.scalar
                    def _(s):
                        s.wait_ge(sdv.h, b_sdv + 1)
                        s.activation(sin_t[:, :], u[:, :], AF.Sin,
                                     bias=0.0, scale=1.0)
                        sev.inc(sf(s, sin_t))
                        s.wait_ge(sdv.h, b_sdv + 2)
                        s.activation(cos_t[:, :], u[:, :], AF.Sin,
                                     bias=0.0, scale=1.0)
                        sev.inc(sf(s, cos_t))

        def emit_load_x(src_ap, wait_sev_to):
            with nc.Block() as blk:
                @blk.gpsimd
                def _(g):
                    g.wait_ge(sev.h, wait_sev_to)
                    for i in range(8):
                        ins = g.dma_start(
                            out=big_a[:, :].rearrange("p (c t) -> p c t", c=NDC)
                                [:, 4 * i:4 * (i + 1), :],
                            in_=src_ap[i * 512:(i + 1) * 512, :]
                                .rearrange("(c p) t -> p c t", p=128))
                        sx.inc(ins, 16)

        def emit_qk_loop(jlist, dsts, dst_off, x_src=None, x_wait_sev=0):
            b_sx0 = sx.n
            with contextlib.ExitStack() as ls:
                stage_w = ls.enter_context(
                    nc.sbuf_tensor(_nm("stage_w1"), [128, 2 * NDC * 128], BF16))
                evac = ls.enter_context(nc.sbuf_tensor(_nm("evac1"), [128, 2 * 512], F32))
                ropa = ls.enter_context(nc.sbuf_tensor(_nm("ropa"), [128, 512], F32))
                ropb = ls.enter_context(nc.sbuf_tensor(_nm("ropb"), [128, 512], F32))
                kq_st = ls.enter_context(nc.sbuf_tensor(_nm("kq_st"), [128, 2 * TOK], BF16))
                ps = ls.enter_context(nc.psum_tensor(_nm("p1_ps"), [128, 2 * 512], F32))
                nJ = len(jlist)
                b_sw = [swp[0].n, swp[1].n]
                b_sc = [scp[0].n, scp[1].n]
                b_sps, b_sev, b_sdv = sps.n, sev.n, sdv.n

                with nc.Block() as blk:
                    @blk.gpsimd
                    def _(g):
                        for idx, (jc, _b) in enumerate(jlist):
                            if idx == 2 and x_src is not None:
                                g.wait_ge(sev.h, x_wait_sev)
                                for i in range(8):
                                    ins = g.dma_start(
                                        out=big_a[:, :]
                                        .rearrange("p (c t) -> p c t", c=NDC)
                                        [:, 4 * i:4 * (i + 1), :],
                                        in_=x_src[i * 512:(i + 1) * 512, :]
                                        .rearrange("(c p) t -> p c t", p=128))
                                    sx.inc(ins, 16)
                            if idx >= 2:
                                g.wait_ge(sps.h, b_sps + NT * (idx - 1))
                            ins = g.dma_start(
                                out=stage_w[:, :]
                                    .rearrange("p (b c j) -> p b c j", b=2, c=NDC)
                                    [:, idx % 2, :, :],
                                in_=wqkvT[:, jc * 128:(jc + 1) * 128]
                                    .rearrange("(c p) j -> p c j", p=128))
                            swp[idx % 2].inc(ins, 16)

                    @blk.tensor
                    def _(t):
                        for idx in range(nJ):
                            t.wait_ge(swp[idx % 2].h,
                                      b_sw[idx % 2] + 16 * (idx // 2 + 1))
                            for nt in range(NT):
                                gi = idx * NT + nt
                                if gi >= 2:
                                    t.wait_ge(sev.h, b_sev + gi - 1)
                                ins = None
                                for c in range(NDC):
                                    if idx == 0 and nt == 0:
                                        t.wait_ge(sx.h,
                                                  b_sx0 + 16 * (c // 4 + 1))
                                    ins = t.matmul(
                                        ps[:, (gi % 2) * 512:(gi % 2) * 512 + 512],
                                        stage_w[:, (idx % 2) * NDC * 128 + c * 128:
                                                (idx % 2) * NDC * 128 + c * 128 + 128],
                                        big_a[:, c * TOK + nt * 512:
                                              c * TOK + nt * 512 + 512],
                                        start=(c == 0), stop=(c == NDC - 1))
                                sps.inc(ins)

                    @blk.scalar
                    def _(s):
                        for idx, (jc, bcol) in enumerate(jlist):
                            for nt in range(NT):
                                gi = idx * NT + nt
                                s.wait_ge(sps.h, b_sps + gi + 1)
                                if gi >= 2:
                                    s.wait_ge(sdv.h, b_sdv + gi - 1)
                                sev.inc(s.activation(
                                    evac[:, (gi % 2) * 512:(gi % 2) * 512 + 512],
                                    ps[:, (gi % 2) * 512:(gi % 2) * 512 + 512],
                                    AF.Identity, bias=bq_col[:, bcol:bcol + 1],
                                    scale=1.0))

                    @blk.vector
                    def _(v):
                        for idx in range(nJ):
                            for nt in range(NT):
                                gi = idx * NT + nt
                                v.wait_ge(sev.h, b_sev + gi + 1)
                                if idx >= 2 and nt == 0:
                                    v.wait_ge(scp[idx % 2].h,
                                              b_sc[idx % 2] + 16 * (idx // 2))
                                ev = evac[:, (gi % 2) * 512:(gi % 2) * 512 + 512]
                                co = cos_t[:, nt * 512:nt * 512 + 512]
                                si = sin_t[:, nt * 512:nt * 512 + 512]
                                ks = kq_st[:, (idx % 2) * TOK + nt * 512:
                                           (idx % 2) * TOK + nt * 512 + 512]
                                v.tensor_mul(ropa[:, :], ev, co)
                                v.tensor_mul(ropb[0:64, :], ev[64:128, :],
                                             si[64:128, :])
                                v.tensor_mul(ropb[64:128, :], ev[0:64, :],
                                             si[0:64, :])
                                v.tensor_sub(ks[0:64, :], ropa[0:64, :],
                                             ropb[0:64, :])
                                sdv.inc(v.tensor_add(ks[64:128, :],
                                                     ropb[64:128, :],
                                                     ropa[64:128, :]))

                    @blk.sync
                    def _(sy):
                        for idx in range(nJ):
                            sy.wait_ge(sdv.h, b_sdv + NT * (idx + 1))
                            ins = sy.dma_start(
                                out=dsts[idx][:, dst_off:dst_off + TOK],
                                in_=kq_st[:, (idx % 2) * TOK:(idx % 2) * TOK + TOK])
                            scp[idx % 2].inc(ins, 16)

        def emit_v_loop(pass_i):
            b_sx = sx.n
            with contextlib.ExitStack() as ls:
                stage_wv = ls.enter_context(
                    nc.sbuf_tensor(_nm("stage_wv"), [128, NDC * 512], BF16))
                v_st = ls.enter_context(nc.sbuf_tensor(_nm("v_st"), [128, 8 * 512], BF16))
                ps = ls.enter_context(nc.psum_tensor(_nm("pv_ps"), [128, 2 * 512], F32))
                b_sw, b_sps, b_sev, b_sscw, b_swc = sw.n, sps.n, sev.n, sscw.n, swc.n

                with nc.Block() as blk:
                    @blk.gpsimd
                    def _(g):
                        g.wait_ge(sx.h, b_sx)
                        for jb in range(2):
                            if jb >= 1:
                                g.wait_ge(sps.h, b_sps + 16 * jb)
                            ins = g.dma_start(
                                out=stage_wv[:, :]
                                    .rearrange("p (c j) -> p c j", c=NDC),
                                in_=wqkvT[:, 5120 + jb * 512:5120 + jb * 512 + 512]
                                    .rearrange("(c p) j -> p c j", p=128))
                            sw.inc(ins, 16)

                    @blk.tensor
                    def _(t):
                        t.wait_ge(sx.h, b_sx)
                        for jb in range(2):
                            t.wait_ge(sw.h, b_sw + 16 * (jb + 1))
                            for tt in range(16):
                                gi = jb * 16 + tt
                                if gi >= 2:
                                    t.wait_ge(sev.h, b_sev + gi - 1)
                                for c in range(NDC):
                                    t.matmul(
                                        ps[:, (gi % 2) * 512:(gi % 2) * 512 + 512],
                                        big_a[:, c * TOK + tt * 128:
                                              c * TOK + tt * 128 + 128],
                                        stage_wv[:, c * 512:c * 512 + 512],
                                        start=(c == 0), stop=False)
                                ins = t.matmul(
                                    ps[:, (gi % 2) * 512:(gi % 2) * 512 + 512],
                                    ones_r1[:, :],
                                    bv_row[0:1, jb * 512:jb * 512 + 512],
                                    start=False, stop=True)
                                sps.inc(ins)

                    @blk.scalar
                    def _(s):
                        for jb in range(2):
                            for tt in range(16):
                                gi = jb * 16 + tt
                                s.wait_ge(sps.h, b_sps + gi + 1)
                                half_i = gi // 8
                                if half_i >= 1 and gi % 8 == 0:
                                    s.wait_ge(sscw.h, b_sscw + 16 * half_i)
                                sev.inc(s.copy(
                                    v_st[:, (tt % 8) * 512:(tt % 8) * 512 + 512],
                                    ps[:, (gi % 2) * 512:(gi % 2) * 512 + 512]))

                    @blk.sync
                    def _(sy):
                        for jb in range(2):
                            for h8 in range(2):
                                sy.wait_ge(sev.h, b_sev + jb * 16 + h8 * 8 + 8)
                                t0g = pass_i * 16 + h8 * 8
                                ins = sy.dma_start(
                                    out=v_scr[t0g:t0g + 8, :, jb * 512:jb * 512 + 512]
                                        .rearrange("t p j -> p t j"),
                                    in_=v_st[:, :].rearrange("p (t j) -> p t j", t=8))
                                sscw.inc(ins, 16)

        for pass_i in range(2):
            emit_trig(pass_i)
            if pass_i == 0 and "trig" in taps:
                with nc.Block() as blk:
                    @blk.gpsimd
                    def _(g):
                        g.wait_ge(sev.h, sev.n)
                        sst.inc(g.dma_start(out=tap_out["trig"][:, 0:TOK],
                                            in_=sin_t[:, :]), 16)
                        sst.inc(g.dma_start(out=tap_out["trig"][:, TOK:2 * TOK],
                                            in_=cos_t[:, :]), 16)
                        with nc.allow_non_contiguous_dma(reason="debug tap"):
                            sst.inc(g.dma_start(
                                out=tap_out["trig"][:, 2 * TOK:2 * TOK + 1],
                                in_=invf[:, :],
                                allow_slow_non_contiguous=True), 16)
            emit_qk_loop([(32 + jk, 32 + jk) for jk in range(NKV)],
                         [k_scr[jk] for jk in range(NKV)], pass_i * TOK,
                         x_src=xT_full[:, pass_i * TOK:(pass_i + 1) * TOK],
                         x_wait_sev=sev.n)
            emit_v_loop(pass_i)
        emit_trig(2)
        if "trigq" in taps:
            with nc.Block() as blk:
                @blk.gpsimd
                def _(g):
                    g.wait_ge(sev.h, sev.n)
                    sst.inc(g.dma_start(out=tap_out["trigq"][:, 0:TOK],
                                        in_=sin_t[:, :]), 16)
                    sst.inc(g.dma_start(out=tap_out["trigq"][:, TOK:2 * TOK],
                                        in_=cos_t[:, :]), 16)
                    sst.inc(g.dma_start(out=tap_out["trigq"][:, 2 * TOK:3 * TOK],
                                        in_=posq_bc[:, :]), 16)
        emit_qk_loop([(jq, jq) for jq in range(NH)],
                     [q_scr[jq] for jq in range(NH)], 0,
                     x_src=xT_own, x_wait_sev=sev.n)

        cs_scope.close()

        if "qkv_q" in taps or "qkv_k" in taps or "qkv_v" in taps:
            with nc.Block() as blk:
                @blk.gpsimd
                def _(g):
                    g.wait_ge(sscw.h, sscw.n)
                    for nm, src in (("qkv_q", q_scr), ("qkv_k", k_scr),
                                    ("qkv_v", v_scr)):
                        if nm in taps:
                            sst.inc(g.dma_start(out=tap_out[nm], in_=src), 16)

        if max_phase < 2:
            _finalize_dbg()
            ep.close()
            return nc, tap_out
        ep1.close()
        with contextlib.ExitStack() as p2:
            masks = p2.enter_context(nc.sbuf_tensor("masks", [128, 8 * 1024], BF16))
            k_g = p2.enter_context(nc.sbuf_tensor("k_g", [128, 2 * S], BF16))
            v_g = p2.enter_context(nc.sbuf_tensor("v_g", [128, 2 * S], BF16))
            NEXP = 8
            expb4 = [p2.enter_context(nc.sbuf_tensor(f"expb{j}", [128, 2 * 1024],
                                                     BF16)) for j in range(4)]

            def ebsl(vi, hf=None):
                o = ((vi // 4) % 2) * 1024
                if hf is None:
                    return expb4[vi % 4][:, o:o + 1024]
                return expb4[vi % 4][:, o + hf * 512:o + hf * 512 + 512]
            acc2 = [p2.enter_context(nc.sbuf_tensor(f"acc{i}", [128, 1024], F32))
                    for i in range(3)]
            accb2 = [p2.enter_context(nc.sbuf_tensor(f"accb{i}", [128, 1024], F32))
                     for i in range(2)]
            rrow = p2.enter_context(nc.sbuf_tensor("rrow_t", [1, 1024], F32))
            av_sb = p2.enter_context(nc.sbuf_tensor("av_sb", [128, 2 * 1024], BF16))
            att_st = p2.enter_context(nc.sbuf_tensor("att_st", [128, 2 * 1024], BF16))
            bc_sb = p2.enter_context(nc.sbuf_tensor("bc_sb", [128, 2 * 1024], F32))
            q_t = p2.enter_context(nc.sbuf_tensor("q_t2", [128, 2 * 1024], BF16))
            ps_sc = p2.enter_context(nc.psum_tensor("ps_sc", [128, 3 * 1024], F32))
            ps_av = p2.enter_context(nc.psum_tensor("ps_av", [128, 1024], F32))

            hv = p2.enter_context(nc.sbuf_tensor("hv", [1, 1], F32))

            def emit_p2(LOOP_LEN, varid):
                heads = [(sl, g, qh) for sl in range(NSLOT) for g in range(NKV)
                         for qh in range(4)]
                vstart = {}
                _vv = 0
                for h_ in heads:
                    vstart[h_] = _vv
                    _vv += LOOP_LEN[h_[0]]
                NVIS = _vv
                vhead = [0] * NVIS
                vkk = [0] * NVIS
                for hidx, h_ in enumerate(heads):
                    for kk in range(LOOP_LEN[h_[0]]):
                        vhead[vstart[h_] + kk] = hidx
                        vkk[vstart[h_] + kk] = kk
                K0 = 6
                vslot = [(k % 2) if k < K0 else [2, 0, 1][(k - K0) % 3] for k in vkk]
                vmask = [False] * NVIS
                vpar = [0] * NVIS
                mcnt = [0] * NVIS
                acnt = [[0] * NVIS, [0] * NVIS]
                firstpar = {}
                mc = 0
                ac = [0, 0]
                for vi in range(NVIS):
                    hidx = vhead[vi]
                    sl = heads[hidx][0]
                    L = LOOP_LEN[sl]
                    kk = vkk[vi]
                    vmask[vi] = (kk - (L - 8)) >= 0
                    if sl == 0:
                        par = 0 if (kk % 16) in (0, 2, 4, 6, 9, 11, 13) else 1
                    else:
                        par = 0 if (kk % 8) in (0, 2, 4, 6, 7) else 1
                    vpar[vi] = par
                    if (hidx, par) not in firstpar:
                        firstpar[(hidx, par)] = vi
                    if vmask[vi]:
                        mc += 1
                    ac[par] += 1
                    mcnt[vi] = mc
                    acnt[0][vi] = ac[0]
                    acnt[1][vi] = ac[1]

                b_ssc, b_sex, b_smk, b_sav = ssc.n, sex.n, smk.n, sav.n
                b_sfin, b_sev = sfin.n, sev.n
                b_sq2, b_skv = sq2.n, skv.n
                b_sad, b_spl, b_smg, b_sae = sad.n, spl.n, smg.n, sae.n
                b_srm, b_sdv = srm.n, sdv.n
                b_srr, b_sbe = srr.n, sbe.n
                b_scp = [scp[0].n, scp[1].n]

                def wait_exp_consumed(eng, vi):
                    pv = vi - NEXP
                    if pv < 0:
                        return
                    eng.wait_ge(sav.h, b_sav + pv + 1)
                    if vpar[pv] == 0:
                        eng.wait_ge(sad.h, b_sad + acnt[0][pv])
                    else:
                        eng.wait_ge(spl.h, b_spl + acnt[1][pv])

                def wait_masked_exp(eng, vi):
                    if vmask[vi]:
                        eng.wait_ge(smk.h, b_smk + mcnt[vi])
                    else:
                        eng.wait_ge(sex.h, b_sex + vi + 1)

                pe_slot_user = {}

                def pe_score_wait(t, slot):
                    u = pe_slot_user.get(slot)
                    if u is None:
                        return
                    if u[0] == "exp":
                        t.wait_ge(sex.h, b_sex + u[1] + 1)
                    else:
                        assert u[0] == "fin"
                        t.wait_ge(sbe.h, b_sbe + u[1] + 1)

                def pe_av(t, pv):
                    ph = vhead[pv]
                    psl = heads[ph][0]
                    ggi = psl * NKV + heads[ph][1]
                    kkp = vkk[pv]
                    Lp = LOOP_LEN[psl]
                    wait_masked_exp(t, pv)
                    if kkp == 0 and ph >= 1:
                        t.wait_ge(sae.h, b_sae + ph)
                    ins = None
                    for hf in range(2):
                        ins = t.matmul(
                            ps_av[:, hf * 512:hf * 512 + 512],
                            v_g[:, (ggi % 2) * S + kkp * 128:
                                (ggi % 2) * S + kkp * 128 + 128],
                            ebsl(pv, hf),
                            start=(kkp == 0), stop=(kkp == Lp - 1),
                            skip_group_check=True)
                    sav.inc(ins)

                def pe_fin_den(t, h2):
                    t.wait_ge(smg.h, b_smg + h2 + 1)
                    pe_score_wait(t, 2)
                    ins = None
                    for hf in range(2):
                        ins = t.matmul(
                            ps_sc[0:1, 2048 + hf * 512:2048 + hf * 512 + 512],
                            ones_c1f[:, :],
                            acc2[h2 % 3][:, hf * 512:hf * 512 + 512],
                            start=True, stop=True, skip_group_check=True)
                    sfin.inc(ins)

                def pe_fin_bc(t, h2):
                    t.wait_ge(srr.h, b_srr + h2 + 1)
                    ins = None
                    for hf in range(2):
                        ins = t.matmul(
                            ps_sc[:, 2048 + hf * 512:2048 + hf * 512 + 512],
                            ones_r1f[:, :],
                            rrow[0:1, hf * 512:hf * 512 + 512],
                            start=True, stop=True, skip_group_check=True)
                    sfin.inc(ins)
                    pe_slot_user[2] = ("fin", h2)

                def act_avevac(s, h2):
                    s.wait_ge(sav.h, b_sav + vstart[heads[h2]]
                              + LOOP_LEN[heads[h2][0]])
                    if h2 >= 2:
                        s.wait_ge(srm.h, b_srm + h2 - 1)
                    s.copy(av_sb[:, (h2 % 2) * 1024:(h2 % 2) * 1024 + 1024],
                           ps_av[:, :])
                    sae.inc(sf(s, av_sb))

                def act_denrow(s, h2):
                    s.wait_ge(sfin.h, b_sfin + 2 * h2 + 1)
                    s.copy(rrow[0:1, :], ps_sc[0:1, 2048:3072])
                    sev.inc(sf(s, rrow))

                def act_bcevac(s, h2):
                    s.wait_ge(sfin.h, b_sfin + 2 * h2 + 2)
                    if h2 >= 2:
                        s.wait_ge(srm.h, b_srm + h2 - 1)
                    s.copy(bc_sb[:, (h2 % 2) * 1024:(h2 % 2) * 1024 + 1024],
                           ps_sc[:, 2048:3072])
                    sbe.inc(sf(s, bc_sb))

                def dve_merge(v, h2):
                    L2 = LOOP_LEN[heads[h2][0]]
                    vs2 = vstart[heads[h2]]
                    v.wait_ge(spl.h, b_spl + acnt[1][vs2 + L2 - 1])
                    smg.inc(v.tensor_add(acc2[h2 % 3][:, :], acc2[h2 % 3][:, :],
                                         accb2[h2 % 2][:, :]))

                def dve_recip_row(v, h2):
                    v.wait_ge(sev.h, b_sev + h2 + 1)
                    srr.inc(v.reciprocal(rrow[0:1, :], rrow[0:1, :]))

                def dve_mul(v, h2):
                    v.wait_ge(sbe.h, b_sbe + h2 + 1)
                    v.wait_ge(sae.h, b_sae + h2 + 1)
                    if h2 >= 2:
                        v.wait_ge(scp[h2 % 2].h, b_scp[h2 % 2] + 16 * (h2 // 2))
                    srm.inc(v.tensor_mul(
                        att_st[:, (h2 % 2) * 1024:(h2 % 2) * 1024 + 1024],
                        av_sb[:, (h2 % 2) * 1024:(h2 % 2) * 1024 + 1024],
                        bc_sb[:, (h2 % 2) * 1024:(h2 % 2) * 1024 + 1024]))

                def st_store(sy, ph):
                    psl2, pg2, pqh2 = heads[ph]
                    phead = pg2 * 4 + pqh2
                    sy.wait_ge(srm.h, b_srm + ph + 1)
                    ins = sy.dma_start(
                        out=attn_scr[phead][:, psl2 * 1024:psl2 * 1024 + 1024],
                        in_=att_st[:, (ph % 2) * 1024:(ph % 2) * 1024 + 1024])
                    scp[ph % 2].inc(ins, 16)

                for sl in range(NSLOT):
                    L = LOOP_LEN[sl]
                    with nc.Block() as blk:
                        @blk.gpsimd
                        def _(g_, sl=sl, L=L):
                            for g in range(NKV):
                                for qh in range(4):
                                    hidx = sl * 32 + g * 4 + qh
                                    vs = vstart[(sl, g, qh)]
                                    for kk in range(L):
                                        vi = vs + kk
                                        if vpar[vi] != 1:
                                            continue
                                        wait_masked_exp(g_, vi)
                                        eb = ebsl(vi)
                                        if vi == firstpar[(hidx, 1)]:
                                            if hidx >= 2:
                                                g_.wait_ge(smg.h, b_smg + hidx - 1)
                                            spl.inc(g_.tensor_copy(
                                                accb2[hidx % 2][:, :], eb))
                                        else:
                                            spl.inc(g_.tensor_tensor(
                                                accb2[hidx % 2][:, :],
                                                accb2[hidx % 2][:, :], eb, ALU.add))

                        @blk.sync
                        def _(sy, sl=sl, L=L):
                            for g in range(NKV):
                                for qh in range(4):
                                    hidx = sl * 32 + g * 4 + qh
                                    head = g * 4 + qh
                                    if hidx >= 2:
                                        prev = heads[hidx - 2]
                                        sy.wait_ge(ssc.h, b_ssc + vstart[prev]
                                                   + LOOP_LEN[prev[0]])
                                    ins = sy.dma_start(
                                        out=q_t[:, (hidx % 2) * 1024:
                                                (hidx % 2) * 1024 + 1024],
                                        in_=q_scr[head][:, sl * 1024:sl * 1024 + 1024])
                                    sq2.inc(ins, 16)
                                    if hidx >= 2:
                                        st_store(sy, hidx - 2)
                            if sl == NSLOT - 1:
                                st_store(sy, 62)
                                st_store(sy, 63)

                        @blk.tensor
                        def _(t, sl=sl, L=L):
                            t.wait_ge(sdv.h, b_sdv + sl + 1)
                            for g in range(NKV):
                                gidx = sl * NKV + g
                                for qh in range(4):
                                    hidx = sl * 32 + g * 4 + qh
                                    vs = vstart[(sl, g, qh)]
                                    for kk in range(L):
                                        vi = vs + kk
                                        slot = vslot[vi]
                                        if kk == 0:
                                            t.wait_ge(skv.h, b_skv + 32 * (gidx + 1))
                                            t.wait_ge(sq2.h, b_sq2 + 16 * (hidx + 1))
                                        pe_score_wait(t, slot)
                                        ins = None
                                        for hf in range(2):
                                            ins = t.matmul(
                                                ps_sc[:, slot * 1024 + hf * 512:
                                                      slot * 1024 + hf * 512 + 512],
                                                k_g[:, (gidx % 2) * S + kk * 128:
                                                    (gidx % 2) * S + kk * 128 + 128],
                                                q_t[:, (hidx % 2) * 1024 + hf * 512:
                                                    (hidx % 2) * 1024 + hf * 512 + 512],
                                                start=True, stop=True,
                                                skip_group_check=True)
                                        ssc.inc(ins)
                                        pe_slot_user[slot] = ("exp", vi)
                                        if vi >= 2:
                                            pe_av(t, vi - 2)
                                        if hidx >= 2:
                                            if kk == 0:
                                                pe_fin_den(t, hidx - 2)
                                            elif kk == 3:
                                                pe_fin_bc(t, hidx - 2)
                            if sl == NSLOT - 1:
                                pe_av(t, NVIS - 2)
                                pe_av(t, NVIS - 1)
                                pe_fin_den(t, 62)
                                pe_fin_bc(t, 62)
                                pe_fin_den(t, 63)
                                pe_fin_bc(t, 63)

                        @blk.scalar
                        def _(s, sl=sl, L=L):
                            def kv_dma(gp):
                                g2 = gp % NKV
                                Lg = LOOP_LEN[gp // NKV]
                                if gp >= 2:
                                    s.wait_ge(sae.h, b_sae + 4 * (gp - 1))
                                ins = s.dma_start(
                                    out=k_g[:, (gp % 2) * S:(gp % 2) * S + Lg * 128],
                                    in_=k_scr[g2][:, 0:Lg * 128])
                                skv.inc(ins, 16)
                                ins = s.dma_start(
                                    out=v_g[:, (gp % 2) * S:(gp % 2) * S + Lg * 128]
                                        .rearrange("p (t h) -> p t h", t=Lg),
                                    in_=v_scr[0:Lg, :, g2 * 128:(g2 + 1) * 128]
                                        .rearrange("t p h -> p t h"))
                                skv.inc(ins, 16)

                            if sl == 0:
                                kv_dma(0)
                                kv_dma(1)
                            for g in range(NKV):
                                for qh in range(4):
                                    hidx = sl * 32 + g * 4 + qh
                                    vs = vstart[(sl, g, qh)]
                                    for kk in range(L):
                                        vi = vs + kk
                                        slot = vslot[vi]
                                        s.wait_ge(ssc.h, b_ssc + vi + 1)
                                        wait_exp_consumed(s, vi)
                                        sex.inc(s.activation(
                                            ebsl(vi),
                                            ps_sc[:, slot * 1024:slot * 1024 + 1024],
                                            AF.Exp, bias=negc[:, 0:1],
                                            scale=SM_SCALE))
                                        if kk == 0:
                                            if hidx >= 1:
                                                act_avevac(s, hidx - 1)
                                            if hidx % 4 == 0 and 2 <= hidx // 4 + 1 <= 15:
                                                kv_dma(hidx // 4 + 1)
                                        elif kk == 1 and hidx >= 2:
                                            act_denrow(s, hidx - 2)
                                        elif kk == 4 and hidx >= 2:
                                            act_bcevac(s, hidx - 2)
                            if sl == NSLOT - 1:
                                act_avevac(s, 63)
                                act_denrow(s, 62)
                                act_bcevac(s, 62)
                                act_denrow(s, 63)
                                act_bcevac(s, 63)

                        @blk.vector
                        def _(v, sl=sl, L=L):
                            ins = None
                            for r in range(8):
                                kk0 = float((L - 8 + r) * 128)
                                ins = v.tensor_scalar(
                                    masks[:, r * 1024:(r + 1) * 1024],
                                    posq_bc[:, sl * 1024:(sl + 1) * 1024],
                                    kk0, iota_p[:, 0:1], ALU.subtract,
                                    ALU.is_ge)
                            sdv.inc(ins)

                            def dve_add(vj):
                                if vj < 0 or vpar[vj] != 0:
                                    return
                                hj = vhead[vj]
                                eb = ebsl(vj)
                                if not vmask[vj]:
                                    v.wait_ge(sex.h, b_sex + vj + 1)
                                if vj == firstpar[(hj, 0)]:
                                    sad.inc(v.tensor_copy(acc2[hj % 3][:, :], eb))
                                else:
                                    sad.inc(v.tensor_add(acc2[hj % 3][:, :],
                                                         acc2[hj % 3][:, :], eb))

                            for g in range(NKV):
                                for qh in range(4):
                                    hidx = sl * 32 + g * 4 + qh
                                    vs = vstart[(sl, g, qh)]
                                    for kk in range(L):
                                        vi = vs + kk
                                        if vmask[vi]:
                                            v.wait_ge(sex.h, b_sex + vi + 1)
                                            rel = kk - (L - 8)
                                            eb = ebsl(vi)
                                            smk.inc(v.tensor_mul(
                                                eb, eb,
                                                masks[:, rel * 1024:rel * 1024 + 1024]))
                                        dve_add(vi - 2)
                                        if kk == 1 and hidx >= 1:
                                            dve_merge(v, hidx - 1)
                                        elif kk == 2 and hidx >= 2:
                                            dve_recip_row(v, hidx - 2)
                                        elif kk == 5 and hidx >= 2:
                                            dve_mul(v, hidx - 2)
                            if sl == NSLOT - 1:
                                dve_add(NVIS - 2)
                                dve_add(NVIS - 1)
                                dve_merge(v, 63)
                                dve_recip_row(v, 62)
                                dve_mul(v, 62)
                                dve_recip_row(v, 63)
                                dve_mul(v, 63)

                assert ssc.n == b_ssc + NVIS and sex.n == b_sex + NVIS
                assert sav.n == b_sav + NVIS
                assert sad.n == b_sad + acnt[0][-1] and spl.n == b_spl + acnt[1][-1]
                assert sfin.n == b_sfin + 2 * 64 and sev.n == b_sev + 64
                assert sdv.n == b_sdv + NSLOT and smg.n == b_smg + 64
                assert sae.n == b_sae + 64 and srm.n == b_srm + 64
                assert srr.n == b_srr + 64 and sbe.n == b_sbe + 64
                with nc.Block() as blk:
                    @blk.gpsimd
                    def _(g_):
                        shs.inc(g_.memset(hv[:, :], float(varid)))

            engines5 = [nc.tensor, nc.vector, nc.scalar, nc.sync, nc.gpsimd]
            sems_p2 = (ssc, sex, smk, sav, sad, spl, smg, sfin, sev, sae,
                       srm, sdv, sq2, skv, scp[0], scp[1], shs, srr, sbe)
            pre = {s_: s_.n for s_ in sems_p2}
            for arm in nc.switch_core_id(engines=engines5, n=128):
                nc.br("p2_var0" if arm.logical % 2 == 0 else "p2_var1")
            nc.br("p2_join")
            ve = [e_.engine for e_ in engines5]
            with nc.body("p2_var0", valid_engines=ve):
                emit_p2((8, 32), 0)
                nc.br("p2_join")
            post0 = {s_: s_.n for s_ in sems_p2}
            for s_, v_ in pre.items():
                s_.n = v_
            with nc.body("p2_var1", valid_engines=ve):
                emit_p2((16, 24), 1)
                nc.br("p2_join")
            post1 = {s_: s_.n for s_ in sems_p2}
            for s_ in (smg, sfin, sev, sae, srm, sdv, sq2, skv, scp[0],
                       scp[1], shs, srr, sbe):
                assert post0[s_] == post1[s_], s_.h
            for s_, v_ in post0.items():
                s_.n = v_
            nc.switch_bb("p2_join")
            with nc.Block() as blk:
                @blk.sync
                def _(sy):
                    sy.wait_ge(shs.h, shs.n)
                    sst.inc(sy.dma_start(out=hsel, in_=hv[:, :]), 16)




        ep.close()

        lp = contextlib.ExitStack()
        big_a = lp.enter_context(nc.sbuf_tensor("big_a", [128, NDC * TOK], BF16))
        rstd_bc = lp.enter_context(nc.sbuf_tensor("rstd_bc", [128, TOK], BF16))
        mr_bc = lp.enter_context(nc.sbuf_tensor("mr_bc", [128, TOK], BF16))
        ssum = lp.enter_context(nc.sbuf_tensor("ssum", [1, TOK], F32))
        ssq = lp.enter_context(nc.sbuf_tensor("ssq", [1, TOK], F32))

        with nc.Block() as blk:
            @blk.sync
            def _(sy):
                sy.wait_ge(scp[0].h, scp[0].n)
                sy.wait_ge(scp[1].h, scp[1].n)
                for hc in range(NH):
                    ins = sy.dma_start(out=big_a[:, hc * TOK:(hc + 1) * TOK],
                                       in_=attn_scr[hc])
                    sat.inc(ins, 16)

            @blk.tensor
            def _(t):
                t.wait_ge(sat.h, sat.n)

        if "attn" in taps:
            with nc.Block() as blk:
                @blk.gpsimd
                def _(g):
                    g.wait_ge(sat.h, sat.n)
                    sst.inc(g.dma_start(out=tap_out["attn"], in_=big_a[:, :]), 16)

        if max_phase < 3:
            _finalize_dbg()
            lp.close()
            return nc, tap_out

        def emit_proj_ln(wT_ap, b_col, lora, scr_out, bt_st=None, g_sb=None):
            with contextlib.ExitStack() as ls:
                stage_w = ls.enter_context(
                    nc.sbuf_tensor(_nm("stage_w3"), [128, 2 * NDC * 128], BF16))
                x_st = (None if lora else
                        ls.enter_context(nc.sbuf_tensor(_nm("x_st"), [128, 2 * TOK], F32)))
                evac = ls.enter_context(nc.sbuf_tensor(_nm("evac3"), [128, 2 * 512], F32))
                sqb = ls.enter_context(nc.sbuf_tensor(_nm("sqb"), [128, 2 * 512], BF16))
                ystg = ls.enter_context(nc.sbuf_tensor(_nm("ystg"), [128, TOK], BF16))
                ps = ls.enter_context(nc.psum_tensor(_nm("p3_ps"), [128, 2 * 512], F32))
                ps_st = ls.enter_context(nc.psum_tensor(_nm("p3_st"), [1, 1024], F32))

                b_sw = [swp[0].n, swp[1].n]
                b_sx2 = [sxp[0].n, sxp[1].n]
                b_sps, b_spst = sps.n, spst.n
                b_sev, b_sdv, b_sscw, b_sfin = sev.n, sdv.n, sscw.n, sfin.n
                NG = NDC * NT

                def emit_stat_mm(t, gi):
                    if DBG_NOSTAT:
                        return
                    t.wait_ge(sev.h, b_sev + 2 * gi + 2)
                    if gi >= 1:
                        t.wait_ge(sfin.h, b_sfin + gi)
                    nt = gi % NT
                    t.matmul(ps_st[0:1, 0:512], ones_c1[:, :],
                             ystg[:, nt * 512:nt * 512 + 512],
                             start=True, stop=True, skip_group_check=True)
                    ins = t.matmul(ps_st[0:1, 512:1024], ones_c1[:, :],
                                   sqb[:, (gi % 2) * 512:(gi % 2) * 512 + 512],
                                   start=True, stop=True, skip_group_check=True)
                    spst.inc(ins)

                with nc.Block() as blk:
                    @blk.gpsimd
                    def _(g):
                        for o in range(NDC):
                            if o >= 2:
                                g.wait_ge(sps.h, b_sps + NT * (o - 1))
                            ins = g.dma_start(
                                out=stage_w[:, :]
                                    .rearrange("p (b c j) -> p b c j", b=2, c=NDC)
                                    [:, o % 2, :, :],
                                in_=wT_ap[:, o * 128:(o + 1) * 128]
                                    .rearrange("(c p) j -> p c j", p=128))
                            swp[o % 2].inc(ins, 16)


                    @blk.tensor
                    def _(t):
                        for o in range(NDC):
                            t.wait_ge(swp[o % 2].h, b_sw[o % 2] + 16 * (o // 2 + 1))
                            for nt in range(NT):
                                gi = o * NT + nt
                                if gi >= 2:
                                    t.wait_ge(sev.h, b_sev + 2 * (gi - 1))
                                ins = None
                                for c in range(NDC):
                                    ins = t.matmul(
                                        ps[:, (gi % 2) * 512:(gi % 2) * 512 + 512],
                                        stage_w[:, (o % 2) * NDC * 128 + c * 128:
                                                (o % 2) * NDC * 128 + c * 128 + 128],
                                        big_a[:, c * TOK + nt * 512:
                                              c * TOK + nt * 512 + 512],
                                        start=(c == 0),
                                        stop=(c == NDC - 1 and not lora))
                                if lora:
                                    ins = t.matmul(
                                        ps[:, (gi % 2) * 512:(gi % 2) * 512 + 512],
                                        bt_st[:, o * 128:o * 128 + 128],
                                        g_sb[:, nt * 512:nt * 512 + 512],
                                        start=False, stop=True,
                                        skip_group_check=True)
                                sps.inc(ins)
                                if gi >= 2:
                                    emit_stat_mm(t, gi - 2)
                        emit_stat_mm(t, NG - 2)
                        emit_stat_mm(t, NG - 1)

                    @blk.scalar
                    def _(s):
                        for o in range(NDC):
                            for nt in range(NT):
                                gi = o * NT + nt
                                s.wait_ge(sps.h, b_sps + gi + 1)
                                if gi >= 2:
                                    s.wait_ge(sdv.h, b_sdv + gi - 1)
                                sev.inc(s.activation(
                                    evac[:, (gi % 2) * 512:(gi % 2) * 512 + 512],
                                    ps[:, (gi % 2) * 512:(gi % 2) * 512 + 512],
                                    AF.Identity, bias=b_col[:, o:o + 1], scale=1.0))
                                s.wait_ge(sdv.h, b_sdv + gi + 1)
                                if gi >= 2 and not DBG_NOSTAT:
                                    s.wait_ge(spst.h, b_spst + gi - 1)
                                sev.inc(s.activation(
                                    sqb[:, (gi % 2) * 512:(gi % 2) * 512 + 512],
                                    ystg[:, nt * 512:nt * 512 + 512],
                                    AF.Square, bias=0.0, scale=1.0))

                    @blk.vector
                    def _(v):
                        def rowadd(gi2):
                            if DBG_NOSTAT:
                                return
                            nt2 = gi2 % NT
                            v.wait_ge(spst.h, b_spst + gi2 + 1)
                            if gi2 < NT:
                                v.tensor_copy(ssum[0:1, nt2 * 512:nt2 * 512 + 512],
                                              ps_st[0:1, 0:512])
                                ins = v.tensor_copy(
                                    ssq[0:1, nt2 * 512:nt2 * 512 + 512],
                                    ps_st[0:1, 512:1024])
                            else:
                                v.tensor_add(ssum[0:1, nt2 * 512:nt2 * 512 + 512],
                                             ssum[0:1, nt2 * 512:nt2 * 512 + 512],
                                             ps_st[0:1, 0:512])
                                ins = v.tensor_add(
                                    ssq[0:1, nt2 * 512:nt2 * 512 + 512],
                                    ssq[0:1, nt2 * 512:nt2 * 512 + 512],
                                    ps_st[0:1, 512:1024])
                            sfin.inc(ins)

                        for o in range(NDC):
                            for nt in range(NT):
                                gi = o * NT + nt
                                v.wait_ge(sev.h, b_sev + 2 * gi + 1)
                                if nt == 0 and o >= 1:
                                    v.wait_ge(sscw.h, b_sscw + 16 * o)
                                src1 = (x_st[:, (o % 2) * TOK + nt * 512:
                                             (o % 2) * TOK + nt * 512 + 512]
                                        if not lora else
                                        big_a[:, o * TOK + nt * 512:
                                              o * TOK + nt * 512 + 512])
                                if not lora:
                                    v.wait_ge(sxp[o % 2].h,
                                              b_sx2[o % 2] + 16 * (o // 2 + 1))
                                sdv.inc(v.tensor_add(
                                    ystg[:, nt * 512:nt * 512 + 512],
                                    evac[:, (gi % 2) * 512:(gi % 2) * 512 + 512],
                                    src1))
                                if gi >= 2:
                                    rowadd(gi - 2)
                        rowadd(NG - 2)
                        rowadd(NG - 1)

                    @blk.sync
                    def _(sy):
                        for o in range(NDC):
                            if not lora:
                                if o >= 2:
                                    sy.wait_ge(sdv.h, b_sdv + NT * (o - 1))
                                ins = sy.dma_start(
                                    out=x_st[:, (o % 2) * TOK:(o % 2) * TOK + TOK],
                                    in_=xT_own[o * 128:(o + 1) * 128, :])
                                sxp[o % 2].inc(ins, 16)
                            if o >= 1:
                                sy.wait_ge(sdv.h, b_sdv + NT * o)
                                ins = sy.dma_start(out=scr_out[o - 1], in_=ystg[:, :])
                                sscw.inc(ins, 16)
                        sy.wait_ge(sdv.h, b_sdv + NT * NDC)
                        ins = sy.dma_start(out=scr_out[NDC - 1], in_=ystg[:, :])
                        sscw.inc(ins, 16)

            if DBG_NOROWS:
                return
            with contextlib.ExitStack() as ls2:
                ps_ln = ls2.enter_context(nc.psum_tensor(_nm("ps_ln"), [128, TOK], F32))
                b2_sdv, b2_sev, b2_sps, b2_sfin = sdv.n, sev.n, sps.n, sfin.n
                with nc.Block() as blk:
                    @blk.vector
                    def _(v):
                        v.wait_ge(sfin.h, b2_sfin)
                        v.tensor_scalar(ssum[0:1, :], ssum[0:1, :], 1.0 / D,
                                        None, ALU.mult)
                        v.tensor_scalar(ssq[0:1, :], ssq[0:1, :], 1.0 / D,
                                        None, ALU.mult)
                        v.tensor_mul(ps_ln[0:1, 0:TOK], ssum[0:1, :], ssum[0:1, :])
                        v.tensor_sub(ssq[0:1, :], ssq[0:1, :], ps_ln[0:1, 0:TOK])
                        sdv.inc(vf(v, ssq))

                    @blk.scalar
                    def _(s):
                        s.wait_ge(sdv.h, b2_sdv + 1)
                        s.activation(ps_ln[0:1, 0:TOK], ssq[0:1, :],
                                     AF.Sqrt, bias=epsb[0:1, 0:1], scale=1.0)
                        sev.inc(sf(s, ps_ln))

                    @blk.vector
                    def _(v):
                        v.wait_ge(sev.h, b2_sev + 1)
                        v.reciprocal(ssq[0:1, :], ps_ln[0:1, 0:TOK])
                        v.tensor_mul(ssum[0:1, :], ssum[0:1, :], ssq[0:1, :])
                        sdv.inc(vf(v, ssum))

                    @blk.tensor
                    def _(t):
                        t.wait_ge(sdv.h, b2_sdv + 2)
                        ins = None
                        for i in range(NT):
                            ins = t.matmul(ps_ln[:, i * 512:(i + 1) * 512],
                                           ones_r1f[:, :],
                                           ssq[0:1, i * 512:(i + 1) * 512],
                                           start=True, stop=True,
                                           skip_group_check=True)
                        sps.inc(ins)
                        t.wait_ge(sev.h, b2_sev + 2)
                        ins = None
                        for i in range(NT):
                            ins = t.matmul(ps_ln[:, i * 512:(i + 1) * 512],
                                           ones_r1f[:, :],
                                           ssum[0:1, i * 512:(i + 1) * 512],
                                           start=True, stop=True,
                                           skip_group_check=True)
                        sps.inc(ins)

                    @blk.scalar
                    def _(s):
                        s.wait_ge(sps.h, b2_sps + 1)
                        s.copy(rstd_bc[:, :], ps_ln[:, :])
                        sev.inc(sf(s, rstd_bc))
                        s.wait_ge(sps.h, b2_sps + 2)
                        s.copy(mr_bc[:, :], ps_ln[:, :])
                        sev.inc(sf(s, mr_bc))

        emit_proj_ln(woT, bo_col, False, [y1_scr[o] for o in range(NDC)])

        if "y1" in taps or "lnr1" in taps:
            with nc.Block() as blk:
                @blk.gpsimd
                def _(g):
                    g.wait_ge(sscw.h, sscw.n)
                    g.wait_ge(sev.h, sev.n)
                    if "y1" in taps:
                        sst.inc(g.dma_start(out=tap_out["y1"], in_=y1_scr), 16)
                    if "lnr1" in taps:
                        sst.inc(g.dma_start(out=tap_out["lnr1"][0:1, 0:TOK],
                                            in_=ssq[0:1, :]), 16)
                        sst.inc(g.dma_start(out=tap_out["lnr1"][0:1, TOK:2 * TOK],
                                            in_=ssum[0:1, :]), 16)

        if max_phase < 4:
            _finalize_dbg()
            lp.close()
            return nc, tap_out
        with contextlib.ExitStack() as p4p:
            y1r = p4p.enter_context(nc.sbuf_tensor("y1r", [128, 2 * TOK], BF16))
            b_sx2 = [sxp[0].n, sxp[1].n]
            b_sdv = sdv.n
            with nc.Block() as blk:
                @blk.sync
                def _(sy):
                    sy.wait_ge(sev.h, sev.n)
                    sy.wait_ge(sscw.h, sscw.n)
                    for c in range(NDC):
                        if c >= 2:
                            sy.wait_ge(sdv.h, b_sdv + 2 * (c - 1))
                        ins = sy.dma_start(
                            out=y1r[:, (c % 2) * TOK:(c % 2) * TOK + TOK],
                            in_=y1_scr[c])
                        sxp[c % 2].inc(ins, 16)

                @blk.vector
                def _(v):
                    for c in range(NDC):
                        v.wait_ge(sxp[c % 2].h, b_sx2[c % 2] + 16 * (c // 2 + 1))
                        dst = big_a[:, c * TOK:(c + 1) * TOK]
                        v.tensor_mul(dst, y1r[:, (c % 2) * TOK:(c % 2) * TOK + TOK],
                                     rstd_bc[:, :])
                        sdv.inc(v.tensor_sub(dst, dst, mr_bc[:, :]), 2)

        if "y1n" in taps:
            with nc.Block() as blk:
                @blk.gpsimd
                def _(g):
                    g.wait_ge(sdv.h, sdv.n)
                    sst.inc(g.dma_start(out=tap_out["y1n"], in_=big_a[:, :]), 16)

        if max_phase < 5:
            _finalize_dbg()
            lp.close()
            return nc, tap_out
        p4s = contextlib.ExitStack()
        bt_st = p4s.enter_context(nc.sbuf_tensor("bt_st", [128, D], BF16))
        g_sb = p4s.enter_context(nc.sbuf_tensor("g_sb", [128, TOK], BF16))
        with contextlib.ExitStack() as p4a:
            wr_st = p4a.enter_context(nc.sbuf_tensor("wr_st", [128, NDC * 8], BF16))
            at_st = p4a.enter_context(nc.sbuf_tensor("at_st", [128, NDC * 128], BF16))
            lg = p4a.enter_context(nc.sbuf_tensor("lg", [8, TOK], F32))
            lgT = p4a.enter_context(nc.sbuf_tensor("lgT", [128, 16 * 8], F32))
            rsc = p4a.enter_context(nc.sbuf_tensor("rsc", [128, 4 * 16], F32))
            rse = p4a.enter_context(nc.sbuf_tensor("rse", [128, 16 * 8], F32))
            rcl = p4a.enter_context(nc.sbuf_tensor("rcl", [128, 16 * 4], F32))
            rc2 = p4a.enter_context(nc.sbuf_tensor("rc2", [128, 8], F32))
            wfb = p4a.enter_context(nc.sbuf_tensor("wfb", [128, 16 * 8], BF16))
            route_row = p4a.enter_context(nc.sbuf_tensor("route_row", [8, TOK], BF16))
            routebc = p4a.enter_context(nc.sbuf_tensor("routebc", [128, TOK], BF16))
            hevac = p4a.enter_context(nc.sbuf_tensor("hevac", [128, 2 * 512], F32))
            ps_rb = p4a.enter_context(nc.psum_tensor("ps_rb", [128, 2 * 512], F32))
            ps_t = p4a.enter_context(nc.psum_tensor("ps_t", [128, 8], F32))
            ps_rt = p4a.enter_context(nc.psum_tensor("ps_rt", [8, TOK], BF16))

            b_sw, b_sps, b_sev, b_sdv, b_sfin = sw.n, sps.n, sev.n, sdv.n, sfin.n
            y1n_done = sdv.n

            with nc.Block() as blk:
                @blk.gpsimd
                def _(g):
                    sw.inc(g.dma_start(
                        out=wr_st[:, :].rearrange("p (c e) -> p c e", c=NDC),
                        in_=wrT.rearrange("(c p) e -> p c e", p=128)), 16)
                    sw.inc(g.dma_start(
                        out=at_st[:, :].rearrange("p (c r) -> p c r", c=NDC),
                        in_=At_d.rearrange("(c p) r -> p c r", p=128)), 16)
                    sw.inc(g.dma_start(out=bt_st[:, :], in_=Bt_d), 16)

                @blk.tensor
                def _(t):
                    t.wait_ge(sw.h, b_sw + 16)
                    t.wait_ge(sdv.h, y1n_done)
                    for nt in range(NT):
                        if nt >= 2:
                            t.wait_ge(sev.h, b_sev + nt - 1)
                        ins = None
                        for c in range(NDC):
                            ins = t.matmul(
                                ps_rb[0:8, (nt % 2) * 512:(nt % 2) * 512 + 512],
                                wr_st[:, c * 8:c * 8 + 8],
                                big_a[:, c * TOK + nt * 512:c * TOK + nt * 512 + 512],
                                start=(c == 0), stop=(c == NDC - 1))
                        sps.inc(ins)

                @blk.scalar
                def _(s):
                    for nt in range(NT):
                        s.wait_ge(sps.h, b_sps + nt + 1)
                        s.copy(lg[0:8, nt * 512:nt * 512 + 512],
                               ps_rb[0:8, (nt % 2) * 512:(nt % 2) * 512 + 512])
                        sev.inc(sf(s, lg))

                @blk.tensor
                def _(t):
                    for tt in range(16):
                        t.wait_ge(sev.h, b_sev + tt // 4 + 1)
                        if tt >= 1:
                            t.wait_ge(sev.h, b_sev + NT + tt)
                        ins = t.transpose(ps_t[:, :],
                                          lg[0:8, tt * 128:tt * 128 + 128],
                                          ident_f8[:, :])
                        sfin.inc(ins)

                @blk.scalar
                def _(s):
                    for tt in range(16):
                        s.wait_ge(sfin.h, b_sfin + tt + 1)
                        s.copy(lgT[:, tt * 8:tt * 8 + 8], ps_t[:, :])
                        sev.inc(sf(s, lgT))

                def _m1(tt):
                    return rcl[:, tt * 4:tt * 4 + 1]

                def _nm1(tt):
                    return rcl[:, tt * 4 + 1:tt * 4 + 2]

                def _m2(tt):
                    return rcl[:, tt * 4 + 2:tt * 4 + 3]

                @blk.vector
                def _(v):
                    for base in range(0, 16, 2):
                        pair = (base, base + 1)
                        for tt in pair:
                            v.wait_ge(sev.h, b_sev + NT + tt + 1)
                        for tt in pair:
                            v.reduce_max(_m1(tt), lgT[:, tt * 8:tt * 8 + 8], AX.X)
                        for tt in pair:
                            pa = (tt % 2) * 32
                            v.tensor_scalar(rsc[:, pa:pa + 8],
                                            lgT[:, tt * 8:tt * 8 + 8],
                                            _m1(tt), None, ALU.is_equal)
                        for tt in pair:
                            v.tensor_scalar(_nm1(tt), _m1(tt), -1.0, None, ALU.mult)
                        for tt in pair:
                            pa = (tt % 2) * 32
                            v.scalar_tensor_tensor(rsc[:, pa + 8:pa + 16],
                                                   rsc[:, pa:pa + 8], -1e30,
                                                   lgT[:, tt * 8:tt * 8 + 8],
                                                   ALU.mult, ALU.add)
                        for tt in pair:
                            pa = (tt % 2) * 32
                            v.reduce_max(_m2(tt), rsc[:, pa + 8:pa + 16], AX.X)
                        for tt in pair:
                            sdv.inc(vf(v, rcl))

                @blk.scalar
                def _(s):
                    for tt in range(16):
                        s.wait_ge(sdv.h, b_sdv + tt + 1)
                        s.activation(rse[:, tt * 8:tt * 8 + 8],
                                     lgT[:, tt * 8:tt * 8 + 8], AF.Exp,
                                     bias=_nm1(tt), scale=1.0)
                        sev.inc(sf(s, rse))

                @blk.vector
                def _(v):
                    for base in range(0, 16, 2):
                        pair = (base, base + 1)
                        for tt in pair:
                            v.wait_ge(sev.h, b_sev + NT + 16 + tt + 1)
                        for tt in pair:
                            pa = (tt % 2) * 32
                            v.tensor_scalar(rsc[:, pa + 16:pa + 24],
                                            lgT[:, tt * 8:tt * 8 + 8],
                                            _m2(tt), None, ALU.is_ge)
                        for tt in pair:
                            pa = (tt % 2) * 32
                            v.tensor_mul(rsc[:, pa + 16:pa + 24],
                                         rsc[:, pa + 16:pa + 24],
                                         rse[:, tt * 8:tt * 8 + 8])
                        for tt in pair:
                            pa = (tt % 2) * 32
                            v.reduce_sum(rc2[:, (tt % 2) * 4:(tt % 2) * 4 + 1],
                                         rsc[:, pa + 16:pa + 24], AX.X)
                        for tt in pair:
                            v.reciprocal(rc2[:, (tt % 2) * 4 + 1:(tt % 2) * 4 + 2],
                                         rc2[:, (tt % 2) * 4:(tt % 2) * 4 + 1])
                        for tt in pair:
                            v.tensor_scalar(rc2[:, (tt % 2) * 4 + 2:(tt % 2) * 4 + 3],
                                            rc2[:, (tt % 2) * 4 + 1:(tt % 2) * 4 + 2],
                                            2.0, None, ALU.mult)
                        for tt in pair:
                            pa = (tt % 2) * 32
                            v.tensor_scalar(wfb[:, tt * 8:tt * 8 + 8],
                                            rsc[:, pa + 16:pa + 24],
                                            rc2[:, (tt % 2) * 4 + 2:(tt % 2) * 4 + 3],
                                            None, ALU.mult)
                        for tt in pair:
                            sdv.inc(vf(v, wfb))

                @blk.tensor
                def _(t):
                    for tt in range(16):
                        t.wait_ge(sdv.h, b_sdv + 16 + tt + 1)
                        ins = t.transpose(ps_rt[0:8, tt * 128:tt * 128 + 128],
                                          wfb[:, tt * 8:tt * 8 + 8],
                                          ident_b[:, :])
                        sfin.inc(ins)

                @blk.scalar
                def _(s):
                    s.wait_ge(sfin.h, b_sfin + 32)
                    s.copy(route_row[0:8, :], ps_rt[0:8, :])
                    sev.inc(sf(s, route_row))

                @blk.tensor
                def _(t):
                    t.wait_ge(sev.h, b_sev + NT + 32 + 1)
                    for nt in range(NT):
                        if nt >= 2:
                            t.wait_ge(sev.h, b_sev + NT + 32 + 1 + nt - 1)
                        ins = t.matmul(ps_rb[:, (nt % 2) * 512:(nt % 2) * 512 + 512],
                                       ebc[0:8, :],
                                       route_row[0:8, nt * 512:nt * 512 + 512],
                                       start=True, stop=True,
                                       skip_group_check=True)
                        sps.inc(ins)

                @blk.scalar
                def _(s):
                    for nt in range(NT):
                        s.wait_ge(sps.h, b_sps + NT + nt + 1)
                        s.copy(routebc[:, nt * 512:nt * 512 + 512],
                               ps_rb[:, (nt % 2) * 512:(nt % 2) * 512 + 512])
                        sev.inc(sf(s, routebc))

            b_sw2, b_sps2, b_sev2, b_sdv2 = sw.n, sps.n, sev.n, sdv.n
            with nc.Block() as blk:
                @blk.tensor
                def _(t):
                    t.wait_ge(sw.h, b_sw2)
                    t.wait_ge(sev.h, b_sev2)
                    for nt in range(NT):
                        if nt >= 2:
                            t.wait_ge(sev.h, b_sev2 + nt - 1)
                        ins = None
                        for c in range(NDC):
                            ins = t.matmul(
                                ps_rb[:, (nt % 2) * 512:(nt % 2) * 512 + 512],
                                at_st[:, c * 128:c * 128 + 128],
                                big_a[:, c * TOK + nt * 512:c * TOK + nt * 512 + 512],
                                start=(c == 0), stop=(c == NDC - 1))
                        sps.inc(ins)

                @blk.scalar
                def _(s):
                    for nt in range(NT):
                        s.wait_ge(sps.h, b_sps2 + nt + 1)
                        if nt >= 2:
                            s.wait_ge(sdv.h, b_sdv2 + nt - 1)
                        s.copy(hevac[:, (nt % 2) * 512:(nt % 2) * 512 + 512],
                               ps_rb[:, (nt % 2) * 512:(nt % 2) * 512 + 512])
                        sev.inc(sf(s, hevac))

                @blk.vector
                def _(v):
                    for nt in range(NT):
                        v.wait_ge(sev.h, b_sev2 + nt + 1)
                        sdv.inc(v.tensor_mul(
                            g_sb[:, nt * 512:nt * 512 + 512],
                            hevac[:, (nt % 2) * 512:(nt % 2) * 512 + 512],
                            routebc[:, nt * 512:nt * 512 + 512]))

        if "route" in taps or "gsb" in taps:
            with nc.Block() as blk:
                @blk.gpsimd
                def _(g):
                    g.wait_ge(sdv.h, sdv.n)
                    if "gsb" in taps:
                        sst.inc(g.dma_start(out=tap_out["gsb"], in_=g_sb[:, :]), 16)

        if max_phase < 6:
            _finalize_dbg()
            p4s.close()
            lp.close()
            return nc, tap_out
        emit_proj_ln(wbT, bb_col, True, [y2_scr[o] for o in range(NDC)],
                     bt_st=bt_st, g_sb=g_sb)
        p4s.close()

        if "y2" in taps:
            with nc.Block() as blk:
                @blk.gpsimd
                def _(g):
                    g.wait_ge(sscw.h, sscw.n)
                    sst.inc(g.dma_start(out=tap_out["y2"], in_=y2_scr), 16)

        if max_phase < 7:
            _finalize_dbg()
            lp.close()
            return nc, tap_out
        with contextlib.ExitStack() as p5:
            y2r = p5.enter_context(nc.sbuf_tensor("y2r", [128, 2 * TOK], BF16))
            outstg = p5.enter_context(nc.sbuf_tensor("outstg", [128, 2 * TOK], F32))
            b_sx2 = [sxp[0].n, sxp[1].n]
            b_st2 = [sstp[0].n, sstp[1].n]
            b_sdv = sdv.n
            with nc.Block() as blk:
                @blk.sync
                def _(sy):
                    sy.wait_ge(sev.h, sev.n)
                    sy.wait_ge(sscw.h, sscw.n)
                    for c in range(NDC):
                        if c >= 2:
                            sy.wait_ge(sdv.h, b_sdv + 2 * (c - 1))
                        ins = sy.dma_start(
                            out=y2r[:, (c % 2) * TOK:(c % 2) * TOK + TOK],
                            in_=y2_scr[c])
                        sxp[c % 2].inc(ins, 16)
                        if c >= 1:
                            sy.wait_ge(sdv.h, b_sdv + 2 * c)
                            ins = sy.dma_start(
                                out=outT[(c - 1) * 128:c * 128, :],
                                in_=outstg[:, ((c - 1) % 2) * TOK:
                                           ((c - 1) % 2) * TOK + TOK])
                            sstp[(c - 1) % 2].inc(ins, 16)
                    sy.wait_ge(sdv.h, b_sdv + 2 * NDC)
                    ins = sy.dma_start(
                        out=outT[(NDC - 1) * 128:NDC * 128, :],
                        in_=outstg[:, ((NDC - 1) % 2) * TOK:
                                   ((NDC - 1) % 2) * TOK + TOK])
                    sstp[(NDC - 1) % 2].inc(ins, 16)
                    sy.wait_ge(sstp[0].h, sstp[0].n)
                    sy.wait_ge(sstp[1].h, sstp[1].n)
                    sy.wait_ge(sst.h, sst.n)

                @blk.vector
                def _(v):
                    for c in range(NDC):
                        v.wait_ge(sxp[c % 2].h, b_sx2[c % 2] + 16 * (c // 2 + 1))
                        if c >= 2:
                            v.wait_ge(sstp[c % 2].h, b_st2[c % 2] + 16 * (c // 2))
                        dst = outstg[:, (c % 2) * TOK:(c % 2) * TOK + TOK]
                        v.tensor_mul(dst, y2r[:, (c % 2) * TOK:(c % 2) * TOK + TOK],
                                     rstd_bc[:, :])
                        sdv.inc(v.tensor_sub(dst, dst, mr_bc[:, :]), 2)

        lp.close()
    return nc, tap_out


def _rope_perm():
    perm = []
    for h in range(NH):
        base = h * HD
        perm += [base + 2 * i for i in range(64)] + [base + 2 * i + 1 for i in range(64)]
    for kh in range(NKV):
        base = NH * HD + kh * HD
        perm += [base + 2 * i for i in range(64)] + [base + 2 * i + 1 for i in range(64)]
    perm += list(range(5120, 6144))
    return np.asarray(perm)


def _core_blocks(c):
    return (0, 3) if c % 2 == 0 else (1, 2)


def make_in_maps(L0_attn_x, L0_attn_w_qkv, L0_attn_b_qkv, L0_attn_w_o, L0_attn_b_o,
                 L0_ffn_W_base, L0_ffn_b_base, L0_ffn_W_router, L0_ffn_A, L0_ffn_B):
    asc = np.ascontiguousarray
    x = np.asarray(L0_attn_x)
    perm = _rope_perm()
    wqkvT = asc(np.asarray(L0_attn_w_qkv)[perm].T)
    bqkv = np.asarray(L0_attn_b_qkv)[perm]
    bq_col = asc(bqkv[:5120].reshape(40, 128).T)
    bv_row = asc(bqkv[5120:].reshape(1, 1024))
    woT = asc(np.asarray(L0_attn_w_o).T)
    bo_col = asc(np.asarray(L0_attn_b_o).reshape(32, 128).T)
    wbT = asc(np.asarray(L0_ffn_W_base).T)
    bb_col = asc(np.asarray(L0_ffn_b_base).reshape(32, 128).T)
    wrT = asc(np.asarray(L0_ffn_W_router).T)
    At = asc(np.asarray(L0_ffn_A).transpose(2, 0, 1).reshape(D, 128))
    Bt = asc(np.asarray(L0_ffn_B).transpose(0, 2, 1).reshape(128, D))

    in_maps = []
    pos_per_core = []
    xT_cache = {}
    for c in range(8):
        b = c // 2
        if b not in xT_cache:
            xT_cache[b] = asc(x[b].T.astype(np.float32))
        blocks = _core_blocks(c)
        pos = np.concatenate([np.arange(1024 * blk, 1024 * blk + 1024)
                              for blk in blocks])
        pos_per_core.append(pos)
        in_maps.append({
            "xT_own": asc(xT_cache[b][:, pos]),
            "xT_full": xT_cache[b],
            "posq": pos.astype(np.float32).reshape(1, TOK),
            "wqkvT": wqkvT, "bq_col_d": bq_col, "bv_row_d": bv_row,
            "woT": woT, "bo_col_d": bo_col,
            "wbT": wbT, "bb_col_d": bb_col,
            "wrT": wrT, "At": At, "Bt": Bt,
        })
    return in_maps, pos_per_core


def kernel(**inputs):
    in_maps, pos_per_core = make_in_maps(**inputs)
    nc, _ = build_program(())
    res = run_bass_kernel_spmd(nc, in_maps, core_ids=list(range(8)))

    out = np.empty((4, S, D), dtype=np.float32)
    for c in range(8):
        b = c // 2
        hv = int(res.results[c]["hsel"][0, 0])
        if hv != c % 2:
            raise RuntimeError(
                f"core {c} ran switch variant {hv}; physical-core mapping "
                f"changed — per-core loop lengths do not match its data")
        out[b, pos_per_core[c], :] = res.results[c]["outT"].T
    return out


if __name__ == "__main__":
    nc, _ = build_program(())
    print("instructions:", nc.n_instructions())
